# version 01 (final)
# baseline (speedup 1.0000x reference)
import numpy as np
import ml_dtypes
import concourse.bass as bass
import concourse.mybir as mybir
from concourse.bass_utils import run_bass_kernel_spmd

F32 = mybir.dt.float32
BF16 = mybir.dt.bfloat16
AF = mybir.ActivationFunctionType
ALU = mybir.AluOpType
AX = mybir.AxisListType
BFNP = ml_dtypes.bfloat16

ENGINES = ("sync", "vector", "scalar", "tensor", "gpsimd")
DMA_QUEUES = ("sync", "gpsimd", "scalar", "vector")

T = 2048
D = 1024
NT = T // 128
ND = D // 128
EPS_LN = 1e-5


class Op:
    __slots__ = ("engine", "fn", "dma", "needs_inc", "inc_val", "waits",
                 "tag", "dma_sem")

    def __init__(self, engine, fn, dma, tag):
        self.engine = engine
        self.fn = fn
        self.dma = dma
        self.needs_inc = dma
        self.inc_val = None
        self.waits = set()
        self.tag = tag
        self.dma_sem = None


class Buf:
    __slots__ = ("name", "win", "mk")

    def __init__(self, name, win, mk):
        self.name = name
        self.win = win
        self.mk = mk

    def ap(self):
        return self.win

    def __getitem__(self, key):
        return self.win[key]

    def slot(self, i):
        return (self, i)

    @property
    def key(self):
        return self.name


class MK:

    def __init__(self, nc, arena_words=53200):
        self.nc = nc
        self.ops = {e: [] for e in ENGINES}
        self.n_ops = 0
        self._track = {}
        self._sems = {}
        self.NDMASEM = 48
        self._dma_rr = 0
        self._dma_sem_uses = [0] * self.NDMASEM
        self._consts = {}
        self._uid = 0
        self.ar = nc.alloc_sbuf_tensor("arena", [128, arena_words], F32).ap()
        self.ar_bf = self.ar.bitcast(BF16)
        self._off = 0
        self._cap = arena_words
        self._marks = []
        self.peak = 0
        self._barrier_last = None
        self._barrier_pending = set()

    def buf(self, name, shape, dtype):
        assert len(shape) == 2 and shape[0] <= 128
        n = int(shape[1])
        w = n if dtype == F32 else (n + 1) // 2
        off = (self._off + 7) & ~7
        self._off = off + w
        self.peak = max(self.peak, self._off)
        assert self._off <= self._cap, \
            f"arena overflow at {name}: {self._off}"
        if dtype == F32:
            win = self.ar[0:shape[0], off:off + n]
        else:
            win = self.ar_bf[0:shape[0], 2 * off:2 * off + n]
        self._uid += 1
        return Buf(f"{name}#{self._uid}", win, self)

    def mark(self):
        self._marks.append(self._off)
        return len(self._marks) - 1

    def release(self, m, barrier=True):
        self._off = self._marks[m]
        del self._marks[m:]
        if barrier:
            self.barrier()

    def barrier(self):
        last = {}
        for e in ENGINES:
            if self.ops[e]:
                last[e] = self.ops[e][-1]
        self._barrier_last = last
        self._barrier_pending = set(ENGINES)

    def psum_init(self):
        self.P = []
        for i in range(4):
            t = self.nc.alloc_psum_tensor(f"psum{i}", [128, 1024], F32)
            self.P.append(Buf(f"psum{i}", t.ap(), self))

    def const(self, name, arr):
        if name in self._consts:
            return self._consts[name]
        assert not self._marks, f"const {name} allocated inside a stack frame"
        assert arr.ndim <= 2
        dram = self.nc.inline_tensor(np.ascontiguousarray(arr), name="c_" + name)
        if arr.ndim == 1:
            shape = [1, arr.shape[0]]
        else:
            shape = [arr.shape[0], arr.shape[1]]
        dtype = BF16 if arr.dtype == BFNP else (F32 if arr.dtype == np.float32 else None)
        assert dtype is not None, arr.dtype
        b = self.buf("c_" + name, shape, dtype)
        src = dram.ap()
        if arr.ndim == 1:
            src = src.unsqueeze(0)
        self.dma("sync", b.ap(), src, rd=(), wr=(b,), tag="const_" + name)
        self._consts[name] = b
        return b

    def _entry(self, buf):
        ent = self._track.get(buf.name)
        if ent is None:
            ent = {"base": [None, []], "slots": {}}
            self._track[buf.name] = ent
        return ent

    def _dep(self, op, producer):
        if producer is None or producer is op:
            return
        if (producer.engine == op.engine and not producer.dma
                and producer.engine == "tensor"):
            return
        if not producer.dma:
            producer.needs_inc = True
        op.waits.add(producer)

    def _read(self, op, item):
        buf, slot = (item, None) if isinstance(item, Buf) else item
        ent = self._entry(buf)
        base = ent["base"]
        self._dep(op, base[0])
        psum = buf.name.startswith("psum")
        if slot is None:
            for st in ent["slots"].values():
                self._dep(op, st[0])
                if psum:
                    for r in st[1]:
                        if r.engine != op.engine:
                            self._dep(op, r)
            if psum:
                for r in base[1]:
                    if r.engine != op.engine:
                        self._dep(op, r)
            base[1].append(op)
        else:
            st = ent["slots"].setdefault(slot, [None, []])
            self._dep(op, st[0])
            if psum:
                for r in st[1] + base[1]:
                    if r.engine != op.engine:
                        self._dep(op, r)
            st[1].append(op)

    def _write(self, op, item):
        buf, slot = (item, None) if isinstance(item, Buf) else item
        ent = self._entry(buf)
        base = ent["base"]
        if slot is None:
            self._dep(op, base[0])
            for r in base[1]:
                self._dep(op, r)
            for st in ent["slots"].values():
                self._dep(op, st[0])
                for r in st[1]:
                    self._dep(op, r)
            self._track[buf.name] = {"base": [op, []], "slots": {}}
        else:
            self._dep(op, base[0])
            for r in base[1]:
                self._dep(op, r)
            st = ent["slots"].setdefault(slot, [None, []])
            self._dep(op, st[0])
            for r in st[1]:
                self._dep(op, r)
            ent["slots"][slot] = [op, []]

    def add(self, engine, fn, rd=(), wr=(), dma=False, tag=""):
        op = Op(engine, fn, dma, tag)
        if engine in self._barrier_pending:
            self._barrier_pending.discard(engine)
            for pe, pop in (self._barrier_last or {}).items():
                self._dep(op, pop)
        for item in rd:
            self._read(op, item)
        for item in wr:
            self._write(op, item)
        if dma:
            sid = self._dma_rr % self.NDMASEM
            self._dma_rr += 1
            self._dma_sem_uses[sid] += 1
            op.dma_sem = sid
            op.inc_val = 16 * self._dma_sem_uses[sid]
        self.ops[engine].append(op)
        self.n_ops += 1
        return op

    def dma(self, queue, out_ap, in_ap, rd=(), wr=(), tag=""):
        def fn(eng, out_ap=out_ap, in_ap=in_ap):
            return eng.dma_start(out=out_ap, in_=in_ap)
        return self.add(queue, fn, rd=rd, wr=wr, dma=True, tag=tag)

    def finalize_and_emit(self):
        nc = self.nc
        counts = {e: 0 for e in ENGINES}
        for e in ENGINES:
            for op in self.ops[e]:
                if not op.dma and op.needs_inc:
                    counts[e] += 1
                    op.inc_val = counts[e]
        from contextlib import ExitStack
        stack = ExitStack()
        for e in ENGINES:
            self._sems[e] = stack.enter_context(nc.semaphore("s_" + e))
        for i in range(self.NDMASEM):
            self._sems[("dma", i)] = stack.enter_context(
                nc.semaphore(f"sdma{i}"))

        def emit_engine(eng, ename):
            watermark = {}
            for op in self.ops[ename]:
                needed = {}
                for p in op.waits:
                    sem_name = ("dma", p.dma_sem) if p.dma else p.engine
                    v = p.inc_val
                    assert v is not None, (p.tag, p.engine, "missing inc")
                    if v > needed.get(sem_name, -1):
                        needed[sem_name] = v
                for sem_name, v in sorted(needed.items(), key=str):
                    if watermark.get(sem_name, 0) >= v:
                        continue
                    eng.wait_ge(self._sems[sem_name], v)
                    watermark[sem_name] = v
                inst = op.fn(eng)
                if isinstance(inst, (list, tuple)):
                    inst = inst[-1]
                if op.dma:
                    inst.then_inc(self._sems[("dma", op.dma_sem)], 16)
                elif op.needs_inc:
                    inst.then_inc(self._sems[op.engine], 1)

        with nc.Block() as block:
            @block.sync
            def _(eng):
                emit_engine(eng, "sync")

            @block.vector
            def _(eng):
                emit_engine(eng, "vector")

            @block.scalar
            def _(eng):
                emit_engine(eng, "scalar")

            @block.tensor
            def _(eng):
                emit_engine(eng, "tensor")

            @block.gpsimd
            def _(eng):
                emit_engine(eng, "gpsimd")
        return stack

    def mm(self, out_ap, lhsT_ap, rhs_ap, start, stop, rd=(), wr=(), tag="mm"):
        def fn(eng, o=out_ap, l=lhsT_ap, r=rhs_ap, st=start, sp=stop):
            return eng.matmul(o, l, r, start=st, stop=sp)
        return self.add("tensor", fn, rd=rd, wr=wr, tag=tag)

    def transpose(self, out_ap, in_ap, rd=(), wr=(), tag="tr"):
        ident = self.const("ident_bf16", np.eye(128, dtype=BFNP))
        p = in_ap.shape[0]
        def fn(eng, o=out_ap, i=in_ap, idb=ident, p=p):
            return eng.transpose(o, i, idb.ap()[0:p, 0:p])
        return self.add("tensor", fn, rd=tuple(rd) + (ident,), wr=wr, tag=tag)

    def transpose_f32(self, out_ap, in_ap, rd=(), wr=(), tag="trf"):
        ident = self.const("ident_f32", np.eye(128, dtype=np.float32))
        p = in_ap.shape[0]
        def fn(eng, o=out_ap, i=in_ap, idb=ident, p=p):
            return eng.transpose(o, i, idb.ap()[0:p, 0:p])
        return self.add("tensor", fn, rd=tuple(rd) + (ident,), wr=wr, tag=tag)

    def act(self, out_ap, in_ap, func, bias=0.0, scale=1.0, accum_out=None,
            rd=(), wr=(), tag="act", engine="scalar"):
        def fn(eng, o=out_ap, i=in_ap, f=func, b=bias, s=scale, a=accum_out):
            return eng.activation(o, i, f, bias=b, scale=s, accum_out=a)
        return self.add(engine, fn, rd=rd, wr=wr, tag=tag)

    def tt(self, out_ap, in0_ap, in1_ap, alu_op, rd=(), wr=(), tag="tt",
           engine="vector"):
        def fn(eng, o=out_ap, a=in0_ap, b=in1_ap, op=alu_op):
            return eng.tensor_tensor(o, a, b, op)
        return self.add(engine, fn, rd=rd, wr=wr, tag=tag)

    def ts(self, out_ap, in_ap, s1, s2, op0, op1=None, rd=(), wr=(),
           accum_out=None, tag="ts", engine="vector"):
        def fn(eng, o=out_ap, i=in_ap, a=s1, b=s2, p0=op0, p1=op1, ac=accum_out):
            if p1 is None and ac is None:
                return eng.tensor_single_scalar(o, i, a, p0)
            return eng.tensor_scalar(o, i, a, b, p0,
                                     p1 if p1 is not None else ALU.bypass,
                                     accum_out=ac)
        return self.add(engine, fn, rd=rd, wr=wr, tag=tag)

    def stt(self, out_ap, in0_ap, scalar, in1_ap, op0, op1, rd=(), wr=(),
            tag="stt", engine="vector", accum_out=None):
        def fn(eng, o=out_ap, a=in0_ap, s=scalar, b=in1_ap, p0=op0, p1=op1,
               ac=accum_out):
            return eng.scalar_tensor_tensor(o, a, s, b, p0, p1,
                                            accum_out=ac)
        return self.add(engine, fn, rd=rd, wr=wr, tag=tag)

    def copy(self, out_ap, in_ap, rd=(), wr=(), tag="copy", engine="vector"):
        if engine == "scalar":
            def fn(eng, o=out_ap, i=in_ap):
                return eng.copy(o, i)
        else:
            def fn(eng, o=out_ap, i=in_ap):
                return eng.tensor_copy(o, i)
        return self.add(engine, fn, rd=rd, wr=wr, tag=tag)

    def memset(self, ap, val, rd=(), wr=(), tag="memset", engine="vector"):
        def fn(eng, a=ap, v=val):
            return eng.memset(a, v)
        return self.add(engine, fn, rd=rd, wr=wr, tag=tag)

    def reduce(self, out_ap, in_ap, alu_op, rd=(), wr=(), negate=None,
               tag="red", engine="vector"):
        def fn(eng, o=out_ap, i=in_ap, op=alu_op, n=negate):
            return eng.tensor_reduce(o, i, AX.X, op, negate=n)
        return self.add(engine, fn, rd=rd, wr=wr, tag=tag)

    def scan(self, out_ap, d0_ap, d1_ap, initial, op0, op1, rd=(), wr=(),
             tag="scan", engine="vector"):
        def fn(eng, o=out_ap, a=d0_ap, b=d1_ap, i=initial, p0=op0, p1=op1):
            return eng.tensor_tensor_scan(o, a, b, i, p0, p1)
        return self.add(engine, fn, rd=rd, wr=wr, tag=tag)

    def recip(self, out_ap, in_ap, rd=(), wr=(), tag="recip"):
        def fn(eng, o=out_ap, i=in_ap):
            return eng.reciprocal(o, i)
        return self.add("vector", fn, rd=rd, wr=wr, tag=tag)



def pbf(pbuf):
    return pbuf.ap().bitcast(BF16)


INPUT_SHAPES = {
    "L0_attn_x": (8, 2048, 1024), "L0_attn_gamma": (1024,),
    "L0_attn_w_q": (1024, 128), "L0_attn_w_kv": (1024, 256),
    "L0_attn_w_out": (128, 1024),
    "L0_ffn_Wr": (1024, 8), "L0_ffn_br": (8,),
    "L0_ffn_W1": (8, 1024, 4096), "L0_ffn_b1": (8, 4096),
    "L0_ffn_W2": (8, 4096, 1024), "L0_ffn_b2": (8, 1024),
    "L1_attn_ln_gamma": (1024,), "L1_attn_ln_beta": (1024,),
    "L1_attn_W_in": (4096, 1024), "L1_attn_conv_w": (2048, 1, 4),
    "L1_attn_conv_b": (2048,), "L1_attn_W_xproj": (160, 2048),
    "L1_attn_W_dt": (2048, 128), "L1_attn_b_dt": (2048,),
    "L1_attn_A_log": (2048, 16), "L1_attn_D_param": (2048,),
    "L1_attn_W_out": (1024, 2048),
    "L1_ffn_Wg": (1024, 16),
    "L1_ffn_W1": (16, 1024, 2048), "L1_ffn_b1": (16, 2048),
    "L1_ffn_W11": (16, 1024, 2048), "L1_ffn_b11": (16, 2048),
    "L1_ffn_W2": (16, 2048, 1024), "L1_ffn_b2": (16, 1024),
    "L1_ffn_Ws1": (1024, 4096), "L1_ffn_bs1": (4096,),
    "L1_ffn_Ws11": (1024, 4096), "L1_ffn_bs11": (4096,),
    "L1_ffn_Ws2": (4096, 1024), "L1_ffn_bs2": (1024,),
    "L2_attn_Wq": (1024, 1024), "L2_attn_Wk": (1024, 256),
    "L2_attn_Wv": (1024, 256), "L2_attn_Wo": (1024, 1024),
    "L2_attn_norm_w": (1024,),
    "L2_ffn_W1": (1024, 4096), "L2_ffn_b1": (4096,),
    "L2_ffn_W2": (4096, 1024), "L2_ffn_b2": (1024,),
    "L3_attn_w_qkv": (3072, 1024), "L3_attn_b_qkv": (3072,),
    "L3_attn_w_o": (1024, 1024), "L3_attn_b_o": (1024,),
    "L3_ffn_gate_weight": (8, 1024), "L3_ffn_gate_up_proj": (8, 2048, 1024),
    "L3_ffn_down_proj": (8, 1024, 1024), "L3_ffn_similarity_matrix": (8, 8),
}


class Kb:

    def __init__(self, nc):
        self.nc = nc
        self.k = MK(nc)
        self.k.psum_init()
        self.inputs = {}
        self.dram_out = nc.dram_tensor("out", [T, D], F32,
                                       kind="ExternalOutput").ap()
        self.x = self.k.buf("x_res", [128, NT * 1024], F32)
        self.xn_fm = self.k.buf("xn_fm", [128, ND * 2048], BF16)
        self._cast_rr = 0
        self.debug = False
        self._dbg = {}
        _sq = self.k.buf("ln_sq", [128, 1024], F32)
        self.ln_sq = [_sq, _sq]
        self.bstg = self.k.buf("bias_stage", [128, 1024], F32)
        self.ln_st = self.k.buf("ln_stats", [128, 8 * NT], F32)
        self.ln_stage = [self.k.buf(f"ln_xnst{i}", [128, 1024], BF16)
                         for i in range(2)]
        self.ln_xn32 = _sq
        self.wstage = [self.k.buf(f"wstage{i}", [128, 1024], F32)
                       for i in range(2)]
        self._wtT = [self.k.buf(f"wtT{i}", [128, 1024], BF16)
                     for i in range(2)]
        self._wtT_rr = 0
        self._stage_rr = 0
        self._dram_scratch = {}

    def dram_scratch(self, name, shape, dtype):
        if name not in self._dram_scratch:
            t = self.nc.dram_tensor("scr_" + name, list(shape), dtype,
                                    kind="Internal")
            self._dram_scratch[name] = Buf("scr_" + name, t.ap(), self.k)
        return self._dram_scratch[name]

    def dbg_dump(self, name, buf, ap=None):
        if not self.debug:
            return
        ap = buf.ap() if ap is None else ap
        shp = list(ap.shape)
        dt_ = ap.dtype
        d = self.nc.dram_tensor("dbg_" + name, shp, dt_, kind="ExternalOutput").ap()
        self.k.dma("gpsimd", d, ap, rd=(buf,), wr=(), tag="dbg_" + name)
        self._dbg[name] = shp

    def input(self, name):
        if name not in self.inputs:
            shp = INPUT_SHAPES[name]
            if name == "L0_attn_x":
                shp = (T, D)
            self.inputs[name] = self.nc.dram_tensor(
                name, list(shp), F32, kind="ExternalInput").ap()
        return self.inputs[name]

    def xt(self, ti):
        return self.x[:, ti * 1024:(ti + 1) * 1024]

    def xnf(self, dc):
        return self.xn_fm[:, dc * 2048:(dc + 1) * 2048]

    def cast_engine(self):
        self._cast_rr += 1
        return ("vector", "scalar", "gpsimd")[self._cast_rr % 3]

    def prewarm_consts(self):
        k = self.k
        k.const("ident_bf16", np.eye(128, dtype=BFNP))
        k.const("ident_f32", np.eye(128, dtype=np.float32))
        self.ones1()
        self.ccol(EPS_LN)
        self.ccol(1e-6)
        self.ccol(1.0)
        k.const("ones128_bf16", np.ones((128, 1), dtype=BFNP))
        selh = np.zeros((16, 8 * 128), np.float32)
        for h in range(8):
            for e in range(16):
                selh[e, h * 128 + h * 16 + e] = 1.0
        k.const("selH16", selh.astype(BFNP))
        k.const("sel_lo", np.concatenate(
            [np.ones((64, 1), np.float32),
             np.zeros((64, 1), np.float32)]).astype(BFNP))
        k.const("sel_hi", np.concatenate(
            [np.zeros((64, 1), np.float32),
             np.ones((64, 1), np.float32)]).astype(BFNP))
        hd = 64
        inv = (1.0 / np.power(np.float32(10000.0),
                              np.arange(0, hd, 2, dtype=np.float32) / np.float32(hd)))
        tpos = np.arange(T, dtype=np.float32)
        fr = np.outer(tpos, inv).astype(np.float32)
        emb = np.concatenate([fr, fr], axis=1)
        cos = np.cos(emb).astype(np.float32).T
        sin = np.sin(emb).astype(np.float32).T
        cos2 = np.concatenate([cos, cos], axis=0)
        sin2 = np.concatenate([sin, sin], axis=0)
        self._rope_dram = (
            self.nc.inline_tensor(np.ascontiguousarray((cos2 * 0.125).astype(BFNP)),
                                  name="c_ropecq"),
            self.nc.inline_tensor(np.ascontiguousarray((sin2 * 0.125).astype(BFNP)),
                                  name="c_ropesq"))
        rc = np.zeros((128, 128), np.float32)
        for i in range(128):
            if i % 64 < 32:
                rc[i + 32, i] = -1.0
            else:
                rc[i - 32, i] = 1.0
        k.const("ropeR", rc.astype(BFNP))

    def emit_load_x(self):
        k = self.k
        src = self.input("L0_attn_x")
        dst = self.x.ap().rearrange("p (tt d) -> p tt d", tt=NT)
        k.dma("sync", dst, src.rearrange("(tt p) d -> p tt d", p=128),
              rd=(), wr=(self.x,), tag="load_x")

    def emit_store_out(self):
        k = self.k
        src = self.x.ap().rearrange("p (tt d) -> p tt d", tt=NT)
        k.dma("gpsimd", self.dram_out.rearrange("(tt p) d -> p tt d", p=128),
              src, rd=(self.x,), wr=(), tag="store_out")

    def emit_ln(self, tag, add_back=False, inner_ln=False):
        k = self.k
        sq = self.ln_sq
        st = self.ln_st
        xn_stage = self.ln_stage
        xn32 = self.ln_xn32
        for ti in range(NT):
            xt = self.xt(ti)
            c0 = st[:, 8 * ti + 0:8 * ti + 1]
            c1 = st[:, 8 * ti + 1:8 * ti + 2]
            c2 = st[:, 8 * ti + 2:8 * ti + 3]
            c3 = st[:, 8 * ti + 3:8 * ti + 4]
            c4 = st[:, 8 * ti + 4:8 * ti + 5]
            c5 = st[:, 8 * ti + 5:8 * ti + 6]
            c6 = st[:, 8 * ti + 6:8 * ti + 7]
            c7 = st[:, 8 * ti + 7:8 * ti + 8]
            sqb = sq[ti % 2]
            k.act(sqb.ap(), xt, AF.Square, accum_out=c1,
                  rd=(self.x.slot(ti),), wr=(sqb, st.slot((ti, 1))),
                  tag=f"{tag}_sq{ti}")
            k.reduce(c0, xt, ALU.add, rd=(self.x.slot(ti),),
                     wr=(st.slot((ti, 0)),), tag=f"{tag}_sum{ti}")
            k.ts(c2, c0, 1.0 / D, None, ALU.mult,
                 rd=(st.slot((ti, 0)),), wr=(st.slot((ti, 2)),), tag=f"{tag}_mu")
            k.ts(c3, c1, 1.0 / D, None, ALU.mult,
                 rd=(st.slot((ti, 1)),), wr=(st.slot((ti, 3)),), tag=f"{tag}_ex2")
            k.stt(c4, c2, c2, c3, ALU.mult, ALU.subtract,
                  rd=(st.slot((ti, 2)), st.slot((ti, 3))),
                  wr=(st.slot((ti, 4)),), tag=f"{tag}_nv")
            epsc = self.ccol(EPS_LN)
            k.act(c4, c4, AF.Ln, bias=epsc.ap(), scale=-1.0,
                  rd=(st.slot((ti, 4)), epsc), wr=(st.slot((ti, 4)),),
                  tag=f"{tag}_ln")
            k.act(c5, c4, AF.Exp, scale=-0.5,
                  rd=(st.slot((ti, 4)),), wr=(st.slot((ti, 5)),),
                  tag=f"{tag}_rstd")
            applyc = c5
            applyslot = st.slot((ti, 5))
            if inner_ln:
                k.tt(c6, c5, c5, ALU.mult,
                     rd=(st.slot((ti, 5)),), wr=(st.slot((ti, 6)),),
                     tag=f"{tag}_r2")
                k.ts(c6, c6, -EPS_LN, 1.0 + EPS_LN, ALU.mult, ALU.add,
                     rd=(st.slot((ti, 6)),), wr=(st.slot((ti, 6)),),
                     tag=f"{tag}_u")
                k.act(c6, c6, AF.Ln, rd=(st.slot((ti, 6)),),
                      wr=(st.slot((ti, 6)),), tag=f"{tag}_lu")
                k.act(c6, c6, AF.Exp, scale=-0.5,
                      rd=(st.slot((ti, 6)),), wr=(st.slot((ti, 6)),),
                      tag=f"{tag}_ru")
                k.tt(c7, c5, c6, ALU.mult,
                     rd=(st.slot((ti, 5)), st.slot((ti, 6))),
                     wr=(st.slot((ti, 7)),), tag=f"{tag}_cmb")
                applyc = c7
                applyslot = st.slot((ti, 7))
            xs = xn_stage[ti % 2]
            k.ts(xs.ap(), xt, c2, applyc, ALU.subtract, ALU.mult,
                 rd=(self.x.slot(ti), st.slot((ti, 2)), applyslot),
                 wr=(xs,), tag=f"{tag}_apply")
            if add_back:
                k.ts(xn32.ap(), xt, c2, c5, ALU.subtract, ALU.mult,
                     rd=(self.x.slot(ti), st.slot((ti, 2)), st.slot((ti, 5))),
                     wr=(xn32,), tag=f"{tag}_apply32")
                k.tt(xt, xt, xn32.ap(), ALU.add,
                     rd=(self.x.slot(ti), xn32), wr=(self.x.slot(ti),),
                     tag=f"{tag}_addback")
            pb = k.P[2 + (ti % 2)]
            pv = pbf(pb)
            for dc in range(ND):
                k.transpose(pv[:, dc * 128:(dc + 1) * 128],
                            xs[:, dc * 128:(dc + 1) * 128],
                            rd=(xs,), wr=(pb.slot(0),), tag=f"{tag}_tr")
            xn3 = self.xn_fm.ap().rearrange("p (dc t) -> p dc t", dc=ND)
            k.copy(xn3[:, :, ti * 128:(ti + 1) * 128],
                   pv[:, 0:1024].rearrange("p (dc t) -> p dc t", dc=ND),
                   rd=(pb.slot(0),),
                   wr=(self.xn_fm.slot(ti),),
                   tag=f"{tag}_evac", engine="scalar")

    def load_weight_cols(self, tag, dram_ap, dest, rows, col0, ncols,
                         scale=None, stage_words=1024, queue="sync",
                         scale_col=None):
        k = self.k
        nrc = rows // 128
        assert dest.ap().shape[1] >= nrc * ncols
        cbs = min(ncols, 1024)
        for cb in range(0, ncols, cbs):
            nc_ = min(cbs, ncols - cb)
            rstep = max(1, 1024 // nc_)
            if scale_col is not None:
                rstep = 1
            for rc in range(0, nrc, rstep):
                nr = min(rstep, nrc - rc)
                sg = self.wstage[self._stage_rr % 2]
                self._stage_rr += 1
                src_ap = dram_ap[rc * 128:(rc + nr) * 128,
                                 col0 + cb:col0 + cb + nc_]
                k.dma(queue,
                      sg.ap()[:, 0:nr * nc_].rearrange(
                          "p (r c) -> p r c", r=nr),
                      src_ap.rearrange("(r p) c -> p r c", p=128),
                      rd=(), wr=(sg,), tag=f"{tag}_dma")
                eng = self.cast_engine()
                if scale_col is not None:
                    k.ts(dest.ap()[:, rc * ncols + cb:rc * ncols + cb + nc_],
                         sg.ap()[:, 0:nc_], scale_col[:, rc:rc + 1], None,
                         ALU.mult, rd=(sg, scale_col),
                         wr=(dest.slot((rc, cb)),), tag=f"{tag}_cast",
                         engine="vector")
                    continue
                if nr == 1 or ncols == cbs:
                    dst_ap = dest.ap()[:, rc * ncols + cb:
                                       rc * ncols + cb + nr * nc_] \
                        if ncols == cbs else \
                        dest.ap()[:, rc * ncols + cb:rc * ncols + cb + nc_]
                    if scale is None:
                        k.copy(dst_ap, sg.ap()[:, 0:nr * nc_],
                               rd=(sg,), wr=(dest.slot((rc, cb)),),
                               tag=f"{tag}_cast", engine=eng)
                    elif eng == "scalar":
                        def fn(e, o=dst_ap, i=sg.ap()[:, 0:nr * nc_],
                               s=scale):
                            return e.mul(o, i, s)
                        k.add(eng, fn, rd=(sg,),
                              wr=(dest.slot((rc, cb)),), tag=f"{tag}_cast")
                    else:
                        k.ts(dst_ap, sg.ap()[:, 0:nr * nc_], scale, None,
                             ALU.mult, rd=(sg,),
                             wr=(dest.slot((rc, cb)),), tag=f"{tag}_cast",
                             engine=eng)
                else:
                    d3 = dest.ap()[:, rc * ncols + cb:
                                   (rc + nr - 1) * ncols + cb + nc_]
                    d3 = dest.ap().rearrange(
                        "p (r c) -> p r c", c=ncols)[:, rc:rc + nr,
                                                     cb:cb + nc_]
                    s3 = sg.ap()[:, 0:nr * nc_].rearrange(
                        "p (r c) -> p r c", r=nr)
                    if scale is None:
                        k.copy(d3, s3, rd=(sg,),
                               wr=(dest.slot((rc, cb)),),
                               tag=f"{tag}_cast", engine=eng)
                    else:
                        k.ts(d3, s3, scale, None, ALU.mult, rd=(sg,),
                             wr=(dest.slot((rc, cb)),), tag=f"{tag}_cast",
                             engine=eng)

    def load_bias(self, tag, dram_ap, n, scale=None, queue="sync",
                  col=True, row=True, colb=None, rowb=None, rsb=None):
        k = self.k
        if col and n >= 128:
            nch = n // 128
            assert nch <= 128
            if colb is None:
                colb = k.buf(f"{tag}_col", [128, nch], F32)
            rs = rsb if rsb is not None else \
                Buf(self.bstg.name, self.bstg.ap()[0:min(nch, 128), 0:128],
                    k)
            k.dma(queue, rs.ap(), dram_ap.rearrange("(c p) -> c p", p=128),
                  rd=(), wr=(rs,), tag=f"{tag}_dmac")
            pb = self.k.P[3]
            k.transpose_f32(pb.ap()[:, 0:nch], rs.ap(), rd=(rs,),
                            wr=(pb.slot(0),), tag=f"{tag}_trc")
            if scale is None:
                k.copy(colb.ap(), pb.ap()[:, 0:nch],
                       rd=(pb.slot(0),), wr=(colb,), tag=f"{tag}_evc")
            else:
                k.ts(colb.ap(), pb.ap()[:, 0:nch], scale, None, ALU.mult,
                     rd=(pb.slot(0),), wr=(colb,), tag=f"{tag}_evc")
        if row:
            if rowb is None:
                if n <= 1024:
                    rowb = Buf(self.bstg.name,
                               self.bstg.ap()[0:1, 0:n], k)
                else:
                    rowb = k.buf(f"{tag}_row", [1, n], F32)
            k.dma(queue, rowb.ap(), dram_ap.unsqueeze(0),
                  rd=(), wr=(rowb,), tag=f"{tag}_dmar")
            if scale is not None:
                k.ts(rowb.ap(), rowb.ap(), scale, None, ALU.mult,
                     rd=(rowb,), wr=(rowb,), tag=f"{tag}_scr")
        return colb, rowb

    def bf_row(self, tag, rowb, n, dest=None):
        k = self.k
        b = dest if dest is not None else k.buf(tag + "_bf", [1, n], BF16)
        k.copy(b.ap(), rowb.ap(), rd=(rowb,), wr=(b,), tag=tag + "_cast")
        return b

    def ones1(self):
        return self.k.const("ones1_bf16", np.ones((1, 128), dtype=BFNP))

    def ccol(self, value):
        name = f"col_{value:.9e}".replace("-", "m").replace("+", "p").replace(".", "_")
        return self.k.const(name, np.full((128, 1), value, dtype=np.float32))

    def load_weight_T(self, tag, dram_ap, dest, row0, nrows, ncols,
                      scale=None, queue="sync", scale_col=None):
        k = self.k
        assert ncols % 128 == 0
        ngr = (nrows + 127) // 128
        ncc = ncols // 128
        assert dest.ap().shape[1] >= ncc * nrows
        for g in range(ngr):
            r0 = row0 + g * 128
            nr_p = min(128, row0 + nrows - r0)
            for c0 in range(0, ncols, 1024):
                nc_ = min(1024, ncols - c0)
                sg = self.wstage[self._stage_rr % 2]
                self._stage_rr += 1
                tmp = self._wtT[self._wtT_rr % 2]
                self._wtT_rr += 1
                k.dma(queue, sg.ap()[0:nr_p, 0:nc_],
                      dram_ap[r0:r0 + nr_p, c0:c0 + nc_],
                      rd=(), wr=(sg,), tag=f"{tag}_dma{g}")
                eng = self.cast_engine()
                if eng == "scalar" and scale is not None:
                    def fn(e, o=tmp.ap()[0:nr_p, 0:nc_],
                           i=sg.ap()[0:nr_p, 0:nc_], s=scale):
                        return e.mul(o, i, s)
                    k.add(eng, fn, rd=(sg,), wr=(tmp,), tag=f"{tag}_cast")
                elif scale is not None:
                    k.ts(tmp.ap()[0:nr_p, 0:nc_], sg.ap()[0:nr_p, 0:nc_],
                         scale, None, ALU.mult, rd=(sg,), wr=(tmp,),
                         tag=f"{tag}_cast", engine=eng)
                else:
                    k.copy(tmp.ap()[0:nr_p, 0:nc_], sg.ap()[0:nr_p, 0:nc_],
                           rd=(sg,), wr=(tmp,), tag=f"{tag}_cast", engine=eng)
                ncb = nc_ // 128
                for cb in range(0, ncb, 8):
                    nb = min(8, ncb - cb)
                    pb = self.k.P[3]
                    pv = pbf(pb)
                    for j in range(nb):
                        k.transpose(pv[0:128, j * 128:j * 128 + nr_p],
                                    tmp.ap()[0:nr_p, (cb + j) * 128:
                                             (cb + j + 1) * 128],
                                    rd=(tmp,), wr=(pb.slot(0),),
                                    tag=f"{tag}_tr")
                    d3 = dest.ap().rearrange("p (cc r) -> p cc r", r=nrows)
                    cc0 = c0 // 128 + cb
                    if scale_col is None:
                        k.copy(d3[:, cc0:cc0 + nb, g * 128:g * 128 + nr_p],
                               pv[:, 0:nb * 128].rearrange(
                                   "p (cc r) -> p cc r", r=128)[:, :, 0:nr_p],
                               rd=(pb.slot(0),), wr=(dest.slot((g, cc0)),),
                               tag=f"{tag}_ev", engine="vector")
                    else:
                        for j in range(nb):
                            k.ts(d3[:, cc0 + j, g * 128:g * 128 + nr_p],
                                 pv[:, j * 128:j * 128 + nr_p],
                                 scale_col[:, cc0 + j:cc0 + j + 1], None,
                                 ALU.mult, rd=(pb.slot(0), scale_col),
                                 wr=(dest.slot((g, cc0 + j)),),
                                 tag=f"{tag}_ev", engine="vector")

    def emit_l3_attn(self, tag):
        k = self.k
        m = k.mark()
        wqkv = self.input("L3_attn_w_qkv")
        wo = self.input("L3_attn_w_o")
        SC = float(1.0 / np.sqrt(np.float32(1024.0)))
        bqc, _ = self.load_bias(tag + "_bq", self.input("L3_attn_b_qkv"),
                                3072, row=False)
        k.ts(bqc.ap()[:, 0:8], bqc.ap()[:, 0:8], SC, None, ALU.mult,
             rd=(bqc,), wr=(bqc,), tag=tag + "_bqsc")
        qb = k.buf(tag + "_q", [128, 8 * 2048], BF16)
        kb_ = k.buf(tag + "_k", [128, 8 * 2048], BF16)
        vtm = self.dram_scratch("vtm16", [16, 128, 16, 64], BF16)
        mp = k.mark()
        wT = [k.buf(f"{tag}_wT{i}", [128, 1024], BF16) for i in range(2)]
        vfm = [k.buf(f"{tag}_vf{i}", [128, 512], BF16) for i in range(2)]
        vstg = [k.buf(f"{tag}_vs{i}", [128, 512], BF16) for i in range(2)]
        for oc in range(24):
            wt = wT[oc % 2]
            self.load_weight_T(f"{tag}_w{oc}", wqkv, wt, oc * 128, 128, 1024,
                               queue="sync")
            for ts4 in range(4):
                pb = k.P[ts4 % 2]
                pv = pb.ap()[:, 0:512]
                for dc in range(ND):
                    k.mm(pv, wt.ap()[:, dc * 128:(dc + 1) * 128],
                         self.xnf(dc)[:, ts4 * 512:(ts4 + 1) * 512],
                         start=(dc == 0), stop=(dc == ND - 1),
                         rd=(wt, self.xn_fm), wr=(pb.slot(0),),
                         tag=f"{tag}_qkv")
                if oc < 16:
                    dst = qb if oc < 8 else kb_
                    dc_out = oc % 8
                    scl = SC if oc < 8 else 1.0
                    k.ts(dst.ap()[:, dc_out * 2048 + ts4 * 512:
                                  dc_out * 2048 + (ts4 + 1) * 512],
                         pv, bqc[:, oc:oc + 1], scl, ALU.add, ALU.mult,
                         rd=(pb.slot(0), bqc), wr=(dst.slot((dc_out, ts4)),),
                         tag=f"{tag}_ev")
                else:
                    vc = oc - 16
                    vf = vfm[ts4 % 2]
                    vs = vstg[ts4 % 2]
                    k.ts(vf.ap(), pv, bqc[:, oc:oc + 1], None, ALU.add,
                         rd=(pb.slot(0), bqc), wr=(vf,), tag=f"{tag}_vev")
                    pt = k.P[2]
                    ptv = pbf(pt)
                    for tt in range(4):
                        k.transpose(ptv[:, tt * 128:(tt + 1) * 128],
                                    vf.ap()[:, tt * 128:(tt + 1) * 128],
                                    rd=(vf,), wr=(pt.slot(0),),
                                    tag=f"{tag}_vtr")
                    k.copy(vs.ap(), ptv[:, 0:512], rd=(pt.slot(0),),
                           wr=(vs,), tag=f"{tag}_vsc", engine="vector")
                    s4 = vs.ap().rearrange("p (tt g c) -> p tt g c",
                                           g=2, c=64)
                    for gg in range(2):
                        k.dma("gpsimd",
                              vtm.ap()[vc * 2 + gg, :,
                                       ts4 * 4:(ts4 + 1) * 4, :],
                              s4[:, :, gg, :],
                              rd=(vs,), wr=(vtm.slot((vc, ts4, gg)),),
                              tag=f"{tag}_vdma")
        k.release(mp)
        self._attn_core(tag, qb, kb_, 16, 16, vtm, self.xn_fm)
        k.release(m)
        m2 = k.mark()
        woT = k.buf(tag + "_woT", [128, 8192], BF16)
        self.load_weight_T(tag + "_wo", wo, woT, 0, 1024, 1024, queue="sync")
        _, bor = self.load_bias(tag + "_bo", self.input("L3_attn_b_o"), 1024,
                                col=False)
        borb = self.bf_row(tag + "_bo", bor, 1024)
        self._proj_out_tm(tag, self.xn_fm, woT, borb)
        k.release(m2)

    def _proj_out_tm(self, tag, src_fm, wT, bias_row_bf, scale_col=None):
        k = self.k
        ones1 = self.ones1()
        s3 = src_fm.ap().rearrange("p (cc t) -> p cc t", t=2048)
        for ti in range(NT):
            pb = k.P[2 + ti % 2]
            for cc in range(ND):
                for jh in range(2):
                    k.mm(pb.ap()[:, jh * 512:(jh + 1) * 512],
                         s3[:, cc, ti * 128:(ti + 1) * 128],
                         wT.ap()[:, cc * 1024 + jh * 512:
                                 cc * 1024 + (jh + 1) * 512],
                         start=(cc == 0),
                         stop=(cc == ND - 1 and bias_row_bf is None),
                         rd=(src_fm, wT), wr=(pb,), tag=f"{tag}_o")
            if bias_row_bf is not None:
                for jh in range(2):
                    k.mm(pb.ap()[:, jh * 512:(jh + 1) * 512], ones1.ap(),
                         bias_row_bf.ap()[:, jh * 512:(jh + 1) * 512],
                         start=False, stop=True,
                         rd=(ones1, bias_row_bf), wr=(pb,), tag=f"{tag}_ob")
            if scale_col is None:
                k.tt(self.xt(ti), pb.ap(), self.xt(ti), ALU.add,
                     rd=(pb, self.x.slot(ti)), wr=(self.x.slot(ti),),
                     tag=f"{tag}_acc")
            else:
                k.stt(self.xt(ti), pb.ap(), scale_col(ti), self.xt(ti),
                      ALU.mult, ALU.add,
                      rd=(pb, self.x.slot(ti)), wr=(self.x.slot(ti),),
                      tag=f"{tag}_acc")

    def _attn_core(self, tag, qb, kb_, nheads, kv_heads, vtm, att_dst,
                   exp_scale=1.0):
        k = self.k
        grp = nheads // kv_heads
        q3 = qb.ap().rearrange("p (c t) -> p c t", t=2048)
        k3 = kb_.ap().rearrange("p (c t) -> p c t", t=2048)
        a3 = att_dst.ap().rearrange("p (c t) -> p c t", t=2048)
        sel = [k.const("sel_lo", np.concatenate(
                   [np.ones((64, 1), np.float32),
                    np.zeros((64, 1), np.float32)]).astype(BFNP)),
               k.const("sel_hi", np.concatenate(
                   [np.zeros((64, 1), np.float32),
                    np.ones((64, 1), np.float32)]).astype(BFNP))]
        ones1 = self.ones1()
        sqt = k.buf(tag + "_sqt", [128, 1024], BF16)
        nq = nheads + kv_heads
        nrm = k.buf(tag + "_nrm", [1, nq + nheads], F32)
        bbf = k.buf(tag + "_bbf", [1, nheads], BF16)
        nbs = k.buf(tag + "_nbs", [128, nheads], F32)
        red = k.buf(tag + "_red", [1, 2], F32)
        if grp == 1:
            kspecs = [(kb_, k3, g // 2, g % 2, nheads + g)
                      for g in range(kv_heads)]
        else:
            kspecs = [(kb_, k3, g, 0, nheads + g) for g in range(kv_heads)]
        specs = [(qb, q3, h // 2, h % 2, h) for h in range(nheads)] + kspecs
        lastc = None
        for (src, s3, hc, half, col_i) in specs:
            if (src is qb and half == 0) or (src is kb_ and half == 0):
                pass
            col = nrm[:, col_i:col_i + 1]
            for rnd in range(2):
                k.act(sqt.ap(), s3[:, hc, rnd * 1024:(rnd + 1) * 1024],
                      AF.Square, rd=(src,), wr=(sqt,), tag=f"{tag}_nsq",
                      engine="scalar")
                pb = k.P[3]
                for jj in range(2):
                    k.mm(pb.ap()[0:1, jj * 512:(jj + 1) * 512],
                         sel[half].ap(),
                         sqt.ap()[:, jj * 512:(jj + 1) * 512],
                         start=True, stop=True,
                         rd=(sqt, sel[half]), wr=(pb.slot(jj),),
                         tag=f"{tag}_nmm")
                k.reduce(red[:, rnd:rnd + 1], pb.ap()[0:1, 0:1024], ALU.max,
                         rd=(pb.slot(0), pb.slot(1)),
                         wr=(red.slot(rnd),), tag=f"{tag}_nred")
            k.reduce(col, red.ap()[:, 0:2], ALU.max,
                     rd=(red,), wr=(nrm.slot(col_i),), tag=f"{tag}_nmax")
        qrow = nrm.ap()[:, 0:nheads]
        krow = nrm.ap()[:, nheads:nq]
        brow = nrm.ap()[:, nq:nq + nheads]
        k.act(nrm.ap()[:, 0:nq], nrm.ap()[:, 0:nq], AF.Sqrt,
              rd=(nrm,), wr=(nrm,), tag=f"{tag}_sqr")
        for h in range(nheads):
            g = h // grp
            k.ts(brow[:, h:h + 1], qrow[:, h:h + 1],
                 krow[:, g:g + 1], None, ALU.mult,
                 rd=(nrm,), wr=(nrm.slot(("b", h)),), tag=f"{tag}_bm")
        k.ts(bbf.ap(), brow, -1.002 * exp_scale, None, ALU.mult,
             rd=tuple(nrm.slot(("b", h)) for h in range(nheads)),
             wr=(bbf,), tag=f"{tag}_bneg")
        for h in range(nheads):
            pb = k.P[3]
            k.mm(pb.ap()[:, 0:1], ones1.ap(), bbf.ap()[:, h:h + 1],
                 start=True, stop=True, rd=(ones1, bbf), wr=(pb.slot(0),),
                 tag=f"{tag}_bbc")
            k.copy(nbs.ap()[:, h:h + 1], pb.ap()[:, 0:1],
                   rd=(pb.slot(0),), wr=(nbs.slot(h),), tag=f"{tag}_bev")
        vext = [k.buf(f"{tag}_vx{i}", [128, 16 * 65], BF16) for i in range(2)]
        expb = [k.buf(f"{tag}_ex{i}", [128, 512], BF16) for i in range(2)]
        _dn = k.buf(tag + "_dn", [1, 512], F32)
        _db = k.buf(tag + "_db", [1, 512], BF16)
        _rb = k.buf(tag + "_rb", [64, 512], F32)
        denr = [_dn, _dn]
        denb = [_db, _db]
        rbs = [_rb, _rb]
        first = [True]
        lastg = [None]
        for h in range(nheads):
            hb = (h % 2) * 64
            hc = h // 2
            g = h // grp
            vx = vext[g % 2]
            if lastg[0] != g:
                lastg[0] = g
                k.memset(vx.ap(), 1.0, rd=(), wr=(vx,), tag=f"{tag}_vxm",
                         engine="gpsimd")
                vx3 = vx.ap().rearrange("p (c e) -> p c e", e=65)
                k.dma("gpsimd", vx3[:, :, 0:64], vtm.ap()[g],
                      rd=(vtm,), wr=(vx.slot("d"),), tag=f"{tag}_vxd")
            for qs in range(4):
                avb = k.P[2]
                av = avb.ap()[0:65, 512:1024]
                for kt in range(16):
                    sb = k.P[kt % 2]
                    sv = sb.ap()[:, 0:512]
                    if grp == 1:
                        kap = k3[hb:hb + 64, hc, kt * 128:(kt + 1) * 128]
                    else:
                        kap = k3[hb:hb + 64, g, kt * 128:(kt + 1) * 128]
                    k.mm(sv, kap,
                         q3[hb:hb + 64, hc, qs * 512:(qs + 1) * 512],
                         start=True, stop=True,
                         rd=(kb_, qb), wr=(sb.slot(0),), tag=f"{tag}_s")
                    eb = expb[kt % 2]
                    k.act(eb.ap(), sv, AF.Exp, bias=nbs[:, h:h + 1],
                          scale=exp_scale,
                          rd=(sb.slot(0), nbs.slot(h)), wr=(eb,),
                          tag=f"{tag}_e")
                    k.mm(av, vx.ap()[:, kt * 65:(kt + 1) * 65], eb.ap(),
                         start=(kt == 0), stop=(kt == 15),
                         rd=(vx, vx.slot("d"), eb), wr=(avb.slot(1),),
                         tag=f"{tag}_av")
                dn = denr[qs % 2]
                db = denb[qs % 2]
                rb = rbs[qs % 2]
                k.copy(dn.ap(), avb.ap()[64:65, 512:1024],
                       rd=(avb.slot(1),), wr=(dn,), tag=f"{tag}_dcp")
                if getattr(self, "dbg_attn", False) and h == 0 and qs == 0:
                    dbg_eb = k.buf(tag + "_dbgeb", [128, 512], BF16)
                    k.copy(dbg_eb.ap(), eb.ap(), rd=(eb,), wr=(dbg_eb,),
                           tag="dbg_eb_cp")
                    self.dbg_dump("den0", dn)
                    self.dbg_dump("eb15", dbg_eb)
                    dbg_av = k.buf(tag + "_dbgav", [65, 512], F32)
                    k.copy(dbg_av.ap(), avb.ap()[0:65, 512:1024],
                           rd=(avb.slot(1),), wr=(dbg_av,), tag="dbg_av_cp")
                    self.dbg_dump("av0", dbg_av)
                k.recip(dn.ap(), dn.ap(), rd=(dn,), wr=(dn,),
                        tag=f"{tag}_dre")
                k.copy(db.ap(), dn.ap(), rd=(dn,), wr=(db,),
                       tag=f"{tag}_dbf")
                pb2 = k.P[3]
                k.mm(pb2.ap()[0:64, 0:512], ones1.ap()[:, 0:64], db.ap(),
                     start=True, stop=True, rd=(ones1, db),
                     wr=(pb2.slot(0),), tag=f"{tag}_rbc")
                k.copy(rb.ap(), pb2.ap()[0:64, 0:512],
                       rd=(pb2.slot(0),), wr=(rb,), tag=f"{tag}_rbe")
                wr_item = att_dst if first[0] else att_dst.slot((h, qs))
                first[0] = False
                k.tt(a3[hb:hb + 64, hc, qs * 512:(qs + 1) * 512],
                     avb.ap()[0:64, 512:1024], rb.ap(), ALU.mult,
                     rd=(avb.slot(1), rb), wr=(wr_item,), tag=f"{tag}_anrm")

    def emit_l2_attn(self, tag):
        k = self.k
        m = k.mark()
        Wq = self.input("L2_attn_Wq")
        Wk = self.input("L2_attn_Wk")
        Wv = self.input("L2_attn_Wv")
        qb = k.buf(tag + "_q", [128, 8 * 2048], BF16)
        kd = k.buf(tag + "_kd", [128, 4 * 2048], BF16)
        vtm = self.dram_scratch("vtm4", [4, 128, 16, 64], BF16)
        mp = k.mark()
        kb_ = k.buf(tag + "_k", [128, 2 * 2048], BF16)
        _wq = k.buf(tag + "_wq", [128, ND * 512], BF16)
        wqs = [_wq, _wq]
        vfm = [k.buf(f"{tag}_vf{i}", [128, 512], BF16) for i in range(2)]
        vstg = [k.buf(f"{tag}_vs{i}", [128, 512], BF16) for i in range(2)]
        rtmp = k.buf(tag + "_rt", [128, 512], F32)
        cosq = k.buf(tag + "_cosq", [128, 2048], BF16)
        sinq = k.buf(tag + "_sinq", [128, 2048], BF16)
        k.dma("scalar", cosq.ap(), self._rope_dram[0].ap(),
              rd=(), wr=(cosq,), tag=tag + "_cql")
        k.dma("scalar", sinq.ap(), self._rope_dram[1].ap(),
              rd=(), wr=(sinq,), tag=tag + "_sql")
        rR = self.k._consts["ropeR"]
        for sl in range(2):
            wb = wqs[sl % 2]
            self.load_weight_cols(f"{tag}_wq{sl}", Wq, wb, 1024,
                                  sl * 512, 512, queue="sync")
            for oc4 in range(4):
                qc = sl * 4 + oc4
                for ts4 in range(4):
                    pb = k.P[ts4 % 2]
                    pv = pb.ap()[:, 0:512]
                    for dc in range(ND):
                        k.mm(pv, wb.ap()[:, dc * 512 + oc4 * 128:
                                         dc * 512 + (oc4 + 1) * 128],
                             self.xnf(dc)[:, ts4 * 512:(ts4 + 1) * 512],
                             start=(dc == 0), stop=(dc == ND - 1),
                             rd=(wb, self.xn_fm), wr=(pb.slot(0),),
                             tag=f"{tag}_qp")
                        k.copy(qb.ap()[:, qc * 2048 + ts4 * 512:
                                       qc * 2048 + (ts4 + 1) * 512],
                               pv, rd=(pb.slot(0),),
                               wr=(qb.slot((qc, ts4)),),
                               tag=f"{tag}_qev", engine="scalar") \
                            if dc == ND - 1 else None
        wkb = wqs[0]
        self.load_weight_cols(f"{tag}_wk", Wk, wkb, 1024, 0, 256,
                              queue="sync")
        for oc2 in range(2):
            for ts4 in range(4):
                pb = k.P[ts4 % 2]
                pv = pb.ap()[:, 0:512]
                for dc in range(ND):
                    k.mm(pv, wkb.ap()[:, dc * 256 + oc2 * 128:
                                      dc * 256 + (oc2 + 1) * 128],
                         self.xnf(dc)[:, ts4 * 512:(ts4 + 1) * 512],
                         start=(dc == 0), stop=(dc == ND - 1),
                         rd=(wkb, self.xn_fm), wr=(pb.slot(0),),
                         tag=f"{tag}_kp")
                k.copy(kb_.ap()[:, oc2 * 2048 + ts4 * 512:
                                oc2 * 2048 + (ts4 + 1) * 512],
                       pv, rd=(pb.slot(0),), wr=(kb_.slot((oc2, ts4)),),
                       tag=f"{tag}_kev", engine="scalar")
        for (buf_, nch, ct, st_) in ((qb, 8, cosq, sinq), (kb_, 2, cosq, sinq)):
            b3 = buf_.ap().rearrange("p (c t) -> p c t", t=2048)
            for hc in range(nch):
                for ts4 in range(4):
                    sl_ = slice(ts4 * 512, (ts4 + 1) * 512)
                    pb = k.P[2]
                    k.mm(pb.ap()[:, 0:512], rR.ap(), b3[:, hc, sl_],
                         start=True, stop=True,
                         rd=(buf_.slot((hc, ts4)), rR), wr=(pb.slot(0),),
                         tag=f"{tag}_rr")
                    k.tt(rtmp.ap(), pb.ap()[:, 0:512], st_.ap()[:, sl_],
                         ALU.mult, rd=(pb.slot(0), st_), wr=(rtmp,),
                         tag=f"{tag}_rs")
                    k.tt(b3[:, hc, sl_], b3[:, hc, sl_], ct.ap()[:, sl_],
                         ALU.mult, rd=(buf_.slot((hc, ts4)), ct),
                         wr=(buf_.slot((hc, ts4)),), tag=f"{tag}_rc")
                    k.tt(b3[:, hc, sl_], b3[:, hc, sl_], rtmp.ap(),
                         ALU.add, rd=(buf_.slot((hc, ts4)), rtmp),
                         wr=(buf_.slot((hc, ts4)),), tag=f"{tag}_ra")
        self.load_weight_cols(f"{tag}_wv", Wv, wkb, 1024, 0, 256,
                              queue="sync")
        for vc in range(2):
            for ts4 in range(4):
                pb = k.P[ts4 % 2]
                pv = pb.ap()[:, 0:512]
                for dc in range(ND):
                    k.mm(pv, wkb.ap()[:, dc * 256 + vc * 128:
                                      dc * 256 + (vc + 1) * 128],
                         self.xnf(dc)[:, ts4 * 512:(ts4 + 1) * 512],
                         start=(dc == 0), stop=(dc == ND - 1),
                         rd=(wkb, self.xn_fm), wr=(pb.slot(0),),
                         tag=f"{tag}_vp")
                vf = vfm[ts4 % 2]
                vs = vstg[ts4 % 2]
                k.copy(vf.ap(), pv, rd=(pb.slot(0),), wr=(vf,),
                       tag=f"{tag}_vev", engine="scalar")
                pt = k.P[2]
                ptv = pbf(pt)
                for tt in range(4):
                    k.transpose(ptv[:, tt * 128:(tt + 1) * 128],
                                vf.ap()[:, tt * 128:(tt + 1) * 128],
                                rd=(vf,), wr=(pt.slot(0),),
                                tag=f"{tag}_vtr")
                k.copy(vs.ap(), ptv[:, 0:512], rd=(pt.slot(0),),
                       wr=(vs,), tag=f"{tag}_vsc", engine="vector")
                s4 = vs.ap().rearrange("p (tt g c) -> p tt g c", g=2, c=64)
                for gg in range(2):
                    k.dma("gpsimd",
                          vtm.ap()[vc * 2 + gg, :, ts4 * 4:(ts4 + 1) * 4, :],
                          s4[:, :, gg, :],
                          rd=(vs,), wr=(vtm.slot((vc, ts4, gg)),),
                          tag=f"{tag}_vdma")
        kd3 = kd.ap().rearrange("p (c t) -> p c t", t=2048)
        k3_ = kb_.ap().rearrange("p (c t) -> p c t", t=2048)
        for g in range(4):
            for half in range(2):
                k.copy(kd3[half * 64:half * 64 + 64, g, :],
                       k3_[(g % 2) * 64:(g % 2) * 64 + 64, g // 2, :],
                       rd=(kb_.slot((g // 2, 0)), kb_.slot((g // 2, 1)),
                           kb_.slot((g // 2, 2)), kb_.slot((g // 2, 3))),
                       wr=(kd.slot((g, half)),), tag=f"{tag}_kdup",
                       engine="vector")
        k.release(mp)
        self._attn_core(tag, qb, kd, 16, 4, vtm, self.xn_fm, exp_scale=8.0)
        k.release(m)
        m2 = k.mark()
        woN = k.buf(tag + "_wo", [128, 8192], BF16)
        self.load_weight_cols(tag + "_won", self.input("L2_attn_Wo"), woN,
                              1024, 0, 1024, queue="sync")
        _, nwr = self.load_bias(tag + "_nw", self.input("L2_attn_norm_w"),
                                1024, col=False)
        nwb = k.buf(tag + "_nwb", [128, 1024], F32)
        nwbf = self.bf_row(tag + "_nw", nwr, 1024)
        ones1 = self.ones1()
        pb0 = k.P[3]
        for jh in range(2):
            k.mm(pb0.ap()[:, jh * 512:(jh + 1) * 512], ones1.ap(),
                 nwbf.ap()[:, jh * 512:(jh + 1) * 512],
                 start=True, stop=True, rd=(ones1, nwbf),
                 wr=(pb0.slot(jh),), tag=f"{tag}_nwb")
        k.copy(nwb.ap(), pb0.ap(), rd=(pb0.slot(0), pb0.slot(1)),
               wr=(nwb,), tag=f"{tag}_nwe", engine="vector")
        att3 = self.xn_fm.ap().rearrange("p (cc t) -> p cc t", t=2048)
        sqs = self.ln_sq[0]
        st = self.ln_st
        eps6 = self.ccol(1e-6)
        t2b = k.buf(tag + "_t2", [128, 1024], F32)
        for ti in range(NT):
            pb = k.P[2 + ti % 2]
            for cc in range(ND):
                for jh in range(2):
                    k.mm(pb.ap()[:, jh * 512:(jh + 1) * 512],
                         att3[:, cc, ti * 128:(ti + 1) * 128],
                         woN.ap()[:, cc * 1024 + jh * 512:
                                  cc * 1024 + (jh + 1) * 512],
                         start=(cc == 0), stop=(cc == ND - 1),
                         rd=(self.xn_fm, woN), wr=(pb,), tag=f"{tag}_o")
            c0 = st[:, 8 * ti:8 * ti + 1]
            c1 = st[:, 8 * ti + 1:8 * ti + 2]
            k.act(sqs.ap(), pb.ap(), AF.Square, accum_out=c0,
                  rd=(pb,), wr=(sqs, st.slot((ti, 0))), tag=f"{tag}_sq")
            k.act(c1, c0, AF.Ln, bias=eps6.ap(), scale=1.0 / D,
                  rd=(st.slot((ti, 0)), eps6), wr=(st.slot((ti, 1)),),
                  tag=f"{tag}_ln")
            k.act(c1, c1, AF.Exp, scale=-0.5,
                  rd=(st.slot((ti, 1)),), wr=(st.slot((ti, 1)),),
                  tag=f"{tag}_ex")
            k.stt(t2b.ap(), pb.ap(), c1, nwb.ap(), ALU.mult, ALU.mult,
                  rd=(pb, st.slot((ti, 1)), nwb), wr=(t2b,),
                  tag=f"{tag}_sc")
            k.tt(self.xt(ti), t2b.ap(), self.xt(ti), ALU.add,
                 rd=(t2b, self.x.slot(ti)), wr=(self.x.slot(ti),),
                 tag=f"{tag}_acc")
        k.release(m2)

    def emit_router(self, tag, E, wr_dram, br_dram, gates, halve=True):
        k = self.k
        wrb = k.buf(tag + "_wr", [128, ND * E], BF16)
        self.load_weight_cols(tag + "_wrl", wr_dram, wrb, 1024, 0, E,
                              queue="scalar")
        brb = None
        if br_dram is not None:
            _, br = self.load_bias(tag + "_br", br_dram, E, col=False,
                                   queue="scalar")
            brb = self.bf_row(tag + "_br", br, E)
        ones1 = self.ones1()
        lgt = [k.buf(f"{tag}_lg{i}", [128, E], F32) for i in range(2)]
        msk = [k.buf(f"{tag}_mk{i}", [128, E], F32) for i in range(2)]
        tmp = [k.buf(f"{tag}_tp{i}", [128, E], F32) for i in range(2)]
        sc = k.buf(tag + "_sc", [128, 4 * NT], F32)
        for ti in range(NT):
            pb = k.P[3]
            pv = pb.ap()[:, 0:E]
            for dc in range(ND):
                k.mm(pv, self.xnf(dc)[:, ti * 128:(ti + 1) * 128],
                     wrb.ap()[:, dc * E:(dc + 1) * E],
                     start=(dc == 0), stop=(dc == ND - 1 and brb is None),
                     rd=(self.xn_fm, wrb), wr=(pb.slot(0),),
                     tag=f"{tag}_lmm")
            if brb is not None:
                k.mm(pv, ones1.ap(), brb.ap(), start=False, stop=True,
                     rd=(ones1, brb), wr=(pb.slot(0),), tag=f"{tag}_lb")
            lg = lgt[ti % 2]
            mk = msk[ti % 2]
            tp = tmp[ti % 2]
            s0 = sc[:, 4 * ti + 0:4 * ti + 1]
            s1 = sc[:, 4 * ti + 1:4 * ti + 2]
            s2 = sc[:, 4 * ti + 2:4 * ti + 3]
            s3 = sc[:, 4 * ti + 3:4 * ti + 4]
            k.copy(lg.ap(), pv, rd=(pb.slot(0),), wr=(lg,),
                   tag=f"{tag}_lev")
            k.reduce(s0, lg.ap(), ALU.max, rd=(lg,),
                     wr=(sc.slot((ti, 0)),), tag=f"{tag}_m1")
            k.ts(s1, s0, -1.0, None, ALU.mult, rd=(sc.slot((ti, 0)),),
                 wr=(sc.slot((ti, 1)),), tag=f"{tag}_nm1")
            k.ts(mk.ap(), lg.ap(), s0, None, ALU.is_ge,
                 rd=(lg, sc.slot((ti, 0))), wr=(mk,), tag=f"{tag}_eq1")
            k.stt(tp.ap(), mk.ap(), -1e30, lg.ap(), ALU.mult, ALU.add,
                  rd=(mk, lg), wr=(tp,), tag=f"{tag}_l2")
            k.reduce(s2, tp.ap(), ALU.max, rd=(tp,),
                     wr=(sc.slot((ti, 2)),), tag=f"{tag}_m2")
            k.ts(mk.ap(), lg.ap(), s2, None, ALU.is_ge,
                 rd=(lg, sc.slot((ti, 2))), wr=(mk,), tag=f"{tag}_keep")
            k.act(tp.ap(), lg.ap(), AF.Exp, bias=s1,
                  rd=(lg, sc.slot((ti, 1))), wr=(tp,), tag=f"{tag}_ex")
            k.tt(tp.ap(), tp.ap(), mk.ap(), ALU.mult, rd=(tp, mk),
                 wr=(tp,), tag=f"{tag}_gt")
            k.reduce(s3, tp.ap(), ALU.add, rd=(tp,),
                     wr=(sc.slot((ti, 3)),), tag=f"{tag}_dn")
            k.recip(s3, s3, rd=(sc.slot((ti, 3)),),
                    wr=(sc.slot((ti, 3)),), tag=f"{tag}_rc")
            if halve:
                k.ts(s3, s3, 0.5, None, ALU.mult,
                     rd=(sc.slot((ti, 3)),), wr=(sc.slot((ti, 3)),),
                     tag=f"{tag}_hv")
            k.ts(gates.ap()[:, ti * E:(ti + 1) * E], tp.ap(), s3, None,
                 ALU.mult, rd=(tp, sc.slot((ti, 3))),
                 wr=(gates.slot(ti),), tag=f"{tag}_g")

    def emit_l0_ffn(self, tag):
        k = self.k
        m = k.mark()
        E = 8
        gates = k.buf(tag + "_gates", [128, NT * E], F32)
        self.emit_router(tag + "_r", E, self.input("L0_ffn_Wr"),
                         self.input("L0_ffn_br"), gates)
        W1 = self.input("L0_ffn_W1")
        W2 = self.input("L0_ffn_W2")
        b1 = self.input("L0_ffn_b1")
        b2 = self.input("L0_ffn_b2")
        ones1 = self.ones1()
        w1s = [k.buf(f"{tag}_w1{i}", [128, ND * 512], BF16) for i in range(2)]
        hsb = k.buf(tag + "_h", [128, 32 * 512], BF16)
        w2c = [k.buf(f"{tag}_w2c{i}", [128, 1024], BF16) for i in range(2)]
        _b2r = k.buf(tag + "_b2r", [1, 1024], BF16)
        b2rb = [_b2r, _b2r]
        _b1c = k.buf(tag + "_b1c", [128, 32], F32)
        b1cb = [_b1c, _b1c]
        for e in range(E):
            b1c, _ = self.load_bias(f"{tag}_b1_{e}", b1[e], 4096, row=False,
                                    queue="scalar", colb=b1cb[e % 2])
            _, b2r = self.load_bias(f"{tag}_b2_{e}", b2[e], 1024, col=False,
                                    queue="scalar")
            b2b = self.bf_row(f"{tag}_b2_{e}", b2r, 1024, dest=b2rb[e % 2])
            for ts4 in range(4):
                for hs in range(8):
                    w1b = w1s[hs % 2]
                    self.load_weight_cols(f"{tag}_w1_{e}_{ts4}_{hs}", W1[e],
                                          w1b, 1024, hs * 512, 512,
                                          queue="sync")
                    for hc4 in range(4):
                        hc = hs * 4 + hc4
                        g = hc % 4
                        pb = k.P[g // 2]
                        sl = g % 2
                        pv = pb.ap()[:, sl * 512:(sl + 1) * 512]
                        for dc in range(ND):
                            k.mm(pv, w1b.ap()[:, dc * 512 + hc4 * 128:
                                              dc * 512 + (hc4 + 1) * 128],
                                 self.xnf(dc)[:, ts4 * 512:(ts4 + 1) * 512],
                                 start=(dc == 0), stop=(dc == ND - 1),
                                 rd=(w1b, self.xn_fm), wr=(pb.slot(sl),),
                                 tag=f"{tag}_hm")
                        k.act(hsb.ap()[:, hc * 512:(hc + 1) * 512], pv,
                              AF.Relu, bias=b1c[:, hc:hc + 1],
                              rd=(pb.slot(sl), b1c), wr=(hsb.slot(hc),),
                              tag=f"{tag}_hr")
                for hc in range(32):
                    wc = w2c[hc % 2]
                    self.load_weight_cols(f"{tag}_w2_{e}_{ts4}_{hc}",
                                          W2[e][hc * 128:(hc + 1) * 128, :],
                                          wc, 128, 0, 1024, queue="gpsimd")
                    for ti in range(4):
                        pb = k.P[ti]
                        for jh in range(2):
                            k.mm(pb.ap()[:, jh * 512:(jh + 1) * 512],
                                 hsb.ap()[:, hc * 512 + ti * 128:
                                          hc * 512 + (ti + 1) * 128],
                                 wc.ap()[:, jh * 512:(jh + 1) * 512],
                                 start=(hc == 0), stop=False,
                                 rd=(hsb.slot(hc), wc), wr=(pb,),
                                 tag=f"{tag}_ym")
                for ti in range(4):
                    pb = k.P[ti]
                    for jh in range(2):
                        k.mm(pb.ap()[:, jh * 512:(jh + 1) * 512],
                             ones1.ap(), b2b.ap()[:, jh * 512:(jh + 1) * 512],
                             start=False, stop=True,
                             rd=(ones1, b2b), wr=(pb,), tag=f"{tag}_yb")
                    tig = ts4 * 4 + ti
                    k.stt(self.xt(tig), pb.ap(),
                          gates[:, tig * E + e:tig * E + e + 1],
                          self.xt(tig), ALU.mult, ALU.add,
                          rd=(pb, gates.slot(tig), self.x.slot(tig)),
                          wr=(self.x.slot(tig),), tag=f"{tag}_acc")
        k.release(m)

    def emit_l1_ffn(self, tag):
        k = self.k
        E = 16
        m = k.mark()
        gates = k.buf(tag + "_gates", [128, NT * E], F32)
        self.emit_router(tag + "_r", E, self.input("L1_ffn_Wg"), None, gates)
        W1 = self.input("L1_ffn_W1")
        W11 = self.input("L1_ffn_W11")
        W2 = self.input("L1_ffn_W2")
        b1 = self.input("L1_ffn_b1")
        b11 = self.input("L1_ffn_b11")
        b2 = self.input("L1_ffn_b2")
        ones1 = self.ones1()
        _w1 = k.buf(tag + "_w1", [128, ND * 2048], BF16)
        w1b = [_w1, _w1]
        w11s = [k.buf(f"{tag}_w11{i}", [128, ND * 512], BF16)
                for i in range(2)]
        hsb = k.buf(tag + "_h", [128, 16 * 512], BF16)
        w2c = [k.buf(f"{tag}_w2c{i}", [128, 1024], BF16) for i in range(2)]
        t1b = k.buf(tag + "_t1", [128, 512], BF16)
        t11b = Buf(self.ln_sq[0].name, self.ln_sq[0].ap()[:, 0:512], k)
        b1cb = k.buf(tag + "_b1c", [128, 16], F32)
        b11cb = k.buf(tag + "_b11c", [128, 16], F32)
        b2rb = k.buf(tag + "_b2r", [1, 1024], BF16)
        for e in range(E):
            wa = w1b[e % 2]
            self.load_weight_cols(f"{tag}_w1_{e}", W1[e], wa, 1024, 0, 2048,
                                  queue="sync")
            b1c, _ = self.load_bias(f"{tag}_b1_{e}", b1[e], 2048, row=False,
                                    queue="scalar", colb=b1cb)
            b11c, _ = self.load_bias(f"{tag}_b11_{e}", b11[e], 2048,
                                     row=False, queue="scalar", colb=b11cb)
            _, b2r = self.load_bias(f"{tag}_b2_{e}", b2[e], 1024, col=False,
                                    queue="scalar")
            b2b = self.bf_row(f"{tag}_b2_{e}", b2r, 1024, dest=b2rb)
            for ts4 in range(4):
                for hc in range(16):
                    if hc % 4 == 0:
                        wb_ = w11s[(hc // 4) % 2]
                        self.load_weight_cols(
                            f"{tag}_w11_{e}_{ts4}_{hc}", W11[e], wb_, 1024,
                            (hc // 4) * 512, 512, queue="sync")
                    p1 = k.P[0]
                    p11 = k.P[1]
                    sl = hc % 2
                    hc4 = hc % 4
                    pv1 = p1.ap()[:, sl * 512:(sl + 1) * 512]
                    pv11 = p11.ap()[:, sl * 512:(sl + 1) * 512]
                    for dc in range(ND):
                        k.mm(pv1, wa.ap()[:, dc * 2048 + hc * 128:
                                          dc * 2048 + (hc + 1) * 128],
                             self.xnf(dc)[:, ts4 * 512:(ts4 + 1) * 512],
                             start=(dc == 0), stop=(dc == ND - 1),
                             rd=(wa, self.xn_fm), wr=(p1.slot(sl),),
                             tag=f"{tag}_h1m")
                    for dc in range(ND):
                        k.mm(pv11, wb_.ap()[:, dc * 512 + hc4 * 128:
                                            dc * 512 + (hc4 + 1) * 128],
                             self.xnf(dc)[:, ts4 * 512:(ts4 + 1) * 512],
                             start=(dc == 0), stop=(dc == ND - 1),
                             rd=(wb_, self.xn_fm), wr=(p11.slot(sl),),
                             tag=f"{tag}_h11m")
                    k.act(t1b.ap(), pv1, AF.Silu, bias=b1c[:, hc:hc + 1],
                          rd=(p1.slot(sl), b1cb), wr=(t1b,),
                          tag=f"{tag}_h1e")
                    k.ts(t11b.ap(), pv11, b11c[:, hc:hc + 1], None, ALU.add,
                         rd=(p11.slot(sl), b11cb), wr=(t11b,),
                         tag=f"{tag}_h11e")
                    k.tt(hsb.ap()[:, hc * 512:(hc + 1) * 512], t1b.ap(),
                         t11b.ap(), ALU.mult, rd=(t1b, t11b),
                         wr=(hsb.slot(hc),), tag=f"{tag}_hx")
                for hc in range(16):
                    wc = w2c[hc % 2]
                    self.load_weight_cols(f"{tag}_w2_{e}_{ts4}_{hc}",
                                          W2[e][hc * 128:(hc + 1) * 128, :],
                                          wc, 128, 0, 1024, queue="gpsimd")
                    for ti in range(4):
                        pb = k.P[ti]
                        for jh in range(2):
                            k.mm(pb.ap()[:, jh * 512:(jh + 1) * 512],
                                 hsb.ap()[:, hc * 512 + ti * 128:
                                          hc * 512 + (ti + 1) * 128],
                                 wc.ap()[:, jh * 512:(jh + 1) * 512],
                                 start=(hc == 0), stop=False,
                                 rd=(hsb.slot(hc), wc), wr=(pb,),
                                 tag=f"{tag}_ym")
                for ti in range(4):
                    pb = k.P[ti]
                    for jh in range(2):
                        k.mm(pb.ap()[:, jh * 512:(jh + 1) * 512],
                             ones1.ap(),
                             b2b.ap()[:, jh * 512:(jh + 1) * 512],
                             start=False, stop=True,
                             rd=(ones1, b2rb), wr=(pb,), tag=f"{tag}_yb")
                    tig = ts4 * 4 + ti
                    k.stt(self.xt(tig), pb.ap(),
                          gates[:, tig * E + e:tig * E + e + 1],
                          self.xt(tig), ALU.mult, ALU.add,
                          rd=(pb, gates.slot(tig), self.x.slot(tig)),
                          wr=(self.x.slot(tig),), tag=f"{tag}_acc")
        k.release(m)
        m2 = k.mark()
        Ws1 = self.input("L1_ffn_Ws1")
        Ws11 = self.input("L1_ffn_Ws11")
        Ws2 = self.input("L1_ffn_Ws2")
        bs1c, _ = self.load_bias(tag + "_bs1", self.input("L1_ffn_bs1"),
                                 4096, row=False, queue="scalar")
        bs11c, _ = self.load_bias(tag + "_bs11", self.input("L1_ffn_bs11"),
                                  4096, row=False, queue="scalar")
        _, bs2r = self.load_bias(tag + "_bs2", self.input("L1_ffn_bs2"),
                                 1024, col=False, scale=0.5, queue="scalar")
        bs2b = self.bf_row(tag + "_bs2", bs2r, 1024)
        ws1s = [k.buf(f"{tag}_ws1{i}", [128, ND * 512], BF16)
                for i in range(2)]
        ws11s = [k.buf(f"{tag}_ws11{i}", [128, ND * 512], BF16)
                 for i in range(2)]
        hsc = k.buf(tag + "_hs", [128, 4 * 512], BF16)
        w2cs = [k.buf(f"{tag}_w2s{i}", [128, 1024], BF16) for i in range(4)]
        for ts4 in range(4):
            for hs in range(8):
                wa, wb_ = ws1s[hs % 2], ws11s[hs % 2]
                self.load_weight_cols(f"{tag}_ws1_{ts4}_{hs}", Ws1, wa,
                                      1024, hs * 512, 512, queue="sync")
                self.load_weight_cols(f"{tag}_ws11_{ts4}_{hs}", Ws11, wb_,
                                      1024, hs * 512, 512, queue="sync")
                for hc4 in range(4):
                    hc = hs * 4 + hc4
                    p1, p11 = k.P[0], k.P[1]
                    sl = hc4 % 2
                    pv1 = p1.ap()[:, sl * 512:(sl + 1) * 512]
                    pv11 = p11.ap()[:, sl * 512:(sl + 1) * 512]
                    for dc in range(ND):
                        k.mm(pv1, wa.ap()[:, dc * 512 + hc4 * 128:
                                          dc * 512 + (hc4 + 1) * 128],
                             self.xnf(dc)[:, ts4 * 512:(ts4 + 1) * 512],
                             start=(dc == 0), stop=(dc == ND - 1),
                             rd=(wa, self.xn_fm), wr=(p1.slot(sl),),
                             tag=f"{tag}_s1m")
                    for dc in range(ND):
                        k.mm(pv11, wb_.ap()[:, dc * 512 + hc4 * 128:
                                            dc * 512 + (hc4 + 1) * 128],
                             self.xnf(dc)[:, ts4 * 512:(ts4 + 1) * 512],
                             start=(dc == 0), stop=(dc == ND - 1),
                             rd=(wb_, self.xn_fm), wr=(p11.slot(sl),),
                             tag=f"{tag}_s11m")
                    k.act(t1b.ap(), pv1, AF.Silu,
                          bias=bs1c[:, hs * 4 + hc4:hs * 4 + hc4 + 1],
                          rd=(p1.slot(sl), bs1c), wr=(t1b,),
                          tag=f"{tag}_s1e")
                    k.ts(t11b.ap(), pv11,
                         bs11c[:, hs * 4 + hc4:hs * 4 + hc4 + 1], None,
                         ALU.add, rd=(p11.slot(sl), bs11c), wr=(t11b,),
                         tag=f"{tag}_s11e")
                    k.tt(hsc.ap()[:, hc4 * 512:(hc4 + 1) * 512], t1b.ap(),
                         t11b.ap(), ALU.mult, rd=(t1b, t11b),
                         wr=(hsc.slot(hc4),), tag=f"{tag}_sx")
                for ti in range(4):
                    pb = k.P[2 + ti % 2]
                    for hc4 in range(4):
                        wc = w2cs[hc4]
                        if ti == 0:
                            self.load_weight_cols(
                                f"{tag}_ws2_{ts4}_{hs}_{hc4}",
                                Ws2[(hs * 4 + hc4) * 128:
                                    (hs * 4 + hc4 + 1) * 128, :],
                                wc, 128, 0, 1024, scale=0.5, queue="gpsimd")
                        for jh in range(2):
                            k.mm(pb.ap()[:, jh * 512:(jh + 1) * 512],
                                 hsc.ap()[:, hc4 * 512 + ti * 128:
                                          hc4 * 512 + (ti + 1) * 128],
                                 wc.ap()[:, jh * 512:(jh + 1) * 512],
                                 start=(hc4 == 0),
                                 stop=(hc4 == 3 and not (ts4 == 0 and False)
                                       and hs > 0),
                                 rd=(hsc.slot(hc4), wc), wr=(pb,),
                                 tag=f"{tag}_sym")
                    if hs == 0:
                        for jh in range(2):
                            k.mm(pb.ap()[:, jh * 512:(jh + 1) * 512],
                                 ones1.ap(),
                                 bs2b.ap()[:, jh * 512:(jh + 1) * 512],
                                 start=False, stop=True,
                                 rd=(ones1, bs2b), wr=(pb,),
                                 tag=f"{tag}_syb")
                    tig = ts4 * 4 + ti
                    k.tt(self.xt(tig), pb.ap(), self.xt(tig), ALU.add,
                         rd=(pb, self.x.slot(tig)), wr=(self.x.slot(tig),),
                         tag=f"{tag}_sacc")
        k.release(m2)

    def emit_l3_ffn(self, tag):
        k = self.k
        E = 8
        m = k.mark()
        ones1 = self.ones1()
        ones128 = k.const("ones128_bf16", np.ones((128, 1), dtype=BFNP))
        identf = self.k._consts["ident_f32"]
        gw = self.input("L3_ffn_gate_weight")
        gwT = k.buf(tag + "_gwT", [128, ND * E], BF16)
        sg = self.wstage[0]
        k.dma("scalar", sg.ap()[0:8, 0:1024], gw, rd=(), wr=(sg,),
              tag=tag + "_gwd")
        gwb = k.buf(tag + "_gwb", [8, 1024], BF16)
        k.copy(gwb.ap(), sg.ap()[0:8, 0:1024], rd=(sg,), wr=(gwb,),
               tag=tag + "_gwc")
        pb0 = k.P[3]
        pv0 = pbf(pb0)
        for dc in range(ND):
            k.transpose(pv0[:, dc * E:(dc + 1) * E],
                        gwb.ap()[:, dc * 128:(dc + 1) * 128],
                        rd=(gwb,), wr=(pb0.slot(0),), tag=tag + "_gwt")
        k.copy(gwT.ap(), pv0[:, 0:ND * E], rd=(pb0.slot(0),), wr=(gwT,),
               tag=tag + "_gwe")
        dense0 = k.buf(tag + "_d0", [128, NT * E], F32)
        combine = k.buf(tag + "_cmb", [128, NT * E], F32)
        lgt = [k.buf(f"{tag}_lg{i}", [128, E], F32) for i in range(2)]
        msk = [k.buf(f"{tag}_mk{i}", [128, E], F32) for i in range(2)]
        tmp = [k.buf(f"{tag}_tp{i}", [128, E], F32) for i in range(2)]
        eqb = [k.buf(f"{tag}_eb{i}", [128, E], BF16) for i in range(2)]
        sc = k.buf(tag + "_sc", [128, 4 * NT], F32)
        cnt_pb = k.P[2]
        for ti in range(NT):
            pb = k.P[3]
            pv = pb.ap()[:, 0:E]
            for dc in range(ND):
                k.mm(pv, self.xnf(dc)[:, ti * 128:(ti + 1) * 128],
                     gwT.ap()[:, dc * E:(dc + 1) * E],
                     start=(dc == 0), stop=(dc == ND - 1),
                     rd=(self.xn_fm, gwT), wr=(pb.slot(0),),
                     tag=f"{tag}_lmm")
            lg, mk, tp, eb = (lgt[ti % 2], msk[ti % 2], tmp[ti % 2],
                              eqb[ti % 2])
            s0 = sc[:, 4 * ti + 0:4 * ti + 1]
            s1 = sc[:, 4 * ti + 1:4 * ti + 2]
            s2 = sc[:, 4 * ti + 2:4 * ti + 3]
            s3 = sc[:, 4 * ti + 3:4 * ti + 4]
            k.copy(lg.ap(), pv, rd=(pb.slot(0),), wr=(lg,),
                   tag=f"{tag}_lev")
            k.reduce(s0, lg.ap(), ALU.max, rd=(lg,),
                     wr=(sc.slot((ti, 0)),), tag=f"{tag}_m0")
            k.ts(s1, s0, -1.0, None, ALU.mult, rd=(sc.slot((ti, 0)),),
                 wr=(sc.slot((ti, 1)),), tag=f"{tag}_nm0")
            k.act(lg.ap(), lg.ap(), AF.Exp, bias=s1,
                  rd=(lg, sc.slot((ti, 1))), wr=(lg,), tag=f"{tag}_ex")
            k.reduce(s2, lg.ap(), ALU.add, rd=(lg,),
                     wr=(sc.slot((ti, 2)),), tag=f"{tag}_sm")
            k.recip(s2, s2, rd=(sc.slot((ti, 2)),),
                    wr=(sc.slot((ti, 2)),), tag=f"{tag}_rc")
            k.ts(lg.ap(), lg.ap(), s2, None, ALU.mult,
                 rd=(lg, sc.slot((ti, 2))), wr=(lg,), tag=f"{tag}_pr")
            k.reduce(s0, lg.ap(), ALU.max, rd=(lg,),
                     wr=(sc.slot((ti, 0)),), tag=f"{tag}_m1")
            k.ts(mk.ap(), lg.ap(), s0, None, ALU.is_ge,
                 rd=(lg, sc.slot((ti, 0))), wr=(mk,), tag=f"{tag}_eq1")
            k.copy(eb.ap(), mk.ap(), rd=(mk,), wr=(eb,),
                   tag=f"{tag}_eqb")
            k.mm(cnt_pb.ap()[0:E, 0:1], eb.ap(), ones128.ap(),
                 start=(ti == 0), stop=(ti == NT - 1),
                 rd=(eb, ones128), wr=(cnt_pb.slot(0),),
                 tag=f"{tag}_cnt")
            k.stt(tp.ap(), mk.ap(), -1e30, lg.ap(), ALU.mult, ALU.add,
                  rd=(mk, lg), wr=(tp,), tag=f"{tag}_l2")
            k.reduce(s2, tp.ap(), ALU.max, rd=(tp,),
                     wr=(sc.slot((ti, 2)),), tag=f"{tag}_m2")
            k.ts(mk.ap(), lg.ap(), s2, None, ALU.is_ge,
                 rd=(lg, sc.slot((ti, 2))), wr=(mk,), tag=f"{tag}_keep")
            k.tt(tp.ap(), lg.ap(), mk.ap(), ALU.mult, rd=(lg, mk),
                 wr=(tp,), tag=f"{tag}_tw")
            k.reduce(s3, tp.ap(), ALU.add, rd=(tp,),
                     wr=(sc.slot((ti, 3)),), tag=f"{tag}_dn")
            k.recip(s3, s3, rd=(sc.slot((ti, 3)),),
                    wr=(sc.slot((ti, 3)),), tag=f"{tag}_rc2")
            k.ts(dense0.ap()[:, ti * E:(ti + 1) * E], tp.ap(), s3, None,
                 ALU.mult, rd=(tp, sc.slot((ti, 3))),
                 wr=(dense0.slot(ti),), tag=f"{tag}_d0w")
        sim = k.buf(tag + "_sim", [8, E], F32)
        k.dma("scalar", sim.ap(), self.input("L3_ffn_similarity_matrix"),
              rd=(), wr=(sim,), tag=tag + "_simd")
        cnt = k.buf(tag + "_cntb", [8, 1], F32)
        notp = k.buf(tag + "_notp", [8, 1], F32)
        k.copy(cnt.ap(), cnt_pb.ap()[0:E, 0:1], rd=(cnt_pb.slot(0),),
               wr=(cnt,), tag=tag + "_ce")
        k.ts(notp.ap(), cnt.ap(), 0.5, None, ALU.is_lt, rd=(cnt,),
             wr=(notp,), tag=tag + "_np")
        prow = k.buf(tag + "_prow", [1, E], F32)
        prowb = k.buf(tag + "_prowb", [1, E], BF16)
        k.transpose_f32(k.P[3].ap()[0:1, 0:E], notp.ap(), rd=(notp,),
                        wr=(k.P[3].slot(0),), tag=tag + "_ptr")
        k.ts(prow.ap(), k.P[3].ap()[0:1, 0:E], -1.0, 1.0, ALU.mult,
             ALU.add, rd=(k.P[3].slot(0),), wr=(prow,), tag=tag + "_pr1")
        k.copy(prowb.ap(), prow.ap(), rd=(prow,), wr=(prowb,),
               tag=tag + "_prb")
        pbc = k.buf(tag + "_pbc", [8, E], F32)
        k.mm(k.P[3].ap()[0:8, 0:E], ones1.ap()[:, 0:8], prowb.ap(),
             start=True, stop=True, rd=(ones1, prowb),
             wr=(k.P[3].slot(0),), tag=tag + "_pbm")
        k.copy(pbc.ap(), k.P[3].ap()[0:8, 0:E], rd=(k.P[3].slot(0),),
               wr=(pbc,), tag=tag + "_pbe")
        smm = k.buf(tag + "_smm", [8, E], F32)
        k.ts(smm.ap(), pbc.ap(), 1.0, 1e30, ALU.subtract, ALU.mult,
             rd=(pbc,), wr=(smm,), tag=tag + "_msk")
        k.tt(smm.ap(), sim.ap(), smm.ap(), ALU.add, rd=(sim, smm),
             wr=(smm,), tag=tag + "_sm2")
        bsim = k.buf(tag + "_bsim", [8, 1], F32)
        k.reduce(bsim.ap(), smm.ap(), ALU.max, rd=(smm,), wr=(bsim,),
                 tag=tag + "_bs")
        eqbm = k.buf(tag + "_eqbm", [8, E], F32)
        k.ts(eqbm.ap(), smm.ap(), bsim.ap(), None, ALU.is_ge,
             rd=(smm, bsim), wr=(eqbm,), tag=tag + "_eqb2")
        cond = k.buf(tag + "_cond", [8, 1], F32)
        k.ts(cond.ap(), bsim.ap(), 0.7, None, ALU.is_ge, rd=(bsim,),
             wr=(cond,), tag=tag + "_c1")
        k.tt(cond.ap(), cond.ap(), notp.ap(), ALU.mult, rd=(cond, notp),
             wr=(cond,), tag=tag + "_c2")
        Rb = k.buf(tag + "_R", [8, E], BF16)
        r1 = k.buf(tag + "_r1", [8, E], F32)
        k.ts(r1.ap(), eqbm.ap(), cond.ap(), None, ALU.mult,
             rd=(eqbm, cond), wr=(r1,), tag=tag + "_r1")
        invc = k.buf(tag + "_ivc", [8, 1], F32)
        k.ts(invc.ap(), cond.ap(), -1.0, 1.0, ALU.mult, ALU.add,
             rd=(cond,), wr=(invc,), tag=tag + "_iv")
        r2 = k.buf(tag + "_r2", [8, E], F32)
        k.ts(r2.ap(), identf.ap()[0:8, 0:E], invc.ap(), None, ALU.mult,
             rd=(identf, invc), wr=(r2,), tag=tag + "_r2")
        k.tt(r1.ap(), r1.ap(), r2.ap(), ALU.add, rd=(r1, r2), wr=(r1,),
             tag=tag + "_rs")
        k.copy(Rb.ap(), r1.ap(), rd=(r1,), wr=(Rb,), tag=tag + "_rb")
        d0b = [k.buf(f"{tag}_d0b{i}", [128, E], BF16) for i in range(2)]
        for ti in range(NT):
            lg, mk, tp = lgt[ti % 2], msk[ti % 2], tmp[ti % 2]
            db_ = d0b[ti % 2]
            s0 = sc[:, 4 * ti + 0:4 * ti + 1]
            s2 = sc[:, 4 * ti + 2:4 * ti + 3]
            k.copy(db_.ap(), dense0.ap()[:, ti * E:(ti + 1) * E],
                   rd=(dense0.slot(ti),), wr=(db_,), tag=f"{tag}_d0c")
            pb = k.P[3]
            pv2 = pbf(pb)
            k.transpose(pv2[0:E, 0:128], db_.ap(), rd=(db_,),
                        wr=(pb.slot(0),), tag=f"{tag}_d0t")
            d0T = k.buf(f"{tag}_d0T{ti % 2}", [8, 128], BF16) \
                if ti < 2 else self._l3f_d0T[ti % 2]
            if ti < 2:
                if not hasattr(self, "_l3f_d0T"):
                    self._l3f_d0T = [None, None]
                self._l3f_d0T[ti % 2] = d0T
            k.copy(d0T.ap(), pv2[0:8, 0:128], rd=(pb.slot(0),),
                   wr=(d0T,), tag=f"{tag}_d0Te")
            k.mm(pb.ap()[:, 0:E], d0T.ap(), Rb.ap(), start=True, stop=True,
                 rd=(d0T, Rb), wr=(pb.slot(0),), tag=f"{tag}_dmm")
            k.copy(lg.ap(), pb.ap()[:, 0:E], rd=(pb.slot(0),), wr=(lg,),
                   tag=f"{tag}_dev")
            k.reduce(s0, lg.ap(), ALU.max, rd=(lg,),
                     wr=(sc.slot((ti, 0)),), tag=f"{tag}_fm1")
            k.ts(mk.ap(), lg.ap(), s0, None, ALU.is_ge,
                 rd=(lg, sc.slot((ti, 0))), wr=(mk,), tag=f"{tag}_feq")
            k.stt(tp.ap(), mk.ap(), -1e30, lg.ap(), ALU.mult, ALU.add,
                  rd=(mk, lg), wr=(tp,), tag=f"{tag}_fl2")
            k.reduce(s2, tp.ap(), ALU.max, rd=(tp,),
                     wr=(sc.slot((ti, 2)),), tag=f"{tag}_fm2")
            k.ts(mk.ap(), lg.ap(), s2, None, ALU.is_ge,
                 rd=(lg, sc.slot((ti, 2))), wr=(mk,), tag=f"{tag}_fkp")
            k.tt(mk.ap(), mk.ap(), lg.ap(), ALU.mult, rd=(mk, lg),
                 wr=(mk,), tag=f"{tag}_fw")
            k.ts(tp.ap(), lg.ap(), 0.0, None, ALU.is_gt, rd=(lg,),
                 wr=(tp,), tag=f"{tag}_fpos")
            k.tt(mk.ap(), mk.ap(), tp.ap(), ALU.mult, rd=(mk, tp),
                 wr=(mk,), tag=f"{tag}_fw2")
            k.ts(combine.ap()[:, ti * E:(ti + 1) * E], mk.ap(), 0.5, None,
                 ALU.mult, rd=(mk,), wr=(combine.slot(ti),),
                 tag=f"{tag}_cw")
        GU = self.input("L3_ffn_gate_up_proj")
        DN = self.input("L3_ffn_down_proj")
        gupT = k.buf(tag + "_gupT", [128, ND * 2048], BF16)
        dnT = k.buf(tag + "_dnT", [128, ND * 1024], BF16)
        hsb = k.buf(tag + "_h", [128, 8 * 512], BF16)
        t1b = k.buf(tag + "_t1", [128, 512], BF16)
        t11b = Buf(self.ln_sq[0].name, self.ln_sq[0].ap()[:, 0:512], k)
        for e in range(E):
            self.load_weight_T(f"{tag}_gu_{e}", GU[e], gupT, 0, 2048, 1024,
                               queue="sync")
            self.load_weight_T(f"{tag}_dn_{e}", DN[e], dnT, 0, 1024, 1024,
                               queue="sync")
            for ts4 in range(4):
                for fc in range(8):
                    p1, p11 = k.P[0], k.P[1]
                    sl = fc % 2
                    pv1 = p1.ap()[:, sl * 512:(sl + 1) * 512]
                    pv11 = p11.ap()[:, sl * 512:(sl + 1) * 512]
                    for dc in range(ND):
                        k.mm(pv1, gupT.ap()[:, dc * 2048 + fc * 128:
                                            dc * 2048 + (fc + 1) * 128],
                             self.xnf(dc)[:, ts4 * 512:(ts4 + 1) * 512],
                             start=(dc == 0), stop=(dc == ND - 1),
                             rd=(gupT, self.xn_fm), wr=(p1.slot(sl),),
                             tag=f"{tag}_gm")
                    for dc in range(ND):
                        k.mm(pv11, gupT.ap()[:, dc * 2048 + 1024 + fc * 128:
                                             dc * 2048 + 1024 +
                                             (fc + 1) * 128],
                             self.xnf(dc)[:, ts4 * 512:(ts4 + 1) * 512],
                             start=(dc == 0), stop=(dc == ND - 1),
                             rd=(gupT, self.xn_fm), wr=(p11.slot(sl),),
                             tag=f"{tag}_um")
                    k.act(t1b.ap(), pv1, AF.Silu,
                          rd=(p1.slot(sl),), wr=(t1b,), tag=f"{tag}_ge")
                    k.copy(t11b.ap(), pv11, rd=(p11.slot(sl),),
                           wr=(t11b,), tag=f"{tag}_ue")
                    k.tt(hsb.ap()[:, fc * 512:(fc + 1) * 512], t1b.ap(),
                         t11b.ap(), ALU.mult, rd=(t1b, t11b),
                         wr=(hsb.slot(fc),), tag=f"{tag}_hx")
                for fc in range(8):
                    for ti in range(4):
                        pb = k.P[ti]
                        for jh in range(2):
                            k.mm(pb.ap()[:, jh * 512:(jh + 1) * 512],
                                 hsb.ap()[:, fc * 512 + ti * 128:
                                          fc * 512 + (ti + 1) * 128],
                                 dnT.ap()[:, fc * 1024 + jh * 512:
                                          fc * 1024 + (jh + 1) * 512],
                                 start=(fc == 0), stop=(fc == 7),
                                 rd=(hsb.slot(fc), dnT), wr=(pb,),
                                 tag=f"{tag}_ym")
                for ti in range(4):
                    pb = k.P[ti]
                    tig = ts4 * 4 + ti
                    k.stt(self.xt(tig), pb.ap(),
                          combine[:, tig * E + e:tig * E + e + 1],
                          self.xt(tig), ALU.mult, ALU.add,
                          rd=(pb, combine.slot(tig), self.x.slot(tig)),
                          wr=(self.x.slot(tig),), tag=f"{tag}_acc")
        k.release(m)

    def emit_l0_attn(self, tag):
        k = self.k
        m = k.mark()
        H, DH, DF = 8, 16, 273
        ones1 = self.ones1()
        ones128 = self.k._consts["ones128_bf16"]
        s4 = float(np.sqrt(np.sqrt(0.5)))
        sq = [k.buf(f"{tag}_sq{i}", [128, 512], BF16) for i in range(2)]
        srow = k.buf(tag + "_srow", [1, 2048], F32)
        srowb = k.buf(tag + "_srowb", [1, 2048], BF16)
        scol = k.buf(tag + "_scol", [128, NT], F32)
        for sl in range(4):
            pb = k.P[3]
            for dc in range(ND):
                k.act(sq[dc % 2].ap(),
                      self.xnf(dc)[:, sl * 512:(sl + 1) * 512], AF.Square,
                      rd=(self.xn_fm,), wr=(sq[dc % 2],),
                      tag=f"{tag}_sq", engine="scalar")
                k.mm(pb.ap()[0:1, sl % 2 * 512:sl % 2 * 512 + 512],
                     ones128.ap(), sq[dc % 2].ap(),
                     start=(dc == 0), stop=(dc == ND - 1),
                     rd=(sq[dc % 2], ones128), wr=(pb.slot(sl % 2),),
                     tag=f"{tag}_ssm")
            k.copy(srow.ap()[:, sl * 512:(sl + 1) * 512],
                   pb.ap()[0:1, sl % 2 * 512:sl % 2 * 512 + 512],
                   rd=(pb.slot(sl % 2),), wr=(srow.slot(sl),),
                   tag=f"{tag}_sev")
        k.act(srow.ap(), srow.ap(), AF.Sqrt,
              rd=tuple(srow.slot(i) for i in range(4)), wr=(srow,),
              tag=f"{tag}_ssq")
        k.ts(srow.ap(), srow.ap(), 1e-12, None, ALU.max, rd=(srow,),
             wr=(srow,), tag=f"{tag}_smx")
        k.recip(srow.ap(), srow.ap(), rd=(srow,), wr=(srow,),
                tag=f"{tag}_src")
        k.ts(srow.ap(), srow.ap(), 32.0, None, ALU.mult, rd=(srow,),
             wr=(srow,), tag=f"{tag}_s32")
        k.copy(srowb.ap(), srow.ap(), rd=(srow,), wr=(srowb,),
               tag=f"{tag}_sbf")
        for ti in range(NT):
            pb = k.P[3]
            k.mm(pb.ap()[:, 0:1], srowb.ap()[:, ti * 128:(ti + 1) * 128],
                 ones1.ap()[:, 0:1], start=True, stop=True,
                 rd=(srowb, ones1), wr=(pb.slot(0),), tag=f"{tag}_scm")
            k.copy(scol.ap()[:, ti:ti + 1], pb.ap()[:, 0:1],
                   rd=(pb.slot(0),), wr=(scol.slot(ti),),
                   tag=f"{tag}_sce")
        gmc, _ = self.load_bias(tag + "_gm", self.input("L0_attn_gamma"),
                                1024, row=False, queue="scalar")
        gqc = k.buf(tag + "_gq", [128, 8], F32)
        k.ts(gqc.ap(), gmc.ap(), 0.25, None, ALU.mult, rd=(gmc,),
             wr=(gqc,), tag=f"{tag}_gq")
        wq = k.buf(tag + "_wq", [128, ND * 128], BF16)
        wkv = k.buf(tag + "_wkv", [128, ND * 256], BF16)
        self.load_weight_cols(tag + "_wql", self.input("L0_attn_w_q"), wq,
                              1024, 0, 128, queue="sync", scale_col=gqc)
        self.load_weight_cols(tag + "_wkvl", self.input("L0_attn_w_kv"),
                              wkv, 1024, 0, 256, queue="sync",
                              scale_col=gmc)
        qf = k.buf(tag + "_qf", [128, 2048], BF16)
        kf = k.buf(tag + "_kf", [128, 2048], BF16)
        vf = k.buf(tag + "_vf", [128, 2048], BF16)
        for (dst, wsrc, co) in ((qf, wq, 0), (kf, wkv, 0), (vf, wkv, 128)):
            ncol = 128 if wsrc is wq else 256
            for sl in range(4):
                pb = k.P[2]
                pv = pb.ap()[:, (sl % 2) * 512:(sl % 2) * 512 + 512]
                for dc in range(ND):
                    k.mm(pv, wsrc.ap()[:, dc * ncol + co:dc * ncol + co + 128],
                         self.xnf(dc)[:, sl * 512:(sl + 1) * 512],
                         start=(dc == 0), stop=(dc == ND - 1),
                         rd=(wsrc, self.xn_fm), wr=(pb.slot(sl % 2),),
                         tag=f"{tag}_pj")
                k.copy(dst.ap()[:, sl * 512:(sl + 1) * 512], pv,
                       rd=(pb.slot(sl % 2),), wr=(dst.slot(sl),),
                       tag=f"{tag}_pje", engine="scalar")
        qtm = k.buf(tag + "_qtm", [128, NT * 128], BF16)
        ktm = k.buf(tag + "_ktm", [128, NT * 128], BF16)
        vtm = k.buf(tag + "_vtm", [128, NT * 128], BF16)
        for (src_, dst) in ((qf, qtm), (kf, ktm), (vf, vtm)):
            for ti in range(NT):
                pb = k.P[3]
                pv = pbf(pb)
                k.transpose(pv[:, 1024 + (ti % 4) * 128:
                               1024 + (ti % 4 + 1) * 128],
                            src_.ap()[:, ti * 128:(ti + 1) * 128],
                            rd=(src_,), wr=(pb.slot(1),), tag=f"{tag}_tr")
                k.ts(dst.ap()[:, ti * 128:(ti + 1) * 128],
                     pv[:, 1024 + (ti % 4) * 128:1024 + (ti % 4 + 1) * 128],
                     scol[:, ti:ti + 1], None, ALU.mult,
                     rd=(pb.slot(1), scol.slot(ti)), wr=(dst.slot(ti),),
                     tag=f"{tag}_trs")
        kp = k.buf(tag + "_kp", [128, NT * DF], BF16)
        qp = k.buf(tag + "_qp", [128, NT * DF], BF16)
        qpf = k.buf(tag + "_qpf", [128, 3 * 2048], BF16)
        kvs = k.buf(tag + "_kvs", [128, 3 * 32], BF16)
        ksb = k.buf(tag + "_ksb", [128, DF], BF16)
        ksr = k.buf(tag + "_ksr", [1, DF], BF16)
        qsb = k.buf(tag + "_qsb", [128, 16], F32)
        dummy = Buf(self.ln_sq[0].name, self.ln_sq[0].ap()[:, 0:DF], k)
        dcol = k.buf(tag + "_dcol", [128, NT], F32)
        numsb = k.buf(tag + "_num", [16, 512], BF16)
        selH = self.k._consts["selH16"]
        kvsl = [(k.P[2], 0, 0), (k.P[2], 1, 512), (k.P[3], 0, 0)]
        csz = [128, 128, 17]
        cof = [0, 128, 256]
        for h in range(H):
            hb = h * 16
            for (tm, pp, scale_q) in ((ktm, kp, False), (qtm, qp, True)):
                for ti in range(NT):
                    base = ti * DF
                    blk = tm.ap()[:, ti * 128 + hb:ti * 128 + hb + 16]
                    k.memset(pp.ap()[:, base:base + 1], 1.0, rd=(),
                             wr=(pp.slot((ti, "o")),), tag=f"{tag}_po",
                             engine="gpsimd")
                    k.copy(pp.ap()[:, base + 1:base + 17], blk,
                           rd=(tm.slot(ti),), wr=(pp.slot((ti, "l")),),
                           tag=f"{tag}_pl", engine="gpsimd")
                    k.ts(qsb.ap(), blk, s4, None, ALU.mult,
                         rd=(tm.slot(ti),), wr=(qsb,), tag=f"{tag}_ps")
                    for j in range(16):
                        k.ts(pp.ap()[:, base + 17 + j * 16:
                                     base + 17 + (j + 1) * 16],
                             qsb.ap(), qsb[:, j:j + 1], None, ALU.mult,
                             rd=(qsb,), wr=(pp.slot((ti, j)),),
                             tag=f"{tag}_p2",
                             engine="vector" if j % 2 else "gpsimd")
            for c in range(3):
                pbuf, slot, off = kvsl[c]
                for ti in range(NT):
                    lhs = kp.ap()[0:128, ti * DF + cof[c]:
                                  ti * DF + cof[c] + csz[c]]
                    k.mm(pbuf.ap()[0:csz[c], off:off + 16], lhs,
                         vtm.ap()[:, ti * 128 + hb:ti * 128 + hb + 16],
                         start=(ti == 0), stop=False,
                         rd=(kp.slot((ti, "o")), kp.slot((ti, "l")),
                             vtm.slot(ti)) +
                            tuple(kp.slot((ti, j)) for j in range(16)),
                         wr=(pbuf.slot(slot),), tag=f"{tag}_kvm")
                    k.mm(pbuf.ap()[0:csz[c], off + 16:off + 17], lhs,
                         ones128.ap(),
                         start=False, stop=(ti == NT - 1),
                         rd=(ones128,),
                         wr=(pbuf.slot(slot),), tag=f"{tag}_ksm")
                k.copy(kvs.ap()[0:csz[c], c * 32:c * 32 + 17],
                       pbuf.ap()[0:csz[c], off:off + 17],
                       rd=(pbuf.slot(slot),), wr=(kvs.slot(c),),
                       tag=f"{tag}_kve")
            for c in range(3):
                pb = k.P[3]
                pv = pbf(pb)
                k.transpose(pv[0:1, 1024 + cof[c]:1024 + cof[c] + csz[c]],
                            kvs.ap()[0:csz[c], c * 32 + 16:c * 32 + 17],
                            rd=(kvs.slot(c),), wr=(pb.slot(1),),
                            tag=f"{tag}_kst")
            k.copy(ksr.ap(), pbf(k.P[3])[0:1, 1024:1024 + DF],
                   rd=(k.P[3].slot(1),), wr=(ksr,), tag=f"{tag}_kse")
            pb = k.P[3]
            k.mm(pb.ap()[:, 512:512 + DF * 0 + 273] if False else
                 pb.ap()[0:128, 512:785] if False else
                 pb.ap()[:, 512:512 + 256], ones1.ap(),
                 ksr.ap()[:, 0:256], start=True, stop=True,
                 rd=(ones1, ksr), wr=(pb.slot(1),), tag=f"{tag}_ksb1")
            k.mm(pb.ap()[:, 768:768 + 17], ones1.ap(),
                 ksr.ap()[:, 256:273], start=True, stop=True,
                 rd=(ones1, ksr), wr=(pb.slot(1),), tag=f"{tag}_ksb2")
            k.copy(ksb.ap()[:, 0:256], pb.ap()[:, 512:768],
                   rd=(pb.slot(1),), wr=(ksb,), tag=f"{tag}_ksbe")
            k.copy(ksb.ap()[:, 256:273], pb.ap()[:, 768:785],
                   rd=(pb.slot(1),), wr=(ksb,), tag=f"{tag}_ksbe2")
            for ti in range(NT):
                base = ti * DF
                rdq = (qp.slot((ti, "o")), qp.slot((ti, "l"))) + \
                    tuple(qp.slot((ti, j)) for j in range(16))
                k.stt(dummy.ap(), qp.ap()[:, base:base + DF], 1.0,
                      ksb.ap(), ALU.mult, ALU.mult,
                      rd=rdq + (ksb,), wr=(dummy,),
                      tag=f"{tag}_den", engine="vector")
                k.reduce(dcol[:, ti:ti + 1], dummy.ap(), ALU.add,
                         rd=(dummy,), wr=(dcol.slot(ti),),
                         tag=f"{tag}_dred")
                k.ts(dcol[:, ti:ti + 1], dcol[:, ti:ti + 1], 1e-5, None,
                     ALU.max, rd=(dcol.slot(ti),), wr=(dcol.slot(ti),),
                     tag=f"{tag}_dmx")
                k.recip(dcol[:, ti:ti + 1], dcol[:, ti:ti + 1],
                        rd=(dcol.slot(ti),), wr=(dcol.slot(ti),),
                        tag=f"{tag}_drc")
                k.ts(qp.ap()[:, base:base + DF], qp.ap()[:, base:base + DF],
                     dcol[:, ti:ti + 1], None, ALU.mult,
                     rd=rdq + (dcol.slot(ti),),
                     wr=(qp.slot((ti, "all")),), tag=f"{tag}_qsc")
            for c in range(3):
                for ti in range(NT):
                    pb = k.P[3]
                    pv = pbf(pb)
                    k.transpose(pv[0:csz[c],
                                   1024 + (ti % 4) * 128:
                                   1024 + (ti % 4) * 128 + 128],
                                qp.ap()[:, ti * DF + cof[c]:
                                        ti * DF + cof[c] + csz[c]],
                                rd=(qp.slot((ti, "all")),),
                                wr=(pb.slot(1),), tag=f"{tag}_qtr")
                    k.copy(qpf.ap()[0:csz[c],
                                    c * 2048 + ti * 128:
                                    c * 2048 + (ti + 1) * 128],
                           pv[0:csz[c], 1024 + (ti % 4) * 128:
                              1024 + (ti % 4) * 128 + 128],
                           rd=(pb.slot(1),), wr=(qpf.slot((c, ti)),),
                           tag=f"{tag}_qte")
            for sl in range(4):
                pb = k.P[3]
                for c in range(3):
                    k.mm(pb.ap()[0:16, 512:512 + 512] if False else
                         pb.ap()[0:16, 512:1024],
                         kvs.ap()[0:csz[c], c * 32:c * 32 + 16],
                         qpf.ap()[0:csz[c],
                                  c * 2048 + sl * 512:c * 2048 +
                                  (sl + 1) * 512],
                         start=(c == 0), stop=(c == 2),
                         rd=(kvs.slot(c),) +
                            tuple(qpf.slot((c, sl * 4 + q4))
                                  for q4 in range(4)),
                         wr=(pb.slot(1),), tag=f"{tag}_num")
                k.copy(numsb.ap(), pb.ap()[0:16, 512:1024],
                       rd=(pb.slot(1),), wr=(numsb,), tag=f"{tag}_nev")
                att_pb = k.P[sl // 2]
                k.mm(att_pb.ap()[:, (sl % 2) * 512:(sl % 2) * 512 + 512],
                     selH.ap()[0:16, h * 128:(h + 1) * 128], numsb.ap(),
                     start=(h == 0), stop=(h == H - 1),
                     rd=(selH, numsb), wr=(att_pb.slot(sl % 2),),
                     tag=f"{tag}_pack")
        att = self.xn_fm
        first = True
        for sl in range(4):
            att_pb = k.P[sl // 2]
            k.copy(att.ap()[:, sl * 512:(sl + 1) * 512],
                   att_pb.ap()[:, (sl % 2) * 512:(sl % 2) * 512 + 512],
                   rd=(att_pb.slot(sl % 2),),
                   wr=(att,) if first else (att.slot(("a", sl)),),
                   tag=f"{tag}_atte", engine="scalar")
            first = False
        wo = k.buf(tag + "_wo", [128, 1024], BF16)
        self.load_weight_cols(tag + "_wol", self.input("L0_attn_w_out"),
                              wo, 128, 0, 1024, queue="sync")
        for ti in range(NT):
            pb = k.P[2 + ti % 2]
            for jh in range(2):
                k.mm(pb.ap()[:, jh * 512:(jh + 1) * 512],
                     att.ap()[:, ti * 128:(ti + 1) * 128],
                     wo.ap()[:, jh * 512:(jh + 1) * 512],
                     start=True, stop=True,
                     rd=(att, wo), wr=(pb,), tag=f"{tag}_om")
            k.tt(self.xt(ti), pb.ap(), self.xt(ti), ALU.add,
                 rd=(pb, self.x.slot(ti)), wr=(self.x.slot(ti),),
                 tag=f"{tag}_acc")
        k.release(m)

    def emit_l1_attn(self, tag):
        k = self.k
        m = k.mark()
        DI, DS = 2048, 16
        NTC, TC = 8, 256
        ones1 = self.ones1()
        ident = self.k._consts["ident_bf16"]
        one1c = self.ccol(1.0)
        gmc, _ = self.load_bias(tag + "_gm", self.input("L1_attn_ln_gamma"),
                                1024, row=False, queue="scalar")
        dtl = k.buf(tag + "_dtl", [128, 2048], BF16)
        bcb = k.buf(tag + "_bcb", [32, 2048], BF16)
        bbeta = k.buf(tag + "_bbeta", [128, 32], F32)
        zdram = self.dram_scratch("zscr", [16, 128, 2048], BF16)
        udram = self.dram_scratch("uscr", [16, 128, 2048], BF16)
        W_in = self.input("L1_attn_W_in")
        dummy = Buf(self.ln_sq[0].name, self.ln_sq[0].ap(), k)
        mp0 = k.mark()
        _, btr = self.load_bias(tag + "_bt", self.input("L1_attn_ln_beta"),
                                1024, col=False, queue="scalar")
        btb = self.bf_row(tag + "_bt", btr, 1024)
        betab = k.buf(tag + "_betab", [128, 1024], BF16)
        pb = k.P[3]
        for jh in range(2):
            k.mm(pb.ap()[:, jh * 512:(jh + 1) * 512], ones1.ap(),
                 btb.ap()[:, jh * 512:(jh + 1) * 512], start=True,
                 stop=True, rd=(ones1, btb), wr=(pb.slot(jh),),
                 tag=f"{tag}_btm")
        k.copy(betab.ap(), pb.ap(), rd=(pb.slot(0), pb.slot(1)),
               wr=(betab,), tag=f"{tag}_bte")
        for oc in range(32):
            sg = self.wstage[self._stage_rr % 2]
            self._stage_rr += 1
            k.dma("sync", sg.ap(), W_in[oc * 128:(oc + 1) * 128, :],
                  rd=(), wr=(sg,), tag=f"{tag}_wrd{oc}")
            k.stt(dummy.ap(), sg.ap(), 1.0, betab.ap(), ALU.mult, ALU.mult,
                  rd=(sg, betab), wr=(dummy, bbeta.slot(oc)),
                  accum_out=bbeta.ap()[:, oc:oc + 1], tag=f"{tag}_bb")
        k.release(mp0)
        if getattr(self, "l1_stop", None) == 0:
            k.release(m)
            return
        mp = k.mark()
        u = k.buf(tag + "_u", [128, 16 * 2048], BF16)
        mp1 = k.mark()
        wT = [k.buf(f"{tag}_wT{i}", [128, 1024], BF16) for i in range(2)]
        _xs = k.buf(tag + "_xs", [128, 2051], BF16)
        xsst = [_xs, _xs]
        zst = [k.buf(f"{tag}_zs{i}", [128, 512], BF16) for i in range(2)]
        cwf = k.buf(tag + "_cwf", [128, 64], F32)
        k.dma("scalar", cwf.ap().rearrange("p (c j) -> p c j", j=4),
              self.input("L1_attn_conv_w")
              .rearrange("(c p) b j -> p c (b j)", p=128),
              rd=(), wr=(cwf,), tag=tag + "_cwd")
        cbc, _ = self.load_bias(tag + "_cb", self.input("L1_attn_conv_b"),
                                2048, row=False, queue="scalar")
        for oc in range(32):
            wt = wT[oc % 2]
            self.load_weight_T(f"{tag}_wi{oc}", W_in, wt, oc * 128, 128,
                               1024, queue="sync", scale_col=gmc)
            fc = oc % 16
            if oc < 16:
                xst = xsst[oc % 2]
                k.memset(xst.ap()[:, 0:3], 0.0, rd=(), wr=(xst.slot("z"),),
                         tag=f"{tag}_xz0", engine="gpsimd")
            for ts4 in range(4):
                pb2 = k.P[ts4 % 2]
                pv = pb2.ap()[:, 0:512]
                for dc in range(ND):
                    k.mm(pv, wt.ap()[:, dc * 128:(dc + 1) * 128],
                         self.xnf(dc)[:, ts4 * 512:(ts4 + 1) * 512],
                         start=(dc == 0), stop=(dc == ND - 1),
                         rd=(wt, self.xn_fm), wr=(pb2.slot(0),),
                         tag=f"{tag}_xm")
                if oc < 16:
                    k.ts(xst.ap()[:, 3 + ts4 * 512:3 + (ts4 + 1) * 512],
                         pv, bbeta[:, oc:oc + 1], None, ALU.add,
                         rd=(pb2.slot(0), bbeta.slot(oc)),
                         wr=(xst.slot(ts4),), tag=f"{tag}_xe")
                else:
                    zb = zst[ts4 % 2]
                    k.ts(dummy.ap()[:, 0:512], pv, bbeta[:, oc:oc + 1],
                         None, ALU.add,
                         rd=(pb2.slot(0), bbeta.slot(oc)),
                         wr=(dummy,), tag=f"{tag}_zb")
                    k.act(zb.ap(), dummy.ap()[:, 0:512], AF.Silu,
                          rd=(dummy,), wr=(zb,), tag=f"{tag}_zsl")
                    k.dma("gpsimd",
                          zdram.ap()[fc, :, ts4 * 512:(ts4 + 1) * 512],
                          zb.ap(), rd=(zb,),
                          wr=(zdram.slot((fc, ts4)),), tag=f"{tag}_zd")
            if oc < 16:
                rdx = (xst.slot("z"), xst.slot(0), xst.slot(1),
                       xst.slot(2), xst.slot(3), cwf)
                for hh in range(2):
                    o0 = hh * 1024
                    k.ts(dummy.ap()[:, 0:1024], xst.ap()[:, o0:o0 + 1024],
                         cwf[:, fc * 4:fc * 4 + 1], None, ALU.mult,
                         rd=rdx, wr=(dummy,), tag=f"{tag}_c0")
                    for j in range(1, 4):
                        k.stt(dummy.ap()[:, 0:1024],
                              xst.ap()[:, o0 + j:o0 + j + 1024],
                              cwf[:, fc * 4 + j:fc * 4 + j + 1],
                              dummy.ap()[:, 0:1024], ALU.mult, ALU.add,
                              rd=rdx + (dummy,), wr=(dummy,),
                              tag=f"{tag}_c{j}")
                    k.act(u.ap()[:, fc * 2048 + o0:fc * 2048 + o0 + 1024],
                          dummy.ap()[:, 0:1024], AF.Silu,
                          bias=cbc[:, fc:fc + 1],
                          rd=(dummy, cbc), wr=(u.slot(fc),),
                          tag=f"{tag}_us")
        for fc in range(16):
            k.dma("gpsimd", udram.ap()[fc], u.ap()[:, fc * 2048:
                                                   (fc + 1) * 2048],
                  rd=(u.slot(fc),), wr=(udram.slot(fc),),
                  tag=f"{tag}_ud{fc}")
        k.release(mp1)
        xpT = k.buf(tag + "_xpT", [128, 16 * 160], BF16)
        self.load_weight_T(tag + "_xp", self.input("L1_attn_W_xproj"), xpT,
                           0, 160, 2048, queue="sync")
        for sl in range(4):
            pbA = k.P[0]
            pbB = k.P[1]
            for dic in range(16):
                k.mm(pbA.ap()[:, 0:512],
                     xpT.ap()[:, dic * 160:dic * 160 + 128],
                     u.ap()[:, dic * 2048 + sl * 512:
                            dic * 2048 + (sl + 1) * 512],
                     start=(dic == 0), stop=(dic == 15),
                     rd=(xpT, u.slot(dic)), wr=(pbA.slot(0),),
                     tag=f"{tag}_pm")
                k.mm(pbB.ap()[0:32, 0:512],
                     xpT.ap()[:, dic * 160 + 128:dic * 160 + 160],
                     u.ap()[:, dic * 2048 + sl * 512:
                            dic * 2048 + (sl + 1) * 512],
                     start=(dic == 0), stop=(dic == 15),
                     rd=(xpT, u.slot(dic)), wr=(pbB.slot(0),),
                     tag=f"{tag}_pbm")
            k.copy(dtl.ap()[:, sl * 512:(sl + 1) * 512], pbA.ap()[:, 0:512],
                   rd=(pbA.slot(0),), wr=(dtl.slot(sl),),
                   tag=f"{tag}_pme", engine="scalar")
            k.copy(bcb.ap()[:, sl * 512:(sl + 1) * 512],
                   pbB.ap()[0:32, 0:512],
                   rd=(pbB.slot(0),), wr=(bcb.slot(sl),),
                   tag=f"{tag}_pbe", engine="vector")
        k.release(mp)
        if getattr(self, "l1_stop", None) == 1:
            k.release(m)
            return
        mp2 = k.mark()
        wdtT = k.buf(tag + "_wdt", [128, 2048], BF16)
        self.load_weight_T(tag + "_wdtl", self.input("L1_attn_W_dt"), wdtT,
                           0, 2048, 128, queue="sync")
        bdtc, _ = self.load_bias(tag + "_bdt", self.input("L1_attn_b_dt"),
                                 2048, row=False, queue="scalar")
        dpc, _ = self.load_bias(tag + "_dp", self.input("L1_attn_D_param"),
                                2048, row=False, queue="scalar")
        acol = k.buf(tag + "_acol", [128, 256], F32)
        k.dma("scalar", acol.ap().rearrange("p (c n) -> p c n", n=16),
              self.input("L1_attn_A_log").rearrange("(c p) n -> p c n",
                                                    p=128),
              rd=(), wr=(acol,), tag=tag + "_ald")
        k.act(acol.ap(), acol.ap(), AF.Exp, rd=(acol,), wr=(acol,),
              tag=tag + "_ale")
        k.ts(acol.ap(), acol.ap(), -1.0, None, ALU.mult, rd=(acol,),
             wr=(acol,), tag=tag + "_aln")
        carry = k.buf(tag + "_carry", [128, 256], F32)
        k.memset(carry.ap(), 0.0, rd=(), wr=(carry,), tag=tag + "_cz")
        delta = [k.buf(f"{tag}_dl{i}", [128, TC], F32) for i in range(2)]
        wti = k.buf(tag + "_wti", [128, TC], F32)
        bb_all = k.buf(tag + "_bba", [128, DS * TC], BF16)
        cb_all = k.buf(tag + "_cba", [128, DS * TC], BF16)
        rowst = k.buf(tag + "_row", [1, TC], BF16)
        datile = k.buf(tag + "_da", [128, TC], F32)
        xtile = k.buf(tag + "_xt", [128, TC], F32)
        htile = k.buf(tag + "_ht", [128, TC], F32)
        hctile = [k.buf(f"{tag}_hc{i}", [128, TC], BF16) for i in range(2)]
        y2t = k.buf(tag + "_y2", [128, 16 * TC], BF16)
        sfix = k.buf(tag + "_sfx", [128, TC], F32)
        ztc = k.buf(tag + "_ztc", [128, 16 * TC], BF16)
        utc = k.buf(tag + "_utc", [128, 16 * TC], BF16)
        woh = k.buf(tag + "_woh", [128, 16 * 512], BF16)
        W_out = self.input("L1_attn_W_out")
        for tc in range(NTC):
            t0 = tc * TC
            slq = t0 // 512
            k.dma("gpsimd", ztc.ap().rearrange("p (c t) -> p c t", t=TC),
                  zdram.ap()[:, :, t0:t0 + TC]
                  .rearrange("c p t -> p c t"),
                  rd=tuple(zdram.slot((c_, s_)) for c_ in range(16)
                           for s_ in range(4)),
                  wr=(ztc,), tag=f"{tag}_zl")
            k.dma("sync", utc.ap().rearrange("p (c t) -> p c t", t=TC),
                  udram.ap()[:, :, t0:t0 + TC]
                  .rearrange("c p t -> p c t"),
                  rd=tuple(udram.slot(c_) for c_ in range(16)),
                  wr=(utc,), tag=f"{tag}_ul")
            for n in range(DS):
                for (roff, dst) in ((n, bb_all), (DS + n, cb_all)):
                    pb = k.P[3]
                    k.mm(pb.ap()[0:1, 0:TC],
                         ident.ap()[0:32, roff:roff + 1],
                         bcb.ap()[0:32, t0:t0 + TC], start=True, stop=True,
                         rd=(bcb.slot(slq), ident), wr=(pb.slot(0),),
                         tag=f"{tag}_rx")
                    k.copy(rowst.ap(), pb.ap()[0:1, 0:TC],
                           rd=(pb.slot(0),), wr=(rowst,), tag=f"{tag}_re")
                    k.mm(pb.ap()[:, 512:512 + TC], ones1.ap(), rowst.ap(),
                         start=True, stop=True, rd=(ones1, rowst),
                         wr=(pb.slot(1),), tag=f"{tag}_rb")
                    k.copy(dst.ap()[:, n * TC:(n + 1) * TC],
                           pb.ap()[:, 512:512 + TC], rd=(pb.slot(1),),
                           wr=(dst.slot(n),), tag=f"{tag}_rbe")
            if getattr(self, "l1_stop", None) == 2:
                continue
            for dic in range(16):
                if getattr(self, "l1_stop", None) == 3 and dic > 0:
                    continue
                dl = delta[dic % 2]
                pb = k.P[2]
                sl2 = dic % 2
                pv = pb.ap()[:, sl2 * 512:sl2 * 512 + TC]
                k.mm(pv, wdtT.ap()[:, dic * 128:(dic + 1) * 128],
                     dtl.ap()[:, t0:t0 + TC], start=True, stop=True,
                     rd=(wdtT, dtl.slot(slq)), wr=(pb.slot(sl2),),
                     tag=f"{tag}_dm")
                if getattr(self, "l1_nosp", False):
                    k.copy(datile.ap(), pv, rd=(pb.slot(sl2),),
                           wr=(datile,), tag=f"{tag}_sa")
                else:
                    k.act(datile.ap(), pv, AF.Abs, bias=bdtc[:, dic:dic + 1],
                          rd=(pb.slot(sl2), bdtc), wr=(datile,),
                          tag=f"{tag}_sa")
                    k.act(datile.ap(), datile.ap(), AF.Exp, scale=-1.0,
                          rd=(datile,), wr=(datile,), tag=f"{tag}_se")
                    k.act(datile.ap(), datile.ap(), AF.Ln, bias=one1c.ap(),
                          rd=(datile, one1c), wr=(datile,), tag=f"{tag}_sl")
                if getattr(self, "l1_nosp", False):
                    k.copy(dl.ap(), datile.ap(), rd=(datile,), wr=(dl,),
                           tag=f"{tag}_sp")
                else:
                    k.ts(xtile.ap(), pv, bdtc[:, dic:dic + 1], 0.0, ALU.add,
                         ALU.max, rd=(pb.slot(sl2), bdtc), wr=(xtile,),
                         tag=f"{tag}_sr")
                    k.tt(dl.ap(), datile.ap(), xtile.ap(), ALU.add,
                         rd=(datile, xtile), wr=(dl,), tag=f"{tag}_sp")
                if getattr(self, "l1_stop", None) == 5:
                    continue
                k.tt(wti.ap(), dl.ap(),
                     utc.ap()[:, dic * TC:(dic + 1) * TC],
                     ALU.mult, rd=(dl, utc), wr=(wti,),
                     tag=f"{tag}_w")
                ypb = k.P[0 if dic % 2 == 0 else 1]
                ysl = (dic // 2) % 2
                ypv = ypb.ap()[:, ysl * 512:ysl * 512 + TC]
                for n in range(DS):
                    if getattr(self, "l1_stop", None) == 6 and n > 0:
                        continue
                    cc = dic * 16 + n
                    k.act(datile.ap(), dl.ap(), AF.Exp,
                          scale=acol[:, cc:cc + 1],
                          rd=(dl, acol), wr=(datile,), tag=f"{tag}_da")
                    k.tt(xtile.ap(), wti.ap(),
                         bb_all.ap()[:, n * TC:(n + 1) * TC], ALU.mult,
                         rd=(wti, bb_all.slot(n)), wr=(xtile,),
                         tag=f"{tag}_x", engine="gpsimd")
                    k.scan(htile.ap(), datile.ap(), xtile.ap(),
                           carry[:, cc:cc + 1], ALU.mult, ALU.add,
                           rd=(datile, xtile, carry.slot(cc)),
                           wr=(htile,), tag=f"{tag}_scan")
                    k.copy(carry.ap()[:, cc:cc + 1],
                           htile.ap()[:, TC - 1:TC],
                           rd=(htile,), wr=(carry.slot(cc),),
                           tag=f"{tag}_cu", engine="scalar")
                    hc_ = hctile[n % 2]
                    k.tt(hc_.ap(), htile.ap(),
                         cb_all.ap()[:, n * TC:(n + 1) * TC], ALU.mult,
                         rd=(htile, cb_all.slot(n)), wr=(hc_,),
                         tag=f"{tag}_hcm", engine="gpsimd")
                    k.mm(ypv, ident.ap(), hc_.ap(),
                         start=(n == 0), stop=(n == DS - 1),
                         rd=(ident, hc_), wr=(ypb.slot(ysl),),
                         tag=f"{tag}_yac")
                if getattr(self, "l1_stop", None) in (6, 7):
                    continue
                k.stt(sfix.ap(),
                      utc.ap()[:, dic * TC:(dic + 1) * TC],
                      dpc[:, dic:dic + 1], ypv, ALU.mult, ALU.add,
                      rd=(utc, dpc, ypb.slot(ysl)),
                      wr=(sfix,), tag=f"{tag}_ud")
                k.tt(y2t.ap()[:, dic * TC:(dic + 1) * TC], sfix.ap(),
                     ztc.ap()[:, dic * TC:(dic + 1) * TC], ALU.mult,
                     rd=(sfix, ztc), wr=(y2t.slot(dic),),
                     tag=f"{tag}_y2")
            if getattr(self, "l1_stop", None) in (2, 3, 4):
                continue
            for half in range(2):
                self.load_weight_T(f"{tag}_wo{tc}_{half}", W_out, woh,
                                   half * 512, 512, 2048, queue="sync")
                for tt2 in range(2):
                    ti = tc * 2 + tt2
                    pb3 = k.P[2 + tt2]
                    pvo = pb3.ap()[:, half * 512:(half + 1) * 512]
                    for dic in range(16):
                        k.mm(pvo,
                             y2t.ap()[:, dic * TC + tt2 * 128:
                                      dic * TC + (tt2 + 1) * 128],
                             woh.ap()[:, dic * 512:(dic + 1) * 512],
                             start=(dic == 0), stop=(dic == 15),
                             rd=(y2t.slot(dic), woh),
                             wr=(pb3.slot(half),), tag=f"{tag}_om")
                    k.tt(self.x[:, ti * 1024 + half * 512:
                                ti * 1024 + (half + 1) * 512],
                         pvo,
                         self.x[:, ti * 1024 + half * 512:
                                ti * 1024 + (half + 1) * 512],
                         ALU.add, rd=(pb3.slot(half), self.x.slot(ti)),
                         wr=(self.x.slot(ti),), tag=f"{tag}_acc")
        k.release(mp2)
        k.release(m)

    def emit_l2_ffn(self, tag, stages='all'):
        k = self.k
        W1 = self.input("L2_ffn_W1")
        W2 = self.input("L2_ffn_W2")
        m = k.mark()
        b1c, _ = self.load_bias(tag + "_b1", self.input("L2_ffn_b1"), 4096, row=False)
        _, b2r = self.load_bias(tag + "_b2", self.input("L2_ffn_b2"), 1024,
                                scale=0.5, col=False)
        b2rb = self.bf_row(tag + "_b2", b2r, 1024)
        ones1 = self.ones1()
        SH = 512
        HS = 4096 // SH
        NHC = SH // 128
        w1s = [k.buf(f"{tag}_w1s{i}", [128, ND * SH], BF16) for i in range(2)]
        w2s = [k.buf(f"{tag}_w2s{i}", [128, NHC * 1024], BF16) for i in range(2)]
        hsb = k.buf(f"{tag}_hsb", [128, NHC * 2048], BF16)
        for hs in range(HS):
            w1b, w2b = w1s[hs % 2], w2s[hs % 2]
            self.load_weight_cols(f"{tag}_w1_{hs}", W1, w1b, 1024,
                                  hs * SH, SH, queue="sync")
            self.load_weight_cols(f"{tag}_w2_{hs}",
                                  W2[hs * SH:(hs + 1) * SH, :],
                                  w2b, SH, 0, 1024, scale=0.5,
                                  queue="gpsimd")
            if stages == 'loads':
                continue
            for hc in range(NHC):
                for tsb in range(4):
                    g = hc * 4 + tsb
                    pb = k.P[g % 2]
                    sl = (g // 2) % 2
                    pv = pb.ap()[:, sl * 512:(sl + 1) * 512]
                    for dc in range(ND):
                        k.mm(pv,
                             w1b.ap()[:, dc * SH + hc * 128:
                                      dc * SH + (hc + 1) * 128],
                             self.xnf(dc)[:, tsb * 512:(tsb + 1) * 512],
                             start=(dc == 0), stop=(dc == ND - 1),
                             rd=(w1b, self.xn_fm),
                             wr=(pb.slot(sl),), tag=f"{tag}_h")
                    hcol = b1c[:, hs * NHC + hc:hs * NHC + hc + 1]
                    k.act(hsb.ap()[:, hc * 2048 + tsb * 512:
                                   hc * 2048 + (tsb + 1) * 512],
                          pv, AF.Relu, bias=hcol,
                          rd=(pb.slot(sl), b1c),
                          wr=(hsb.slot((hc, tsb)),), tag=f"{tag}_relu")
            if stages == 'h':
                continue
            for ti in range(NT):
                pb = k.P[2 + ti % 2]
                for hc in range(NHC):
                    last = (hc == NHC - 1) and hs > 0
                    for jh in range(2):
                        k.mm(pb.ap()[:, jh * 512:(jh + 1) * 512],
                             hsb.ap()[:, hc * 2048 + ti * 128:
                                      hc * 2048 + (ti + 1) * 128],
                             w2b.ap()[:, hc * 1024 + jh * 512:
                                      hc * 1024 + (jh + 1) * 512],
                             start=(hc == 0), stop=last,
                             rd=(hsb.slot((hc, ti // 4)), w2b), wr=(pb,),
                             tag=f"{tag}_y")
                if hs == 0:
                    for jh in range(2):
                        k.mm(pb.ap()[:, jh * 512:(jh + 1) * 512],
                             ones1.ap(),
                             b2rb.ap()[:, jh * 512:(jh + 1) * 512],
                             start=False, stop=True,
                             rd=(ones1, b2rb), wr=(pb,), tag=f"{tag}_yb")
                k.tt(self.xt(ti), pb.ap(), self.xt(ti), ALU.add,
                     rd=(pb, self.x.slot(ti)), wr=(self.x.slot(ti),),
                     tag=f"{tag}_acc")
        k.release(m)



SUBLAYERS_ALL = [
    "L0_ffn_a", "L0_attn", "L0_ffn_b",
    "L1_ffn_a", "L1_attn", "L1_ffn_b",
    "L2_ffn_a", "L2_attn", "L2_ffn_b",
    "L3_ffn_a", "L3_attn", "L3_ffn_b",
]


def build_nc(sublayers=None):
    sublayers = SUBLAYERS_ALL if sublayers is None else sublayers
    nc = bass.Bass("TRN2", target_bir_lowering=False)
    kb = Kb(nc)
    kb.prewarm_consts()
    kb.emit_load_x()
    for i, sl in enumerate(sublayers):
        if sl == "copyonly":
            continue
        add_back = sl in ("L1_attn", "L3_attn")
        kb.emit_ln(f"ln{i}", add_back=add_back)
        if sl == "lnonly":
            continue
        if sl.startswith("L2_ffn"):
            st_ = "all"
            if sl.endswith("loads"): st_ = "loads"
            if sl.endswith("hstage"): st_ = "h"
            kb.emit_l2_ffn(f"s{i}", stages=st_)
        elif sl.startswith("L0_ffn"):
            kb.emit_l0_ffn(f"s{i}")
        elif sl.startswith("L1_ffn"):
            kb.emit_l1_ffn(f"s{i}")
        elif sl.startswith("L3_ffn"):
            kb.emit_l3_ffn(f"s{i}")
        elif sl == "L0_attn":
            kb.emit_l0_attn(f"s{i}")
        elif sl == "L1_attn":
            kb.emit_l1_attn(f"s{i}")
        elif sl == "L2_attn":
            kb.emit_l2_attn(f"s{i}")
        elif sl == "L3_attn":
            kb.emit_l3_attn(f"s{i}")
        else:
            raise ValueError(sl)
    kb.emit_store_out()
    stack = kb.k.finalize_and_emit()
    return nc, kb, stack


def kernel(**inputs):
    nc, kb, stack = build_nc()
    used = set(kb.inputs.keys())
    x_full = np.ascontiguousarray(inputs["L0_attn_x"], dtype=np.float32)
    in_maps = []
    for c in range(8):
        d = {}
        for name in used:
            if name == "L0_attn_x":
                d[name] = np.ascontiguousarray(x_full[c])
            else:
                d[name] = np.ascontiguousarray(inputs[name], dtype=np.float32)
        in_maps.append(d)
    res = run_bass_kernel_spmd(nc, in_maps, core_ids=list(range(8)))
    stack.close()
    out = np.stack([res.results[c]["out"] for c in range(8)], axis=0)
    return out.astype(np.float32)


# version 03
# speedup vs baseline: 1.1198x; 1.1198x over previous
import numpy as np
import ml_dtypes
import concourse.bass as bass
import concourse.mybir as mybir
from concourse.bass_utils import run_bass_kernel_spmd

F32 = mybir.dt.float32
BF16 = mybir.dt.bfloat16
AF = mybir.ActivationFunctionType
ALU = mybir.AluOpType
AX = mybir.AxisListType
BFNP = ml_dtypes.bfloat16

ENGINES = ("sync", "vector", "scalar", "tensor", "gpsimd")
DMA_QUEUES = ("sync", "gpsimd", "scalar", "vector")

T = 2048
D = 1024
NT = T // 128
ND = D // 128
EPS_LN = 1e-5


class Op:
    __slots__ = ("engine", "fn", "dma", "needs_inc", "inc_val", "waits",
                 "tag", "dma_sem")

    def __init__(self, engine, fn, dma, tag):
        self.engine = engine
        self.fn = fn
        self.dma = dma
        self.needs_inc = dma
        self.inc_val = None
        self.waits = set()
        self.tag = tag
        self.dma_sem = None


class Buf:
    __slots__ = ("name", "win", "mk")

    def __init__(self, name, win, mk):
        self.name = name
        self.win = win
        self.mk = mk

    def ap(self):
        return self.win

    def __getitem__(self, key):
        return self.win[key]

    def slot(self, i):
        return (self, i)

    @property
    def key(self):
        return self.name


class MK:

    def __init__(self, nc, arena_words=53200):
        self.nc = nc
        self.ops = {e: [] for e in ENGINES}
        self.n_ops = 0
        self._track = {}
        self._sems = {}
        self.NDMASEM = 48
        self._dma_rr = 0
        self._dma_sem_uses = [0] * self.NDMASEM
        self._consts = {}
        self._uid = 0
        self.ar = nc.alloc_sbuf_tensor("arena", [128, arena_words], F32).ap()
        self.ar_bf = self.ar.bitcast(BF16)
        self._off = 0
        self._cap = arena_words
        self._marks = []
        self.peak = 0
        self._barrier_last = None
        self._barrier_pending = set()

    def buf(self, name, shape, dtype):
        assert len(shape) == 2 and shape[0] <= 128
        n = int(shape[1])
        w = n if dtype == F32 else (n + 1) // 2
        off = (self._off + 7) & ~7
        self._off = off + w
        self.peak = max(self.peak, self._off)
        assert self._off <= self._cap, \
            f"arena overflow at {name}: {self._off}"
        if dtype == F32:
            win = self.ar[0:shape[0], off:off + n]
        else:
            win = self.ar_bf[0:shape[0], 2 * off:2 * off + n]
        self._uid += 1
        return Buf(f"{name}#{self._uid}", win, self)

    def mark(self):
        self._marks.append(self._off)
        return len(self._marks) - 1

    def release(self, m, barrier=True):
        self._off = self._marks[m]
        del self._marks[m:]
        if barrier:
            self.barrier()

    def barrier(self):
        last = {}
        for e in ENGINES:
            if self.ops[e]:
                last[e] = self.ops[e][-1]
        self._barrier_last = last
        self._barrier_pending = set(ENGINES)

    def psum_init(self):
        self.P = []
        for i in range(4):
            t = self.nc.alloc_psum_tensor(f"psum{i}", [128, 1024], F32)
            self.P.append(Buf(f"psum{i}", t.ap(), self))

    def const(self, name, arr):
        if name in self._consts:
            return self._consts[name]
        assert not self._marks, f"const {name} allocated inside a stack frame"
        assert arr.ndim <= 2
        dram = self.nc.inline_tensor(np.ascontiguousarray(arr), name="c_" + name)
        if arr.ndim == 1:
            shape = [1, arr.shape[0]]
        else:
            shape = [arr.shape[0], arr.shape[1]]
        dtype = BF16 if arr.dtype == BFNP else (F32 if arr.dtype == np.float32 else None)
        assert dtype is not None, arr.dtype
        b = self.buf("c_" + name, shape, dtype)
        src = dram.ap()
        if arr.ndim == 1:
            src = src.unsqueeze(0)
        self.dma("sync", b.ap(), src, rd=(), wr=(b,), tag="const_" + name)
        self._consts[name] = b
        return b

    def _entry(self, buf):
        ent = self._track.get(buf.name)
        if ent is None:
            ent = {"base": [None, []], "slots": {}}
            self._track[buf.name] = ent
        return ent

    def _dep(self, op, producer):
        if producer is None or producer is op:
            return
        if (producer.engine == op.engine and not producer.dma
                and producer.engine == "tensor"):
            return
        if not producer.dma:
            producer.needs_inc = True
        op.waits.add(producer)

    def _read(self, op, item):
        buf, slot = (item, None) if isinstance(item, Buf) else item
        ent = self._entry(buf)
        base = ent["base"]
        self._dep(op, base[0])
        psum = buf.name.startswith("psum")
        if slot is None:
            for st in ent["slots"].values():
                self._dep(op, st[0])
                if psum:
                    for r in st[1]:
                        if r.engine != op.engine:
                            self._dep(op, r)
            if psum:
                for r in base[1]:
                    if r.engine != op.engine:
                        self._dep(op, r)
            base[1].append(op)
        else:
            st = ent["slots"].setdefault(slot, [None, []])
            self._dep(op, st[0])
            if psum:
                for r in st[1] + base[1]:
                    if r.engine != op.engine:
                        self._dep(op, r)
            st[1].append(op)

    def _write(self, op, item):
        buf, slot = (item, None) if isinstance(item, Buf) else item
        ent = self._entry(buf)
        base = ent["base"]
        if slot is None:
            self._dep(op, base[0])
            for r in base[1]:
                self._dep(op, r)
            for st in ent["slots"].values():
                self._dep(op, st[0])
                for r in st[1]:
                    self._dep(op, r)
            self._track[buf.name] = {"base": [op, []], "slots": {}}
        else:
            self._dep(op, base[0])
            for r in base[1]:
                self._dep(op, r)
            st = ent["slots"].setdefault(slot, [None, []])
            self._dep(op, st[0])
            for r in st[1]:
                self._dep(op, r)
            ent["slots"][slot] = [op, []]

    def add(self, engine, fn, rd=(), wr=(), dma=False, tag=""):
        op = Op(engine, fn, dma, tag)
        if engine in self._barrier_pending:
            self._barrier_pending.discard(engine)
            for pe, pop in (self._barrier_last or {}).items():
                self._dep(op, pop)
        for item in rd:
            self._read(op, item)
        for item in wr:
            self._write(op, item)
        if dma:
            sid = self._dma_rr % self.NDMASEM
            self._dma_rr += 1
            self._dma_sem_uses[sid] += 1
            op.dma_sem = sid
            op.inc_val = 16 * self._dma_sem_uses[sid]
        self.ops[engine].append(op)
        self.n_ops += 1
        return op

    def dma(self, queue, out_ap, in_ap, rd=(), wr=(), tag=""):
        def fn(eng, out_ap=out_ap, in_ap=in_ap):
            return eng.dma_start(out=out_ap, in_=in_ap)
        return self.add(queue, fn, rd=rd, wr=wr, dma=True, tag=tag)

    def finalize_and_emit(self):
        nc = self.nc
        counts = {e: 0 for e in ENGINES}
        for e in ENGINES:
            for op in self.ops[e]:
                if not op.dma and op.needs_inc:
                    counts[e] += 1
                    op.inc_val = counts[e]
        from contextlib import ExitStack
        stack = ExitStack()
        for e in ENGINES:
            self._sems[e] = stack.enter_context(nc.semaphore("s_" + e))
        for i in range(self.NDMASEM):
            self._sems[("dma", i)] = stack.enter_context(
                nc.semaphore(f"sdma{i}"))

        def emit_engine(eng, ename):
            watermark = {}
            for op in self.ops[ename]:
                needed = {}
                for p in op.waits:
                    sem_name = ("dma", p.dma_sem) if p.dma else p.engine
                    v = p.inc_val
                    assert v is not None, (p.tag, p.engine, "missing inc")
                    if v > needed.get(sem_name, -1):
                        needed[sem_name] = v
                for sem_name, v in sorted(needed.items(), key=str):
                    if watermark.get(sem_name, 0) >= v:
                        continue
                    eng.wait_ge(self._sems[sem_name], v)
                    watermark[sem_name] = v
                inst = op.fn(eng)
                if isinstance(inst, (list, tuple)):
                    inst = inst[-1]
                if op.dma:
                    inst.then_inc(self._sems[("dma", op.dma_sem)], 16)
                elif op.needs_inc:
                    inst.then_inc(self._sems[op.engine], 1)

        with nc.Block() as block:
            @block.sync
            def _(eng):
                emit_engine(eng, "sync")

            @block.vector
            def _(eng):
                emit_engine(eng, "vector")

            @block.scalar
            def _(eng):
                emit_engine(eng, "scalar")

            @block.tensor
            def _(eng):
                emit_engine(eng, "tensor")

            @block.gpsimd
            def _(eng):
                emit_engine(eng, "gpsimd")
        return stack

    def mm(self, out_ap, lhsT_ap, rhs_ap, start, stop, rd=(), wr=(), tag="mm"):
        def fn(eng, o=out_ap, l=lhsT_ap, r=rhs_ap, st=start, sp=stop):
            return eng.matmul(o, l, r, start=st, stop=sp)
        return self.add("tensor", fn, rd=rd, wr=wr, tag=tag)

    def transpose(self, out_ap, in_ap, rd=(), wr=(), tag="tr"):
        ident = self.const("ident_bf16", np.eye(128, dtype=BFNP))
        p = in_ap.shape[0]
        def fn(eng, o=out_ap, i=in_ap, idb=ident, p=p):
            return eng.transpose(o, i, idb.ap()[0:p, 0:p])
        return self.add("tensor", fn, rd=tuple(rd) + (ident,), wr=wr, tag=tag)

    def transpose_f32(self, out_ap, in_ap, rd=(), wr=(), tag="trf"):
        ident = self.const("ident_f32", np.eye(128, dtype=np.float32))
        p = in_ap.shape[0]
        def fn(eng, o=out_ap, i=in_ap, idb=ident, p=p):
            return eng.transpose(o, i, idb.ap()[0:p, 0:p])
        return self.add("tensor", fn, rd=tuple(rd) + (ident,), wr=wr, tag=tag)

    def act(self, out_ap, in_ap, func, bias=0.0, scale=1.0, accum_out=None,
            rd=(), wr=(), tag="act", engine="scalar"):
        def fn(eng, o=out_ap, i=in_ap, f=func, b=bias, s=scale, a=accum_out):
            return eng.activation(o, i, f, bias=b, scale=s, accum_out=a)
        return self.add(engine, fn, rd=rd, wr=wr, tag=tag)

    def tt(self, out_ap, in0_ap, in1_ap, alu_op, rd=(), wr=(), tag="tt",
           engine="vector"):
        def fn(eng, o=out_ap, a=in0_ap, b=in1_ap, op=alu_op):
            return eng.tensor_tensor(o, a, b, op)
        return self.add(engine, fn, rd=rd, wr=wr, tag=tag)

    def ts(self, out_ap, in_ap, s1, s2, op0, op1=None, rd=(), wr=(),
           accum_out=None, tag="ts", engine="vector"):
        def fn(eng, o=out_ap, i=in_ap, a=s1, b=s2, p0=op0, p1=op1, ac=accum_out):
            if p1 is None and ac is None:
                return eng.tensor_single_scalar(o, i, a, p0)
            return eng.tensor_scalar(o, i, a, b, p0,
                                     p1 if p1 is not None else ALU.bypass,
                                     accum_out=ac)
        return self.add(engine, fn, rd=rd, wr=wr, tag=tag)

    def stt(self, out_ap, in0_ap, scalar, in1_ap, op0, op1, rd=(), wr=(),
            tag="stt", engine="vector", accum_out=None):
        def fn(eng, o=out_ap, a=in0_ap, s=scalar, b=in1_ap, p0=op0, p1=op1,
               ac=accum_out):
            return eng.scalar_tensor_tensor(o, a, s, b, p0, p1,
                                            accum_out=ac)
        return self.add(engine, fn, rd=rd, wr=wr, tag=tag)

    def copy(self, out_ap, in_ap, rd=(), wr=(), tag="copy", engine="vector"):
        if engine == "scalar":
            def fn(eng, o=out_ap, i=in_ap):
                return eng.copy(o, i)
        else:
            def fn(eng, o=out_ap, i=in_ap):
                return eng.tensor_copy(o, i)
        return self.add(engine, fn, rd=rd, wr=wr, tag=tag)

    def memset(self, ap, val, rd=(), wr=(), tag="memset", engine="vector"):
        def fn(eng, a=ap, v=val):
            return eng.memset(a, v)
        return self.add(engine, fn, rd=rd, wr=wr, tag=tag)

    def reduce(self, out_ap, in_ap, alu_op, rd=(), wr=(), negate=None,
               tag="red", engine="vector"):
        def fn(eng, o=out_ap, i=in_ap, op=alu_op, n=negate):
            return eng.tensor_reduce(o, i, AX.X, op, negate=n)
        return self.add(engine, fn, rd=rd, wr=wr, tag=tag)

    def scan(self, out_ap, d0_ap, d1_ap, initial, op0, op1, rd=(), wr=(),
             tag="scan", engine="vector"):
        def fn(eng, o=out_ap, a=d0_ap, b=d1_ap, i=initial, p0=op0, p1=op1):
            return eng.tensor_tensor_scan(o, a, b, i, p0, p1)
        return self.add(engine, fn, rd=rd, wr=wr, tag=tag)

    def recip(self, out_ap, in_ap, rd=(), wr=(), tag="recip"):
        def fn(eng, o=out_ap, i=in_ap):
            return eng.reciprocal(o, i)
        return self.add("vector", fn, rd=rd, wr=wr, tag=tag)



def pbf(pbuf):
    return pbuf.ap().bitcast(BF16)


INPUT_SHAPES = {
    "L0_attn_x": (8, 2048, 1024), "L0_attn_gamma": (1024,),
    "L0_attn_w_q": (1024, 128), "L0_attn_w_kv": (1024, 256),
    "L0_attn_w_out": (128, 1024),
    "L0_ffn_Wr": (1024, 8), "L0_ffn_br": (8,),
    "L0_ffn_W1": (8, 1024, 4096), "L0_ffn_b1": (8, 4096),
    "L0_ffn_W2": (8, 4096, 1024), "L0_ffn_b2": (8, 1024),
    "L1_attn_ln_gamma": (1024,), "L1_attn_ln_beta": (1024,),
    "L1_attn_W_in": (4096, 1024), "L1_attn_conv_w": (2048, 1, 4),
    "L1_attn_conv_b": (2048,), "L1_attn_W_xproj": (160, 2048),
    "L1_attn_W_dt": (2048, 128), "L1_attn_b_dt": (2048,),
    "L1_attn_A_log": (2048, 16), "L1_attn_D_param": (2048,),
    "L1_attn_W_out": (1024, 2048),
    "L1_ffn_Wg": (1024, 16),
    "L1_ffn_W1": (16, 1024, 2048), "L1_ffn_b1": (16, 2048),
    "L1_ffn_W11": (16, 1024, 2048), "L1_ffn_b11": (16, 2048),
    "L1_ffn_W2": (16, 2048, 1024), "L1_ffn_b2": (16, 1024),
    "L1_ffn_Ws1": (1024, 4096), "L1_ffn_bs1": (4096,),
    "L1_ffn_Ws11": (1024, 4096), "L1_ffn_bs11": (4096,),
    "L1_ffn_Ws2": (4096, 1024), "L1_ffn_bs2": (1024,),
    "L2_attn_Wq": (1024, 1024), "L2_attn_Wk": (1024, 256),
    "L2_attn_Wv": (1024, 256), "L2_attn_Wo": (1024, 1024),
    "L2_attn_norm_w": (1024,),
    "L2_ffn_W1": (1024, 4096), "L2_ffn_b1": (4096,),
    "L2_ffn_W2": (4096, 1024), "L2_ffn_b2": (1024,),
    "L3_attn_w_qkv": (3072, 1024), "L3_attn_b_qkv": (3072,),
    "L3_attn_w_o": (1024, 1024), "L3_attn_b_o": (1024,),
    "L3_ffn_gate_weight": (8, 1024), "L3_ffn_gate_up_proj": (8, 2048, 1024),
    "L3_ffn_down_proj": (8, 1024, 1024), "L3_ffn_similarity_matrix": (8, 8),
}


class Kb:

    def __init__(self, nc):
        self.nc = nc
        self.k = MK(nc)
        self.k.psum_init()
        self.inputs = {}
        self.dram_out = nc.dram_tensor("out", [T, D], F32,
                                       kind="ExternalOutput").ap()
        self.x = self.k.buf("x_res", [128, NT * 1024], F32)
        self.xn_fm = self.k.buf("xn_fm", [128, ND * 2048], BF16)
        self._cast_rr = 0
        self.debug = False
        self._dbg = {}
        _sq = self.k.buf("ln_sq", [128, 1024], F32)
        self.ln_sq = [_sq, _sq]
        self.bstg = self.k.buf("bias_stage", [128, 1024], F32)
        self.ln_st = self.k.buf("ln_stats", [128, 8 * NT], F32)
        self.ln_stage = [self.k.buf(f"ln_xnst{i}", [128, 1024], BF16)
                         for i in range(2)]
        self.ln_xn32 = _sq
        self.wstage = [self.k.buf(f"wstage{i}", [128, 1024], F32)
                       for i in range(2)]
        self._wtT = [self.k.buf(f"wtT{i}", [128, 1024], BF16)
                     for i in range(2)]
        self._wtT_rr = 0
        self._stage_rr = 0
        self._dram_scratch = {}

    def dram_scratch(self, name, shape, dtype):
        if name not in self._dram_scratch:
            t = self.nc.dram_tensor("scr_" + name, list(shape), dtype,
                                    kind="Internal")
            self._dram_scratch[name] = Buf("scr_" + name, t.ap(), self.k)
        return self._dram_scratch[name]

    def dbg_dump(self, name, buf, ap=None):
        if not self.debug:
            return
        ap = buf.ap() if ap is None else ap
        shp = list(ap.shape)
        dt_ = ap.dtype
        d = self.nc.dram_tensor("dbg_" + name, shp, dt_, kind="ExternalOutput").ap()
        self.k.dma("gpsimd", d, ap, rd=(buf,), wr=(), tag="dbg_" + name)
        self._dbg[name] = shp

    def input(self, name):
        if name not in self.inputs:
            shp = INPUT_SHAPES[name]
            if name == "L0_attn_x":
                shp = (T, D)
            self.inputs[name] = self.nc.dram_tensor(
                name, list(shp), F32, kind="ExternalInput").ap()
        return self.inputs[name]

    def xt(self, ti):
        return self.x[:, ti * 1024:(ti + 1) * 1024]

    def xnf(self, dc):
        return self.xn_fm[:, dc * 2048:(dc + 1) * 2048]

    def cast_engine(self):
        self._cast_rr += 1
        return ("vector", "scalar")[self._cast_rr % 2]

    def prewarm_consts(self):
        k = self.k
        k.const("ident_bf16", np.eye(128, dtype=BFNP))
        k.const("ident_f32", np.eye(128, dtype=np.float32))
        self.ones1()
        self.ccol(EPS_LN)
        self.ccol(1e-6)
        self.ccol(1.0)
        k.const("ones128_bf16", np.ones((128, 1), dtype=BFNP))
        selh = np.zeros((16, 8 * 128), np.float32)
        for h in range(8):
            for e in range(16):
                selh[e, h * 128 + h * 16 + e] = 1.0
        k.const("selH16", selh.astype(BFNP))
        k.const("sel_lo", np.concatenate(
            [np.ones((64, 1), np.float32),
             np.zeros((64, 1), np.float32)]).astype(BFNP))
        k.const("sel_hi", np.concatenate(
            [np.zeros((64, 1), np.float32),
             np.ones((64, 1), np.float32)]).astype(BFNP))
        hd = 64
        inv = (1.0 / np.power(np.float32(10000.0),
                              np.arange(0, hd, 2, dtype=np.float32) / np.float32(hd)))
        tpos = np.arange(T, dtype=np.float32)
        fr = np.outer(tpos, inv).astype(np.float32)
        emb = np.concatenate([fr, fr], axis=1)
        cos = np.cos(emb).astype(np.float32).T
        sin = np.sin(emb).astype(np.float32).T
        cos2 = np.concatenate([cos, cos], axis=0)
        sin2 = np.concatenate([sin, sin], axis=0)
        self._rope_dram = (
            self.nc.inline_tensor(np.ascontiguousarray((cos2 * 0.125).astype(BFNP)),
                                  name="c_ropecq"),
            self.nc.inline_tensor(np.ascontiguousarray((sin2 * 0.125).astype(BFNP)),
                                  name="c_ropesq"))
        rc = np.zeros((128, 128), np.float32)
        for i in range(128):
            if i % 64 < 32:
                rc[i + 32, i] = -1.0
            else:
                rc[i - 32, i] = 1.0
        k.const("ropeR", rc.astype(BFNP))

    def emit_load_x(self):
        k = self.k
        src = self.input("L0_attn_x")
        dst = self.x.ap().rearrange("p (tt d) -> p tt d", tt=NT)
        k.dma("sync", dst, src.rearrange("(tt p) d -> p tt d", p=128),
              rd=(), wr=(self.x,), tag="load_x")

    def emit_store_out(self):
        k = self.k
        src = self.x.ap().rearrange("p (tt d) -> p tt d", tt=NT)
        k.dma("gpsimd", self.dram_out.rearrange("(tt p) d -> p tt d", p=128),
              src, rd=(self.x,), wr=(), tag="store_out")

    def emit_ln(self, tag, add_back=False, inner_ln=False):
        k = self.k
        sq = self.ln_sq
        st = self.ln_st
        xn_stage = self.ln_stage
        xn32 = self.ln_xn32
        for ti in range(NT):
            xt = self.xt(ti)
            c0 = st[:, 8 * ti + 0:8 * ti + 1]
            c1 = st[:, 8 * ti + 1:8 * ti + 2]
            c2 = st[:, 8 * ti + 2:8 * ti + 3]
            c3 = st[:, 8 * ti + 3:8 * ti + 4]
            c4 = st[:, 8 * ti + 4:8 * ti + 5]
            c5 = st[:, 8 * ti + 5:8 * ti + 6]
            c6 = st[:, 8 * ti + 6:8 * ti + 7]
            c7 = st[:, 8 * ti + 7:8 * ti + 8]
            sqb = sq[ti % 2]
            k.act(sqb.ap(), xt, AF.Square, accum_out=c1,
                  rd=(self.x.slot(ti),), wr=(sqb, st.slot((ti, 1))),
                  tag=f"{tag}_sq{ti}")
            k.reduce(c0, xt, ALU.add, rd=(self.x.slot(ti),),
                     wr=(st.slot((ti, 0)),), tag=f"{tag}_sum{ti}")
            k.ts(c2, c0, 1.0 / D, None, ALU.mult,
                 rd=(st.slot((ti, 0)),), wr=(st.slot((ti, 2)),), tag=f"{tag}_mu")
            k.ts(c3, c1, 1.0 / D, None, ALU.mult,
                 rd=(st.slot((ti, 1)),), wr=(st.slot((ti, 3)),), tag=f"{tag}_ex2")
            k.stt(c4, c2, c2, c3, ALU.mult, ALU.subtract,
                  rd=(st.slot((ti, 2)), st.slot((ti, 3))),
                  wr=(st.slot((ti, 4)),), tag=f"{tag}_nv")
            epsc = self.ccol(EPS_LN)
            k.act(c4, c4, AF.Ln, bias=epsc.ap(), scale=-1.0,
                  rd=(st.slot((ti, 4)), epsc), wr=(st.slot((ti, 4)),),
                  tag=f"{tag}_ln")
            k.act(c5, c4, AF.Exp, scale=-0.5,
                  rd=(st.slot((ti, 4)),), wr=(st.slot((ti, 5)),),
                  tag=f"{tag}_rstd")
            applyc = c5
            applyslot = st.slot((ti, 5))
            if inner_ln:
                k.tt(c6, c5, c5, ALU.mult,
                     rd=(st.slot((ti, 5)),), wr=(st.slot((ti, 6)),),
                     tag=f"{tag}_r2")
                k.ts(c6, c6, -EPS_LN, 1.0 + EPS_LN, ALU.mult, ALU.add,
                     rd=(st.slot((ti, 6)),), wr=(st.slot((ti, 6)),),
                     tag=f"{tag}_u")
                k.act(c6, c6, AF.Ln, rd=(st.slot((ti, 6)),),
                      wr=(st.slot((ti, 6)),), tag=f"{tag}_lu")
                k.act(c6, c6, AF.Exp, scale=-0.5,
                      rd=(st.slot((ti, 6)),), wr=(st.slot((ti, 6)),),
                      tag=f"{tag}_ru")
                k.tt(c7, c5, c6, ALU.mult,
                     rd=(st.slot((ti, 5)), st.slot((ti, 6))),
                     wr=(st.slot((ti, 7)),), tag=f"{tag}_cmb")
                applyc = c7
                applyslot = st.slot((ti, 7))
            xs = xn_stage[ti % 2]
            k.ts(xs.ap(), xt, c2, applyc, ALU.subtract, ALU.mult,
                 rd=(self.x.slot(ti), st.slot((ti, 2)), applyslot),
                 wr=(xs,), tag=f"{tag}_apply")
            if add_back:
                k.ts(xn32.ap(), xt, c2, c5, ALU.subtract, ALU.mult,
                     rd=(self.x.slot(ti), st.slot((ti, 2)), st.slot((ti, 5))),
                     wr=(xn32,), tag=f"{tag}_apply32")
                k.tt(xt, xt, xn32.ap(), ALU.add,
                     rd=(self.x.slot(ti), xn32), wr=(self.x.slot(ti),),
                     tag=f"{tag}_addback")
            pb = k.P[2 + (ti % 2)]
            pv = pbf(pb)
            for dc in range(ND):
                k.transpose(pv[:, dc * 128:(dc + 1) * 128],
                            xs[:, dc * 128:(dc + 1) * 128],
                            rd=(xs,), wr=(pb.slot(0),), tag=f"{tag}_tr")
            xn3 = self.xn_fm.ap().rearrange("p (dc t) -> p dc t", dc=ND)
            k.copy(xn3[:, :, ti * 128:(ti + 1) * 128],
                   pv[:, 0:1024].rearrange("p (dc t) -> p dc t", dc=ND),
                   rd=(pb.slot(0),),
                   wr=(self.xn_fm.slot(ti),),
                   tag=f"{tag}_evac", engine="scalar")

    def load_weight_cols(self, tag, dram_ap, dest, rows, col0, ncols,
                         scale=None, stage_words=1024, queue="sync",
                         scale_col=None):
        k = self.k
        nrc = rows // 128
        assert dest.ap().shape[1] >= nrc * ncols
        cbs = min(ncols, 1024)
        for cb in range(0, ncols, cbs):
            nc_ = min(cbs, ncols - cb)
            rstep = max(1, 1024 // nc_)
            if scale_col is not None:
                rstep = 1
            for rc in range(0, nrc, rstep):
                nr = min(rstep, nrc - rc)
                sg = self.wstage[self._stage_rr % 2]
                self._stage_rr += 1
                src_ap = dram_ap[rc * 128:(rc + nr) * 128,
                                 col0 + cb:col0 + cb + nc_]
                k.dma(queue,
                      sg.ap()[:, 0:nr * nc_].rearrange(
                          "p (r c) -> p r c", r=nr),
                      src_ap.rearrange("(r p) c -> p r c", p=128),
                      rd=(), wr=(sg,), tag=f"{tag}_dma")
                eng = self.cast_engine()
                if scale_col is not None:
                    k.ts(dest.ap()[:, rc * ncols + cb:rc * ncols + cb + nc_],
                         sg.ap()[:, 0:nc_], scale_col[:, rc:rc + 1], None,
                         ALU.mult, rd=(sg, scale_col),
                         wr=(dest.slot((rc, cb)),), tag=f"{tag}_cast",
                         engine="vector")
                    continue
                if nr == 1 or ncols == cbs:
                    dst_ap = dest.ap()[:, rc * ncols + cb:
                                       rc * ncols + cb + nr * nc_] \
                        if ncols == cbs else \
                        dest.ap()[:, rc * ncols + cb:rc * ncols + cb + nc_]
                    if scale is None:
                        k.copy(dst_ap, sg.ap()[:, 0:nr * nc_],
                               rd=(sg,), wr=(dest.slot((rc, cb)),),
                               tag=f"{tag}_cast", engine=eng)
                    elif eng == "scalar":
                        def fn(e, o=dst_ap, i=sg.ap()[:, 0:nr * nc_],
                               s=scale):
                            return e.mul(o, i, s)
                        k.add(eng, fn, rd=(sg,),
                              wr=(dest.slot((rc, cb)),), tag=f"{tag}_cast")
                    else:
                        k.ts(dst_ap, sg.ap()[:, 0:nr * nc_], scale, None,
                             ALU.mult, rd=(sg,),
                             wr=(dest.slot((rc, cb)),), tag=f"{tag}_cast",
                             engine=eng)
                else:
                    d3 = dest.ap()[:, rc * ncols + cb:
                                   (rc + nr - 1) * ncols + cb + nc_]
                    d3 = dest.ap().rearrange(
                        "p (r c) -> p r c", c=ncols)[:, rc:rc + nr,
                                                     cb:cb + nc_]
                    s3 = sg.ap()[:, 0:nr * nc_].rearrange(
                        "p (r c) -> p r c", r=nr)
                    if scale is None:
                        k.copy(d3, s3, rd=(sg,),
                               wr=(dest.slot((rc, cb)),),
                               tag=f"{tag}_cast", engine=eng)
                    else:
                        k.ts(d3, s3, scale, None, ALU.mult, rd=(sg,),
                             wr=(dest.slot((rc, cb)),), tag=f"{tag}_cast",
                             engine=eng)

    def load_bias(self, tag, dram_ap, n, scale=None, queue="sync",
                  col=True, row=True, colb=None, rowb=None, rsb=None):
        k = self.k
        if col and n >= 128:
            nch = n // 128
            assert nch <= 128
            if colb is None:
                colb = k.buf(f"{tag}_col", [128, nch], F32)
            rs = rsb if rsb is not None else \
                Buf(self.bstg.name, self.bstg.ap()[0:min(nch, 128), 0:128],
                    k)
            k.dma(queue, rs.ap(), dram_ap.rearrange("(c p) -> c p", p=128),
                  rd=(), wr=(rs,), tag=f"{tag}_dmac")
            pb = self.k.P[3]
            k.transpose_f32(pb.ap()[:, 0:nch], rs.ap(), rd=(rs,),
                            wr=(pb.slot(0),), tag=f"{tag}_trc")
            if scale is None:
                k.copy(colb.ap(), pb.ap()[:, 0:nch],
                       rd=(pb.slot(0),), wr=(colb,), tag=f"{tag}_evc")
            else:
                k.ts(colb.ap(), pb.ap()[:, 0:nch], scale, None, ALU.mult,
                     rd=(pb.slot(0),), wr=(colb,), tag=f"{tag}_evc")
        if row:
            if rowb is None:
                if n <= 1024:
                    rowb = Buf(self.bstg.name,
                               self.bstg.ap()[0:1, 0:n], k)
                else:
                    rowb = k.buf(f"{tag}_row", [1, n], F32)
            k.dma(queue, rowb.ap(), dram_ap.unsqueeze(0),
                  rd=(), wr=(rowb,), tag=f"{tag}_dmar")
            if scale is not None:
                k.ts(rowb.ap(), rowb.ap(), scale, None, ALU.mult,
                     rd=(rowb,), wr=(rowb,), tag=f"{tag}_scr")
        return colb, rowb

    def bf_row(self, tag, rowb, n, dest=None):
        k = self.k
        b = dest if dest is not None else k.buf(tag + "_bf", [1, n], BF16)
        k.copy(b.ap(), rowb.ap(), rd=(rowb,), wr=(b,), tag=tag + "_cast")
        return b

    def ones1(self):
        return self.k.const("ones1_bf16", np.ones((1, 128), dtype=BFNP))

    def ccol(self, value):
        name = f"col_{value:.9e}".replace("-", "m").replace("+", "p").replace(".", "_")
        return self.k.const(name, np.full((128, 1), value, dtype=np.float32))

    def load_weight_T(self, tag, dram_ap, dest, row0, nrows, ncols,
                      scale=None, queue="sync", scale_col=None):
        k = self.k
        assert ncols % 128 == 0
        ngr = (nrows + 127) // 128
        ncc = ncols // 128
        assert dest.ap().shape[1] >= ncc * nrows
        for g in range(ngr):
            r0 = row0 + g * 128
            nr_p = min(128, row0 + nrows - r0)
            for c0 in range(0, ncols, 1024):
                nc_ = min(1024, ncols - c0)
                sg = self.wstage[self._stage_rr % 2]
                self._stage_rr += 1
                tmp = self._wtT[self._wtT_rr % 2]
                self._wtT_rr += 1
                k.dma(queue, sg.ap()[0:nr_p, 0:nc_],
                      dram_ap[r0:r0 + nr_p, c0:c0 + nc_],
                      rd=(), wr=(sg,), tag=f"{tag}_dma{g}")
                eng = self.cast_engine()
                if eng == "scalar" and scale is not None:
                    def fn(e, o=tmp.ap()[0:nr_p, 0:nc_],
                           i=sg.ap()[0:nr_p, 0:nc_], s=scale):
                        return e.mul(o, i, s)
                    k.add(eng, fn, rd=(sg,), wr=(tmp,), tag=f"{tag}_cast")
                elif scale is not None:
                    k.ts(tmp.ap()[0:nr_p, 0:nc_], sg.ap()[0:nr_p, 0:nc_],
                         scale, None, ALU.mult, rd=(sg,), wr=(tmp,),
                         tag=f"{tag}_cast", engine=eng)
                else:
                    k.copy(tmp.ap()[0:nr_p, 0:nc_], sg.ap()[0:nr_p, 0:nc_],
                           rd=(sg,), wr=(tmp,), tag=f"{tag}_cast", engine=eng)
                ncb = nc_ // 128
                for cb in range(0, ncb, 8):
                    nb = min(8, ncb - cb)
                    pb = self.k.P[3]
                    pv = pbf(pb)
                    for j in range(nb):
                        k.transpose(pv[0:128, j * 128:j * 128 + nr_p],
                                    tmp.ap()[0:nr_p, (cb + j) * 128:
                                             (cb + j + 1) * 128],
                                    rd=(tmp,), wr=(pb.slot(0),),
                                    tag=f"{tag}_tr")
                    d3 = dest.ap().rearrange("p (cc r) -> p cc r", r=nrows)
                    cc0 = c0 // 128 + cb
                    if scale_col is None:
                        k.copy(d3[:, cc0:cc0 + nb, g * 128:g * 128 + nr_p],
                               pv[:, 0:nb * 128].rearrange(
                                   "p (cc r) -> p cc r", r=128)[:, :, 0:nr_p],
                               rd=(pb.slot(0),), wr=(dest.slot((g, cc0)),),
                               tag=f"{tag}_ev", engine="vector")
                    else:
                        for j in range(nb):
                            k.ts(d3[:, cc0 + j, g * 128:g * 128 + nr_p],
                                 pv[:, j * 128:j * 128 + nr_p],
                                 scale_col[:, cc0 + j:cc0 + j + 1], None,
                                 ALU.mult, rd=(pb.slot(0), scale_col),
                                 wr=(dest.slot((g, cc0 + j)),),
                                 tag=f"{tag}_ev", engine="vector")

    def emit_l3_attn(self, tag):
        k = self.k
        m = k.mark()
        wqkv = self.input("L3_attn_w_qkv")
        wo = self.input("L3_attn_w_o")
        SC = float(1.0 / np.sqrt(np.float32(1024.0)))
        bqc, _ = self.load_bias(tag + "_bq", self.input("L3_attn_b_qkv"),
                                3072, row=False)
        k.ts(bqc.ap()[:, 0:8], bqc.ap()[:, 0:8], SC, None, ALU.mult,
             rd=(bqc,), wr=(bqc,), tag=tag + "_bqsc")
        qb = k.buf(tag + "_q", [128, 8 * 2048], BF16)
        kb_ = k.buf(tag + "_k", [128, 8 * 2048], BF16)
        vtm = self.dram_scratch("vtm16", [16, 128, 16, 64], BF16)
        mp = k.mark()
        wT = [k.buf(f"{tag}_wT{i}", [128, 1024], BF16) for i in range(2)]
        vfm = [k.buf(f"{tag}_vf{i}", [128, 512], BF16) for i in range(2)]
        vstg = [k.buf(f"{tag}_vs{i}", [128, 512], BF16) for i in range(2)]
        for oc in range(24):
            wt = wT[oc % 2]
            self.load_weight_T(f"{tag}_w{oc}", wqkv, wt, oc * 128, 128, 1024,
                               queue="sync")
            for ts4 in range(4):
                pb = k.P[ts4 % 2]
                pv = pb.ap()[:, 0:512]
                for dc in range(ND):
                    k.mm(pv, wt.ap()[:, dc * 128:(dc + 1) * 128],
                         self.xnf(dc)[:, ts4 * 512:(ts4 + 1) * 512],
                         start=(dc == 0), stop=(dc == ND - 1),
                         rd=(wt, self.xn_fm), wr=(pb.slot(0),),
                         tag=f"{tag}_qkv")
                if oc < 16:
                    dst = qb if oc < 8 else kb_
                    dc_out = oc % 8
                    scl = SC if oc < 8 else 1.0
                    k.ts(dst.ap()[:, dc_out * 2048 + ts4 * 512:
                                  dc_out * 2048 + (ts4 + 1) * 512],
                         pv, bqc[:, oc:oc + 1], scl, ALU.add, ALU.mult,
                         rd=(pb.slot(0), bqc), wr=(dst.slot((dc_out, ts4)),),
                         tag=f"{tag}_ev")
                else:
                    vc = oc - 16
                    vf = vfm[ts4 % 2]
                    vs = vstg[ts4 % 2]
                    k.ts(vf.ap(), pv, bqc[:, oc:oc + 1], None, ALU.add,
                         rd=(pb.slot(0), bqc), wr=(vf,), tag=f"{tag}_vev")
                    pt = k.P[2]
                    ptv = pbf(pt)
                    for tt in range(4):
                        k.transpose(ptv[:, tt * 128:(tt + 1) * 128],
                                    vf.ap()[:, tt * 128:(tt + 1) * 128],
                                    rd=(vf,), wr=(pt.slot(0),),
                                    tag=f"{tag}_vtr")
                    k.copy(vs.ap(), ptv[:, 0:512], rd=(pt.slot(0),),
                           wr=(vs,), tag=f"{tag}_vsc", engine="vector")
                    s4 = vs.ap().rearrange("p (tt g c) -> p tt g c",
                                           g=2, c=64)
                    for gg in range(2):
                        k.dma("gpsimd",
                              vtm.ap()[vc * 2 + gg, :,
                                       ts4 * 4:(ts4 + 1) * 4, :],
                              s4[:, :, gg, :],
                              rd=(vs,), wr=(vtm.slot((vc, ts4, gg)),),
                              tag=f"{tag}_vdma")
        k.release(mp)
        self._attn_core(tag, qb, kb_, 16, 16, vtm, self.xn_fm)
        k.release(m)
        m2 = k.mark()
        woT = k.buf(tag + "_woT", [128, 8192], BF16)
        self.load_weight_T(tag + "_wo", wo, woT, 0, 1024, 1024, queue="sync")
        _, bor = self.load_bias(tag + "_bo", self.input("L3_attn_b_o"), 1024,
                                col=False)
        borb = self.bf_row(tag + "_bo", bor, 1024)
        self._proj_out_tm(tag, self.xn_fm, woT, borb)
        k.release(m2)

    def _proj_out_tm(self, tag, src_fm, wT, bias_row_bf, scale_col=None):
        k = self.k
        ones1 = self.ones1()
        s3 = src_fm.ap().rearrange("p (cc t) -> p cc t", t=2048)
        for ti in range(NT):
            pb = k.P[2 + ti % 2]
            for cc in range(ND):
                for jh in range(2):
                    k.mm(pb.ap()[:, jh * 512:(jh + 1) * 512],
                         s3[:, cc, ti * 128:(ti + 1) * 128],
                         wT.ap()[:, cc * 1024 + jh * 512:
                                 cc * 1024 + (jh + 1) * 512],
                         start=(cc == 0),
                         stop=(cc == ND - 1 and bias_row_bf is None),
                         rd=(src_fm, wT), wr=(pb,), tag=f"{tag}_o")
            if bias_row_bf is not None:
                for jh in range(2):
                    k.mm(pb.ap()[:, jh * 512:(jh + 1) * 512], ones1.ap(),
                         bias_row_bf.ap()[:, jh * 512:(jh + 1) * 512],
                         start=False, stop=True,
                         rd=(ones1, bias_row_bf), wr=(pb,), tag=f"{tag}_ob")
            if scale_col is None:
                k.tt(self.xt(ti), pb.ap(), self.xt(ti), ALU.add,
                     rd=(pb, self.x.slot(ti)), wr=(self.x.slot(ti),),
                     tag=f"{tag}_acc")
            else:
                k.stt(self.xt(ti), pb.ap(), scale_col(ti), self.xt(ti),
                      ALU.mult, ALU.add,
                      rd=(pb, self.x.slot(ti)), wr=(self.x.slot(ti),),
                      tag=f"{tag}_acc")

    def _attn_core(self, tag, qb, kb_, nheads, kv_heads, vtm, att_dst,
                   exp_scale=1.0):
        k = self.k
        grp = nheads // kv_heads
        q3 = qb.ap().rearrange("p (c t) -> p c t", t=2048)
        k3 = kb_.ap().rearrange("p (c t) -> p c t", t=2048)
        a3 = att_dst.ap().rearrange("p (c t) -> p c t", t=2048)
        sel = [k.const("sel_lo", np.concatenate(
                   [np.ones((64, 1), np.float32),
                    np.zeros((64, 1), np.float32)]).astype(BFNP)),
               k.const("sel_hi", np.concatenate(
                   [np.zeros((64, 1), np.float32),
                    np.ones((64, 1), np.float32)]).astype(BFNP))]
        ones1 = self.ones1()
        sqt = k.buf(tag + "_sqt", [128, 1024], BF16)
        nq = nheads + kv_heads
        nrm = k.buf(tag + "_nrm", [1, nq + nheads], F32)
        bbf = k.buf(tag + "_bbf", [1, nheads], BF16)
        nbs = k.buf(tag + "_nbs", [128, nheads], F32)
        red = k.buf(tag + "_red", [1, 2], F32)
        if grp == 1:
            kspecs = [(kb_, k3, g // 2, g % 2, nheads + g)
                      for g in range(kv_heads)]
        else:
            kspecs = [(kb_, k3, g, 0, nheads + g) for g in range(kv_heads)]
        specs = [(qb, q3, h // 2, h % 2, h) for h in range(nheads)] + kspecs
        lastc = None
        for (src, s3, hc, half, col_i) in specs:
            if (src is qb and half == 0) or (src is kb_ and half == 0):
                pass
            col = nrm[:, col_i:col_i + 1]
            for rnd in range(2):
                k.act(sqt.ap(), s3[:, hc, rnd * 1024:(rnd + 1) * 1024],
                      AF.Square, rd=(src,), wr=(sqt,), tag=f"{tag}_nsq",
                      engine="scalar")
                pb = k.P[3]
                for jj in range(2):
                    k.mm(pb.ap()[0:1, jj * 512:(jj + 1) * 512],
                         sel[half].ap(),
                         sqt.ap()[:, jj * 512:(jj + 1) * 512],
                         start=True, stop=True,
                         rd=(sqt, sel[half]), wr=(pb.slot(jj),),
                         tag=f"{tag}_nmm")
                k.reduce(red[:, rnd:rnd + 1], pb.ap()[0:1, 0:1024], ALU.max,
                         rd=(pb.slot(0), pb.slot(1)),
                         wr=(red.slot(rnd),), tag=f"{tag}_nred")
            k.reduce(col, red.ap()[:, 0:2], ALU.max,
                     rd=(red,), wr=(nrm.slot(col_i),), tag=f"{tag}_nmax")
        qrow = nrm.ap()[:, 0:nheads]
        krow = nrm.ap()[:, nheads:nq]
        brow = nrm.ap()[:, nq:nq + nheads]
        k.act(nrm.ap()[:, 0:nq], nrm.ap()[:, 0:nq], AF.Sqrt,
              rd=(nrm,), wr=(nrm,), tag=f"{tag}_sqr")
        for h in range(nheads):
            g = h // grp
            k.ts(brow[:, h:h + 1], qrow[:, h:h + 1],
                 krow[:, g:g + 1], None, ALU.mult,
                 rd=(nrm,), wr=(nrm.slot(("b", h)),), tag=f"{tag}_bm")
        k.ts(bbf.ap(), brow, -1.002 * exp_scale, None, ALU.mult,
             rd=tuple(nrm.slot(("b", h)) for h in range(nheads)),
             wr=(bbf,), tag=f"{tag}_bneg")
        for h in range(nheads):
            pb = k.P[3]
            k.mm(pb.ap()[:, 0:1], ones1.ap(), bbf.ap()[:, h:h + 1],
                 start=True, stop=True, rd=(ones1, bbf), wr=(pb.slot(0),),
                 tag=f"{tag}_bbc")
            k.copy(nbs.ap()[:, h:h + 1], pb.ap()[:, 0:1],
                   rd=(pb.slot(0),), wr=(nbs.slot(h),), tag=f"{tag}_bev")
        vext = [k.buf(f"{tag}_vx{i}", [128, 16 * 65], BF16) for i in range(2)]
        expb = [k.buf(f"{tag}_ex{i}", [128, 512], BF16) for i in range(2)]
        _dn = k.buf(tag + "_dn", [1, 512], F32)
        _db = k.buf(tag + "_db", [1, 512], BF16)
        _rb = k.buf(tag + "_rb", [64, 512], F32)
        denr = [_dn, _dn]
        denb = [_db, _db]
        rbs = [_rb, _rb]
        first = [True]
        lastg = [None]
        for h in range(nheads):
            hb = (h % 2) * 64
            hc = h // 2
            g = h // grp
            vx = vext[g % 2]
            if lastg[0] != g:
                lastg[0] = g
                k.memset(vx.ap(), 1.0, rd=(), wr=(vx,), tag=f"{tag}_vxm",
                         engine="gpsimd")
                vx3 = vx.ap().rearrange("p (c e) -> p c e", e=65)
                k.dma("gpsimd", vx3[:, :, 0:64], vtm.ap()[g],
                      rd=(vtm,), wr=(vx.slot("d"),), tag=f"{tag}_vxd")
            for qs in range(4):
                avb = k.P[2]
                av = avb.ap()[0:65, 512:1024]
                for kt in range(16):
                    sb = k.P[kt % 2]
                    sv = sb.ap()[:, 0:512]
                    if grp == 1:
                        kap = k3[hb:hb + 64, hc, kt * 128:(kt + 1) * 128]
                    else:
                        kap = k3[hb:hb + 64, g, kt * 128:(kt + 1) * 128]
                    k.mm(sv, kap,
                         q3[hb:hb + 64, hc, qs * 512:(qs + 1) * 512],
                         start=True, stop=True,
                         rd=(kb_, qb), wr=(sb.slot(0),), tag=f"{tag}_s")
                    eb = expb[kt % 2]
                    k.act(eb.ap(), sv, AF.Exp, bias=nbs[:, h:h + 1],
                          scale=exp_scale,
                          rd=(sb.slot(0), nbs.slot(h)), wr=(eb,),
                          tag=f"{tag}_e")
                    k.mm(av, vx.ap()[:, kt * 65:(kt + 1) * 65], eb.ap(),
                         start=(kt == 0), stop=(kt == 15),
                         rd=(vx, vx.slot("d"), eb), wr=(avb.slot(1),),
                         tag=f"{tag}_av")
                dn = denr[qs % 2]
                db = denb[qs % 2]
                rb = rbs[qs % 2]
                k.copy(dn.ap(), avb.ap()[64:65, 512:1024],
                       rd=(avb.slot(1),), wr=(dn,), tag=f"{tag}_dcp")
                if getattr(self, "dbg_attn", False) and h == 0 and qs == 0:
                    dbg_eb = k.buf(tag + "_dbgeb", [128, 512], BF16)
                    k.copy(dbg_eb.ap(), eb.ap(), rd=(eb,), wr=(dbg_eb,),
                           tag="dbg_eb_cp")
                    self.dbg_dump("den0", dn)
                    self.dbg_dump("eb15", dbg_eb)
                    dbg_av = k.buf(tag + "_dbgav", [65, 512], F32)
                    k.copy(dbg_av.ap(), avb.ap()[0:65, 512:1024],
                           rd=(avb.slot(1),), wr=(dbg_av,), tag="dbg_av_cp")
                    self.dbg_dump("av0", dbg_av)
                k.recip(dn.ap(), dn.ap(), rd=(dn,), wr=(dn,),
                        tag=f"{tag}_dre")
                k.copy(db.ap(), dn.ap(), rd=(dn,), wr=(db,),
                       tag=f"{tag}_dbf")
                pb2 = k.P[3]
                k.mm(pb2.ap()[0:64, 0:512], ones1.ap()[:, 0:64], db.ap(),
                     start=True, stop=True, rd=(ones1, db),
                     wr=(pb2.slot(0),), tag=f"{tag}_rbc")
                k.copy(rb.ap(), pb2.ap()[0:64, 0:512],
                       rd=(pb2.slot(0),), wr=(rb,), tag=f"{tag}_rbe")
                wr_item = att_dst if first[0] else att_dst.slot((h, qs))
                first[0] = False
                k.tt(a3[hb:hb + 64, hc, qs * 512:(qs + 1) * 512],
                     avb.ap()[0:64, 512:1024], rb.ap(), ALU.mult,
                     rd=(avb.slot(1), rb), wr=(wr_item,), tag=f"{tag}_anrm")

    def emit_l2_attn(self, tag):
        k = self.k
        m = k.mark()
        Wq = self.input("L2_attn_Wq")
        Wk = self.input("L2_attn_Wk")
        Wv = self.input("L2_attn_Wv")
        qb = k.buf(tag + "_q", [128, 8 * 2048], BF16)
        kd = k.buf(tag + "_kd", [128, 4 * 2048], BF16)
        vtm = self.dram_scratch("vtm4", [4, 128, 16, 64], BF16)
        mp = k.mark()
        kb_ = k.buf(tag + "_k", [128, 2 * 2048], BF16)
        _wq = k.buf(tag + "_wq", [128, ND * 512], BF16)
        wqs = [_wq, _wq]
        vfm = [k.buf(f"{tag}_vf{i}", [128, 512], BF16) for i in range(2)]
        vstg = [k.buf(f"{tag}_vs{i}", [128, 512], BF16) for i in range(2)]
        rtmp = k.buf(tag + "_rt", [128, 512], F32)
        cosq = k.buf(tag + "_cosq", [128, 2048], BF16)
        sinq = k.buf(tag + "_sinq", [128, 2048], BF16)
        k.dma("scalar", cosq.ap(), self._rope_dram[0].ap(),
              rd=(), wr=(cosq,), tag=tag + "_cql")
        k.dma("scalar", sinq.ap(), self._rope_dram[1].ap(),
              rd=(), wr=(sinq,), tag=tag + "_sql")
        rR = self.k._consts["ropeR"]
        for sl in range(2):
            wb = wqs[sl % 2]
            self.load_weight_cols(f"{tag}_wq{sl}", Wq, wb, 1024,
                                  sl * 512, 512, queue="sync")
            for oc4 in range(4):
                qc = sl * 4 + oc4
                for ts4 in range(4):
                    pb = k.P[ts4 % 2]
                    pv = pb.ap()[:, 0:512]
                    for dc in range(ND):
                        k.mm(pv, wb.ap()[:, dc * 512 + oc4 * 128:
                                         dc * 512 + (oc4 + 1) * 128],
                             self.xnf(dc)[:, ts4 * 512:(ts4 + 1) * 512],
                             start=(dc == 0), stop=(dc == ND - 1),
                             rd=(wb, self.xn_fm), wr=(pb.slot(0),),
                             tag=f"{tag}_qp")
                        k.copy(qb.ap()[:, qc * 2048 + ts4 * 512:
                                       qc * 2048 + (ts4 + 1) * 512],
                               pv, rd=(pb.slot(0),),
                               wr=(qb.slot((qc, ts4)),),
                               tag=f"{tag}_qev", engine="scalar") \
                            if dc == ND - 1 else None
        wkb = wqs[0]
        self.load_weight_cols(f"{tag}_wk", Wk, wkb, 1024, 0, 256,
                              queue="sync")
        for oc2 in range(2):
            for ts4 in range(4):
                pb = k.P[ts4 % 2]
                pv = pb.ap()[:, 0:512]
                for dc in range(ND):
                    k.mm(pv, wkb.ap()[:, dc * 256 + oc2 * 128:
                                      dc * 256 + (oc2 + 1) * 128],
                         self.xnf(dc)[:, ts4 * 512:(ts4 + 1) * 512],
                         start=(dc == 0), stop=(dc == ND - 1),
                         rd=(wkb, self.xn_fm), wr=(pb.slot(0),),
                         tag=f"{tag}_kp")
                k.copy(kb_.ap()[:, oc2 * 2048 + ts4 * 512:
                                oc2 * 2048 + (ts4 + 1) * 512],
                       pv, rd=(pb.slot(0),), wr=(kb_.slot((oc2, ts4)),),
                       tag=f"{tag}_kev", engine="scalar")
        for (buf_, nch, ct, st_) in ((qb, 8, cosq, sinq), (kb_, 2, cosq, sinq)):
            b3 = buf_.ap().rearrange("p (c t) -> p c t", t=2048)
            for hc in range(nch):
                for ts4 in range(4):
                    sl_ = slice(ts4 * 512, (ts4 + 1) * 512)
                    pb = k.P[2]
                    k.mm(pb.ap()[:, 0:512], rR.ap(), b3[:, hc, sl_],
                         start=True, stop=True,
                         rd=(buf_.slot((hc, ts4)), rR), wr=(pb.slot(0),),
                         tag=f"{tag}_rr")
                    k.tt(rtmp.ap(), pb.ap()[:, 0:512], st_.ap()[:, sl_],
                         ALU.mult, rd=(pb.slot(0), st_), wr=(rtmp,),
                         tag=f"{tag}_rs")
                    k.tt(b3[:, hc, sl_], b3[:, hc, sl_], ct.ap()[:, sl_],
                         ALU.mult, rd=(buf_.slot((hc, ts4)), ct),
                         wr=(buf_.slot((hc, ts4)),), tag=f"{tag}_rc")
                    k.tt(b3[:, hc, sl_], b3[:, hc, sl_], rtmp.ap(),
                         ALU.add, rd=(buf_.slot((hc, ts4)), rtmp),
                         wr=(buf_.slot((hc, ts4)),), tag=f"{tag}_ra")
        self.load_weight_cols(f"{tag}_wv", Wv, wkb, 1024, 0, 256,
                              queue="sync")
        for vc in range(2):
            for ts4 in range(4):
                pb = k.P[ts4 % 2]
                pv = pb.ap()[:, 0:512]
                for dc in range(ND):
                    k.mm(pv, wkb.ap()[:, dc * 256 + vc * 128:
                                      dc * 256 + (vc + 1) * 128],
                         self.xnf(dc)[:, ts4 * 512:(ts4 + 1) * 512],
                         start=(dc == 0), stop=(dc == ND - 1),
                         rd=(wkb, self.xn_fm), wr=(pb.slot(0),),
                         tag=f"{tag}_vp")
                vf = vfm[ts4 % 2]
                vs = vstg[ts4 % 2]
                k.copy(vf.ap(), pv, rd=(pb.slot(0),), wr=(vf,),
                       tag=f"{tag}_vev", engine="scalar")
                pt = k.P[2]
                ptv = pbf(pt)
                for tt in range(4):
                    k.transpose(ptv[:, tt * 128:(tt + 1) * 128],
                                vf.ap()[:, tt * 128:(tt + 1) * 128],
                                rd=(vf,), wr=(pt.slot(0),),
                                tag=f"{tag}_vtr")
                k.copy(vs.ap(), ptv[:, 0:512], rd=(pt.slot(0),),
                       wr=(vs,), tag=f"{tag}_vsc", engine="vector")
                s4 = vs.ap().rearrange("p (tt g c) -> p tt g c", g=2, c=64)
                for gg in range(2):
                    k.dma("gpsimd",
                          vtm.ap()[vc * 2 + gg, :, ts4 * 4:(ts4 + 1) * 4, :],
                          s4[:, :, gg, :],
                          rd=(vs,), wr=(vtm.slot((vc, ts4, gg)),),
                          tag=f"{tag}_vdma")
        kd3 = kd.ap().rearrange("p (c t) -> p c t", t=2048)
        k3_ = kb_.ap().rearrange("p (c t) -> p c t", t=2048)
        for g in range(4):
            for half in range(2):
                k.copy(kd3[half * 64:half * 64 + 64, g, :],
                       k3_[(g % 2) * 64:(g % 2) * 64 + 64, g // 2, :],
                       rd=(kb_.slot((g // 2, 0)), kb_.slot((g // 2, 1)),
                           kb_.slot((g // 2, 2)), kb_.slot((g // 2, 3))),
                       wr=(kd.slot((g, half)),), tag=f"{tag}_kdup",
                       engine="vector")
        k.release(mp)
        self._attn_core(tag, qb, kd, 16, 4, vtm, self.xn_fm, exp_scale=8.0)
        k.release(m)
        m2 = k.mark()
        woN = k.buf(tag + "_wo", [128, 8192], BF16)
        self.load_weight_cols(tag + "_won", self.input("L2_attn_Wo"), woN,
                              1024, 0, 1024, queue="sync")
        _, nwr = self.load_bias(tag + "_nw", self.input("L2_attn_norm_w"),
                                1024, col=False)
        nwb = k.buf(tag + "_nwb", [128, 1024], F32)
        nwbf = self.bf_row(tag + "_nw", nwr, 1024)
        ones1 = self.ones1()
        pb0 = k.P[3]
        for jh in range(2):
            k.mm(pb0.ap()[:, jh * 512:(jh + 1) * 512], ones1.ap(),
                 nwbf.ap()[:, jh * 512:(jh + 1) * 512],
                 start=True, stop=True, rd=(ones1, nwbf),
                 wr=(pb0.slot(jh),), tag=f"{tag}_nwb")
        k.copy(nwb.ap(), pb0.ap(), rd=(pb0.slot(0), pb0.slot(1)),
               wr=(nwb,), tag=f"{tag}_nwe", engine="vector")
        att3 = self.xn_fm.ap().rearrange("p (cc t) -> p cc t", t=2048)
        sqs = self.ln_sq[0]
        st = self.ln_st
        eps6 = self.ccol(1e-6)
        t2b = k.buf(tag + "_t2", [128, 1024], F32)
        for ti in range(NT):
            pb = k.P[2 + ti % 2]
            for cc in range(ND):
                for jh in range(2):
                    k.mm(pb.ap()[:, jh * 512:(jh + 1) * 512],
                         att3[:, cc, ti * 128:(ti + 1) * 128],
                         woN.ap()[:, cc * 1024 + jh * 512:
                                  cc * 1024 + (jh + 1) * 512],
                         start=(cc == 0), stop=(cc == ND - 1),
                         rd=(self.xn_fm, woN), wr=(pb,), tag=f"{tag}_o")
            c0 = st[:, 8 * ti:8 * ti + 1]
            c1 = st[:, 8 * ti + 1:8 * ti + 2]
            k.act(sqs.ap(), pb.ap(), AF.Square, accum_out=c0,
                  rd=(pb,), wr=(sqs, st.slot((ti, 0))), tag=f"{tag}_sq")
            k.act(c1, c0, AF.Ln, bias=eps6.ap(), scale=1.0 / D,
                  rd=(st.slot((ti, 0)), eps6), wr=(st.slot((ti, 1)),),
                  tag=f"{tag}_ln")
            k.act(c1, c1, AF.Exp, scale=-0.5,
                  rd=(st.slot((ti, 1)),), wr=(st.slot((ti, 1)),),
                  tag=f"{tag}_ex")
            k.stt(t2b.ap(), pb.ap(), c1, nwb.ap(), ALU.mult, ALU.mult,
                  rd=(pb, st.slot((ti, 1)), nwb), wr=(t2b,),
                  tag=f"{tag}_sc")
            k.tt(self.xt(ti), t2b.ap(), self.xt(ti), ALU.add,
                 rd=(t2b, self.x.slot(ti)), wr=(self.x.slot(ti),),
                 tag=f"{tag}_acc")
        k.release(m2)

    def emit_router(self, tag, E, wr_dram, br_dram, gates, halve=True):
        k = self.k
        wrb = k.buf(tag + "_wr", [128, ND * E], BF16)
        self.load_weight_cols(tag + "_wrl", wr_dram, wrb, 1024, 0, E,
                              queue="scalar")
        brb = None
        if br_dram is not None:
            _, br = self.load_bias(tag + "_br", br_dram, E, col=False,
                                   queue="scalar")
            brb = self.bf_row(tag + "_br", br, E)
        ones1 = self.ones1()
        lgt = [k.buf(f"{tag}_lg{i}", [128, E], F32) for i in range(2)]
        msk = [k.buf(f"{tag}_mk{i}", [128, E], F32) for i in range(2)]
        tmp = [k.buf(f"{tag}_tp{i}", [128, E], F32) for i in range(2)]
        sc = k.buf(tag + "_sc", [128, 4 * NT], F32)
        for ti in range(NT):
            pb = k.P[3]
            pv = pb.ap()[:, 0:E]
            for dc in range(ND):
                k.mm(pv, self.xnf(dc)[:, ti * 128:(ti + 1) * 128],
                     wrb.ap()[:, dc * E:(dc + 1) * E],
                     start=(dc == 0), stop=(dc == ND - 1 and brb is None),
                     rd=(self.xn_fm, wrb), wr=(pb.slot(0),),
                     tag=f"{tag}_lmm")
            if brb is not None:
                k.mm(pv, ones1.ap(), brb.ap(), start=False, stop=True,
                     rd=(ones1, brb), wr=(pb.slot(0),), tag=f"{tag}_lb")
            lg = lgt[ti % 2]
            mk = msk[ti % 2]
            tp = tmp[ti % 2]
            s0 = sc[:, 4 * ti + 0:4 * ti + 1]
            s1 = sc[:, 4 * ti + 1:4 * ti + 2]
            s2 = sc[:, 4 * ti + 2:4 * ti + 3]
            s3 = sc[:, 4 * ti + 3:4 * ti + 4]
            k.copy(lg.ap(), pv, rd=(pb.slot(0),), wr=(lg,),
                   tag=f"{tag}_lev")
            k.reduce(s0, lg.ap(), ALU.max, rd=(lg,),
                     wr=(sc.slot((ti, 0)),), tag=f"{tag}_m1")
            k.ts(s1, s0, -1.0, None, ALU.mult, rd=(sc.slot((ti, 0)),),
                 wr=(sc.slot((ti, 1)),), tag=f"{tag}_nm1")
            k.ts(mk.ap(), lg.ap(), s0, None, ALU.is_ge,
                 rd=(lg, sc.slot((ti, 0))), wr=(mk,), tag=f"{tag}_eq1")
            k.stt(tp.ap(), mk.ap(), -1e30, lg.ap(), ALU.mult, ALU.add,
                  rd=(mk, lg), wr=(tp,), tag=f"{tag}_l2")
            k.reduce(s2, tp.ap(), ALU.max, rd=(tp,),
                     wr=(sc.slot((ti, 2)),), tag=f"{tag}_m2")
            k.ts(mk.ap(), lg.ap(), s2, None, ALU.is_ge,
                 rd=(lg, sc.slot((ti, 2))), wr=(mk,), tag=f"{tag}_keep")
            k.act(tp.ap(), lg.ap(), AF.Exp, bias=s1,
                  rd=(lg, sc.slot((ti, 1))), wr=(tp,), tag=f"{tag}_ex")
            k.tt(tp.ap(), tp.ap(), mk.ap(), ALU.mult, rd=(tp, mk),
                 wr=(tp,), tag=f"{tag}_gt")
            k.reduce(s3, tp.ap(), ALU.add, rd=(tp,),
                     wr=(sc.slot((ti, 3)),), tag=f"{tag}_dn")
            k.recip(s3, s3, rd=(sc.slot((ti, 3)),),
                    wr=(sc.slot((ti, 3)),), tag=f"{tag}_rc")
            if halve:
                k.ts(s3, s3, 0.5, None, ALU.mult,
                     rd=(sc.slot((ti, 3)),), wr=(sc.slot((ti, 3)),),
                     tag=f"{tag}_hv")
            k.ts(gates.ap()[:, ti * E:(ti + 1) * E], tp.ap(), s3, None,
                 ALU.mult, rd=(tp, sc.slot((ti, 3))),
                 wr=(gates.slot(ti),), tag=f"{tag}_g")

    def emit_l0_ffn(self, tag):
        k = self.k
        m = k.mark()
        E = 8
        gates = k.buf(tag + "_gates", [128, NT * E], F32)
        self.emit_router(tag + "_r", E, self.input("L0_ffn_Wr"),
                         self.input("L0_ffn_br"), gates)
        W1 = self.input("L0_ffn_W1")
        W2 = self.input("L0_ffn_W2")
        b1 = self.input("L0_ffn_b1")
        b2 = self.input("L0_ffn_b2")
        ones1 = self.ones1()
        w1s = [k.buf(f"{tag}_w1{i}", [128, ND * 512], BF16) for i in range(2)]
        hsb = k.buf(tag + "_h", [128, 32 * 512], BF16)
        w2c = [k.buf(f"{tag}_w2c{i}", [128, 1024], BF16) for i in range(2)]
        _b2r = k.buf(tag + "_b2r", [1, 1024], BF16)
        b2rb = [_b2r, _b2r]
        _b1c = k.buf(tag + "_b1c", [128, 32], F32)
        b1cb = [_b1c, _b1c]
        for e in range(E):
            b1c, _ = self.load_bias(f"{tag}_b1_{e}", b1[e], 4096, row=False,
                                    queue="scalar", colb=b1cb[e % 2])
            _, b2r = self.load_bias(f"{tag}_b2_{e}", b2[e], 1024, col=False,
                                    queue="scalar")
            b2b = self.bf_row(f"{tag}_b2_{e}", b2r, 1024, dest=b2rb[e % 2])
            for ts4 in range(4):
                for hs in range(8):
                    w1b = w1s[hs % 2]
                    self.load_weight_cols(f"{tag}_w1_{e}_{ts4}_{hs}", W1[e],
                                          w1b, 1024, hs * 512, 512,
                                          queue="sync")
                    for hc4 in range(4):
                        hc = hs * 4 + hc4
                        g = hc % 4
                        pb = k.P[g // 2]
                        sl = g % 2
                        pv = pb.ap()[:, sl * 512:(sl + 1) * 512]
                        for dc in range(ND):
                            k.mm(pv, w1b.ap()[:, dc * 512 + hc4 * 128:
                                              dc * 512 + (hc4 + 1) * 128],
                                 self.xnf(dc)[:, ts4 * 512:(ts4 + 1) * 512],
                                 start=(dc == 0), stop=(dc == ND - 1),
                                 rd=(w1b, self.xn_fm), wr=(pb.slot(sl),),
                                 tag=f"{tag}_hm")
                        k.act(hsb.ap()[:, hc * 512:(hc + 1) * 512], pv,
                              AF.Relu, bias=b1c[:, hc:hc + 1],
                              rd=(pb.slot(sl), b1c), wr=(hsb.slot(hc),),
                              tag=f"{tag}_hr")
                for hc in range(32):
                    wc = w2c[hc % 2]
                    self.load_weight_cols(f"{tag}_w2_{e}_{ts4}_{hc}",
                                          W2[e][hc * 128:(hc + 1) * 128, :],
                                          wc, 128, 0, 1024, queue="gpsimd")
                    for ti in range(4):
                        pb = k.P[ti]
                        for jh in range(2):
                            k.mm(pb.ap()[:, jh * 512:(jh + 1) * 512],
                                 hsb.ap()[:, hc * 512 + ti * 128:
                                          hc * 512 + (ti + 1) * 128],
                                 wc.ap()[:, jh * 512:(jh + 1) * 512],
                                 start=(hc == 0), stop=False,
                                 rd=(hsb.slot(hc), wc), wr=(pb,),
                                 tag=f"{tag}_ym")
                for ti in range(4):
                    pb = k.P[ti]
                    for jh in range(2):
                        k.mm(pb.ap()[:, jh * 512:(jh + 1) * 512],
                             ones1.ap(), b2b.ap()[:, jh * 512:(jh + 1) * 512],
                             start=False, stop=True,
                             rd=(ones1, b2b), wr=(pb,), tag=f"{tag}_yb")
                    tig = ts4 * 4 + ti
                    k.stt(self.xt(tig), pb.ap(),
                          gates[:, tig * E + e:tig * E + e + 1],
                          self.xt(tig), ALU.mult, ALU.add,
                          rd=(pb, gates.slot(tig), self.x.slot(tig)),
                          wr=(self.x.slot(tig),), tag=f"{tag}_acc")
        k.release(m)

    def emit_l1_ffn(self, tag):
        k = self.k
        E = 16
        m = k.mark()
        gates = k.buf(tag + "_gates", [128, NT * E], F32)
        self.emit_router(tag + "_r", E, self.input("L1_ffn_Wg"), None, gates)
        W1 = self.input("L1_ffn_W1")
        W11 = self.input("L1_ffn_W11")
        W2 = self.input("L1_ffn_W2")
        b1 = self.input("L1_ffn_b1")
        b11 = self.input("L1_ffn_b11")
        b2 = self.input("L1_ffn_b2")
        ones1 = self.ones1()
        _w1 = k.buf(tag + "_w1", [128, ND * 2048], BF16)
        w1b = [_w1, _w1]
        w11s = [k.buf(f"{tag}_w11{i}", [128, ND * 512], BF16)
                for i in range(2)]
        hsb = k.buf(tag + "_h", [128, 16 * 512], BF16)
        w2c = [k.buf(f"{tag}_w2c{i}", [128, 1024], BF16) for i in range(2)]
        t1b = k.buf(tag + "_t1", [128, 512], BF16)
        t11b = Buf(self.ln_sq[0].name, self.ln_sq[0].ap()[:, 0:512], k)
        b1cb = k.buf(tag + "_b1c", [128, 16], F32)
        b11cb = k.buf(tag + "_b11c", [128, 16], F32)
        b2rb = k.buf(tag + "_b2r", [1, 1024], BF16)
        for e in range(E):
            wa = w1b[e % 2]
            self.load_weight_cols(f"{tag}_w1_{e}", W1[e], wa, 1024, 0, 2048,
                                  queue="sync")
            b1c, _ = self.load_bias(f"{tag}_b1_{e}", b1[e], 2048, row=False,
                                    queue="scalar", colb=b1cb)
            b11c, _ = self.load_bias(f"{tag}_b11_{e}", b11[e], 2048,
                                     row=False, queue="scalar", colb=b11cb)
            _, b2r = self.load_bias(f"{tag}_b2_{e}", b2[e], 1024, col=False,
                                    queue="scalar")
            b2b = self.bf_row(f"{tag}_b2_{e}", b2r, 1024, dest=b2rb)
            for ts4 in range(4):
                for hc in range(16):
                    if hc % 4 == 0:
                        wb_ = w11s[(hc // 4) % 2]
                        self.load_weight_cols(
                            f"{tag}_w11_{e}_{ts4}_{hc}", W11[e], wb_, 1024,
                            (hc // 4) * 512, 512, queue="sync")
                    p1 = k.P[0]
                    p11 = k.P[1]
                    sl = hc % 2
                    hc4 = hc % 4
                    pv1 = p1.ap()[:, sl * 512:(sl + 1) * 512]
                    pv11 = p11.ap()[:, sl * 512:(sl + 1) * 512]
                    for dc in range(ND):
                        k.mm(pv1, wa.ap()[:, dc * 2048 + hc * 128:
                                          dc * 2048 + (hc + 1) * 128],
                             self.xnf(dc)[:, ts4 * 512:(ts4 + 1) * 512],
                             start=(dc == 0), stop=(dc == ND - 1),
                             rd=(wa, self.xn_fm), wr=(p1.slot(sl),),
                             tag=f"{tag}_h1m")
                    for dc in range(ND):
                        k.mm(pv11, wb_.ap()[:, dc * 512 + hc4 * 128:
                                            dc * 512 + (hc4 + 1) * 128],
                             self.xnf(dc)[:, ts4 * 512:(ts4 + 1) * 512],
                             start=(dc == 0), stop=(dc == ND - 1),
                             rd=(wb_, self.xn_fm), wr=(p11.slot(sl),),
                             tag=f"{tag}_h11m")
                    k.act(t1b.ap(), pv1, AF.Silu, bias=b1c[:, hc:hc + 1],
                          rd=(p1.slot(sl), b1cb), wr=(t1b,),
                          tag=f"{tag}_h1e")
                    k.ts(t11b.ap(), pv11, b11c[:, hc:hc + 1], None, ALU.add,
                         rd=(p11.slot(sl), b11cb), wr=(t11b,),
                         tag=f"{tag}_h11e")
                    k.tt(hsb.ap()[:, hc * 512:(hc + 1) * 512], t1b.ap(),
                         t11b.ap(), ALU.mult, rd=(t1b, t11b),
                         wr=(hsb.slot(hc),), tag=f"{tag}_hx")
                for hc in range(16):
                    wc = w2c[hc % 2]
                    self.load_weight_cols(f"{tag}_w2_{e}_{ts4}_{hc}",
                                          W2[e][hc * 128:(hc + 1) * 128, :],
                                          wc, 128, 0, 1024, queue="gpsimd")
                    for ti in range(4):
                        pb = k.P[ti]
                        for jh in range(2):
                            k.mm(pb.ap()[:, jh * 512:(jh + 1) * 512],
                                 hsb.ap()[:, hc * 512 + ti * 128:
                                          hc * 512 + (ti + 1) * 128],
                                 wc.ap()[:, jh * 512:(jh + 1) * 512],
                                 start=(hc == 0), stop=False,
                                 rd=(hsb.slot(hc), wc), wr=(pb,),
                                 tag=f"{tag}_ym")
                for ti in range(4):
                    pb = k.P[ti]
                    for jh in range(2):
                        k.mm(pb.ap()[:, jh * 512:(jh + 1) * 512],
                             ones1.ap(),
                             b2b.ap()[:, jh * 512:(jh + 1) * 512],
                             start=False, stop=True,
                             rd=(ones1, b2rb), wr=(pb,), tag=f"{tag}_yb")
                    tig = ts4 * 4 + ti
                    k.stt(self.xt(tig), pb.ap(),
                          gates[:, tig * E + e:tig * E + e + 1],
                          self.xt(tig), ALU.mult, ALU.add,
                          rd=(pb, gates.slot(tig), self.x.slot(tig)),
                          wr=(self.x.slot(tig),), tag=f"{tag}_acc")
        k.release(m)
        m2 = k.mark()
        Ws1 = self.input("L1_ffn_Ws1")
        Ws11 = self.input("L1_ffn_Ws11")
        Ws2 = self.input("L1_ffn_Ws2")
        bs1c, _ = self.load_bias(tag + "_bs1", self.input("L1_ffn_bs1"),
                                 4096, row=False, queue="scalar")
        bs11c, _ = self.load_bias(tag + "_bs11", self.input("L1_ffn_bs11"),
                                  4096, row=False, queue="scalar")
        _, bs2r = self.load_bias(tag + "_bs2", self.input("L1_ffn_bs2"),
                                 1024, col=False, scale=0.5, queue="scalar")
        bs2b = self.bf_row(tag + "_bs2", bs2r, 1024)
        ws1s = [k.buf(f"{tag}_ws1{i}", [128, ND * 512], BF16)
                for i in range(2)]
        ws11s = [k.buf(f"{tag}_ws11{i}", [128, ND * 512], BF16)
                 for i in range(2)]
        hsc = k.buf(tag + "_hs", [128, 4 * 512], BF16)
        w2cs = [k.buf(f"{tag}_w2s{i}", [128, 1024], BF16) for i in range(4)]
        for ts4 in range(4):
            for hs in range(8):
                wa, wb_ = ws1s[hs % 2], ws11s[hs % 2]
                self.load_weight_cols(f"{tag}_ws1_{ts4}_{hs}", Ws1, wa,
                                      1024, hs * 512, 512, queue="sync")
                self.load_weight_cols(f"{tag}_ws11_{ts4}_{hs}", Ws11, wb_,
                                      1024, hs * 512, 512, queue="sync")
                for hc4 in range(4):
                    hc = hs * 4 + hc4
                    p1, p11 = k.P[0], k.P[1]
                    sl = hc4 % 2
                    pv1 = p1.ap()[:, sl * 512:(sl + 1) * 512]
                    pv11 = p11.ap()[:, sl * 512:(sl + 1) * 512]
                    for dc in range(ND):
                        k.mm(pv1, wa.ap()[:, dc * 512 + hc4 * 128:
                                          dc * 512 + (hc4 + 1) * 128],
                             self.xnf(dc)[:, ts4 * 512:(ts4 + 1) * 512],
                             start=(dc == 0), stop=(dc == ND - 1),
                             rd=(wa, self.xn_fm), wr=(p1.slot(sl),),
                             tag=f"{tag}_s1m")
                    for dc in range(ND):
                        k.mm(pv11, wb_.ap()[:, dc * 512 + hc4 * 128:
                                            dc * 512 + (hc4 + 1) * 128],
                             self.xnf(dc)[:, ts4 * 512:(ts4 + 1) * 512],
                             start=(dc == 0), stop=(dc == ND - 1),
                             rd=(wb_, self.xn_fm), wr=(p11.slot(sl),),
                             tag=f"{tag}_s11m")
                    k.act(t1b.ap(), pv1, AF.Silu,
                          bias=bs1c[:, hs * 4 + hc4:hs * 4 + hc4 + 1],
                          rd=(p1.slot(sl), bs1c), wr=(t1b,),
                          tag=f"{tag}_s1e")
                    k.ts(t11b.ap(), pv11,
                         bs11c[:, hs * 4 + hc4:hs * 4 + hc4 + 1], None,
                         ALU.add, rd=(p11.slot(sl), bs11c), wr=(t11b,),
                         tag=f"{tag}_s11e")
                    k.tt(hsc.ap()[:, hc4 * 512:(hc4 + 1) * 512], t1b.ap(),
                         t11b.ap(), ALU.mult, rd=(t1b, t11b),
                         wr=(hsc.slot(hc4),), tag=f"{tag}_sx")
                for ti in range(4):
                    pb = k.P[2 + ti % 2]
                    for hc4 in range(4):
                        wc = w2cs[hc4]
                        if ti == 0:
                            self.load_weight_cols(
                                f"{tag}_ws2_{ts4}_{hs}_{hc4}",
                                Ws2[(hs * 4 + hc4) * 128:
                                    (hs * 4 + hc4 + 1) * 128, :],
                                wc, 128, 0, 1024, scale=0.5, queue="gpsimd")
                        for jh in range(2):
                            k.mm(pb.ap()[:, jh * 512:(jh + 1) * 512],
                                 hsc.ap()[:, hc4 * 512 + ti * 128:
                                          hc4 * 512 + (ti + 1) * 128],
                                 wc.ap()[:, jh * 512:(jh + 1) * 512],
                                 start=(hc4 == 0),
                                 stop=(hc4 == 3 and not (ts4 == 0 and False)
                                       and hs > 0),
                                 rd=(hsc.slot(hc4), wc), wr=(pb,),
                                 tag=f"{tag}_sym")
                    if hs == 0:
                        for jh in range(2):
                            k.mm(pb.ap()[:, jh * 512:(jh + 1) * 512],
                                 ones1.ap(),
                                 bs2b.ap()[:, jh * 512:(jh + 1) * 512],
                                 start=False, stop=True,
                                 rd=(ones1, bs2b), wr=(pb,),
                                 tag=f"{tag}_syb")
                    tig = ts4 * 4 + ti
                    k.tt(self.xt(tig), pb.ap(), self.xt(tig), ALU.add,
                         rd=(pb, self.x.slot(tig)), wr=(self.x.slot(tig),),
                         tag=f"{tag}_sacc")
        k.release(m2)

    def emit_l3_ffn(self, tag):
        k = self.k
        E = 8
        m = k.mark()
        ones1 = self.ones1()
        ones128 = k.const("ones128_bf16", np.ones((128, 1), dtype=BFNP))
        identf = self.k._consts["ident_f32"]
        gw = self.input("L3_ffn_gate_weight")
        gwT = k.buf(tag + "_gwT", [128, ND * E], BF16)
        sg = self.wstage[0]
        k.dma("scalar", sg.ap()[0:8, 0:1024], gw, rd=(), wr=(sg,),
              tag=tag + "_gwd")
        gwb = k.buf(tag + "_gwb", [8, 1024], BF16)
        k.copy(gwb.ap(), sg.ap()[0:8, 0:1024], rd=(sg,), wr=(gwb,),
               tag=tag + "_gwc")
        pb0 = k.P[3]
        pv0 = pbf(pb0)
        for dc in range(ND):
            k.transpose(pv0[:, dc * E:(dc + 1) * E],
                        gwb.ap()[:, dc * 128:(dc + 1) * 128],
                        rd=(gwb,), wr=(pb0.slot(0),), tag=tag + "_gwt")
        k.copy(gwT.ap(), pv0[:, 0:ND * E], rd=(pb0.slot(0),), wr=(gwT,),
               tag=tag + "_gwe")
        dense0 = k.buf(tag + "_d0", [128, NT * E], F32)
        combine = k.buf(tag + "_cmb", [128, NT * E], F32)
        lgt = [k.buf(f"{tag}_lg{i}", [128, E], F32) for i in range(2)]
        msk = [k.buf(f"{tag}_mk{i}", [128, E], F32) for i in range(2)]
        tmp = [k.buf(f"{tag}_tp{i}", [128, E], F32) for i in range(2)]
        eqb = [k.buf(f"{tag}_eb{i}", [128, E], BF16) for i in range(2)]
        sc = k.buf(tag + "_sc", [128, 4 * NT], F32)
        cnt_pb = k.P[2]
        for ti in range(NT):
            pb = k.P[3]
            pv = pb.ap()[:, 0:E]
            for dc in range(ND):
                k.mm(pv, self.xnf(dc)[:, ti * 128:(ti + 1) * 128],
                     gwT.ap()[:, dc * E:(dc + 1) * E],
                     start=(dc == 0), stop=(dc == ND - 1),
                     rd=(self.xn_fm, gwT), wr=(pb.slot(0),),
                     tag=f"{tag}_lmm")
            lg, mk, tp, eb = (lgt[ti % 2], msk[ti % 2], tmp[ti % 2],
                              eqb[ti % 2])
            s0 = sc[:, 4 * ti + 0:4 * ti + 1]
            s1 = sc[:, 4 * ti + 1:4 * ti + 2]
            s2 = sc[:, 4 * ti + 2:4 * ti + 3]
            s3 = sc[:, 4 * ti + 3:4 * ti + 4]
            k.copy(lg.ap(), pv, rd=(pb.slot(0),), wr=(lg,),
                   tag=f"{tag}_lev")
            k.reduce(s0, lg.ap(), ALU.max, rd=(lg,),
                     wr=(sc.slot((ti, 0)),), tag=f"{tag}_m0")
            k.ts(s1, s0, -1.0, None, ALU.mult, rd=(sc.slot((ti, 0)),),
                 wr=(sc.slot((ti, 1)),), tag=f"{tag}_nm0")
            k.act(lg.ap(), lg.ap(), AF.Exp, bias=s1,
                  rd=(lg, sc.slot((ti, 1))), wr=(lg,), tag=f"{tag}_ex")
            k.reduce(s2, lg.ap(), ALU.add, rd=(lg,),
                     wr=(sc.slot((ti, 2)),), tag=f"{tag}_sm")
            k.recip(s2, s2, rd=(sc.slot((ti, 2)),),
                    wr=(sc.slot((ti, 2)),), tag=f"{tag}_rc")
            k.ts(lg.ap(), lg.ap(), s2, None, ALU.mult,
                 rd=(lg, sc.slot((ti, 2))), wr=(lg,), tag=f"{tag}_pr")
            k.reduce(s0, lg.ap(), ALU.max, rd=(lg,),
                     wr=(sc.slot((ti, 0)),), tag=f"{tag}_m1")
            k.ts(mk.ap(), lg.ap(), s0, None, ALU.is_ge,
                 rd=(lg, sc.slot((ti, 0))), wr=(mk,), tag=f"{tag}_eq1")
            k.copy(eb.ap(), mk.ap(), rd=(mk,), wr=(eb,),
                   tag=f"{tag}_eqb")
            k.mm(cnt_pb.ap()[0:E, 0:1], eb.ap(), ones128.ap(),
                 start=(ti == 0), stop=(ti == NT - 1),
                 rd=(eb, ones128), wr=(cnt_pb.slot(0),),
                 tag=f"{tag}_cnt")
            k.stt(tp.ap(), mk.ap(), -1e30, lg.ap(), ALU.mult, ALU.add,
                  rd=(mk, lg), wr=(tp,), tag=f"{tag}_l2")
            k.reduce(s2, tp.ap(), ALU.max, rd=(tp,),
                     wr=(sc.slot((ti, 2)),), tag=f"{tag}_m2")
            k.ts(mk.ap(), lg.ap(), s2, None, ALU.is_ge,
                 rd=(lg, sc.slot((ti, 2))), wr=(mk,), tag=f"{tag}_keep")
            k.tt(tp.ap(), lg.ap(), mk.ap(), ALU.mult, rd=(lg, mk),
                 wr=(tp,), tag=f"{tag}_tw")
            k.reduce(s3, tp.ap(), ALU.add, rd=(tp,),
                     wr=(sc.slot((ti, 3)),), tag=f"{tag}_dn")
            k.recip(s3, s3, rd=(sc.slot((ti, 3)),),
                    wr=(sc.slot((ti, 3)),), tag=f"{tag}_rc2")
            k.ts(dense0.ap()[:, ti * E:(ti + 1) * E], tp.ap(), s3, None,
                 ALU.mult, rd=(tp, sc.slot((ti, 3))),
                 wr=(dense0.slot(ti),), tag=f"{tag}_d0w")
        sim = k.buf(tag + "_sim", [8, E], F32)
        k.dma("scalar", sim.ap(), self.input("L3_ffn_similarity_matrix"),
              rd=(), wr=(sim,), tag=tag + "_simd")
        cnt = k.buf(tag + "_cntb", [8, 1], F32)
        notp = k.buf(tag + "_notp", [8, 1], F32)
        k.copy(cnt.ap(), cnt_pb.ap()[0:E, 0:1], rd=(cnt_pb.slot(0),),
               wr=(cnt,), tag=tag + "_ce")
        k.ts(notp.ap(), cnt.ap(), 0.5, None, ALU.is_lt, rd=(cnt,),
             wr=(notp,), tag=tag + "_np")
        prow = k.buf(tag + "_prow", [1, E], F32)
        prowb = k.buf(tag + "_prowb", [1, E], BF16)
        k.transpose_f32(k.P[3].ap()[0:1, 0:E], notp.ap(), rd=(notp,),
                        wr=(k.P[3].slot(0),), tag=tag + "_ptr")
        k.ts(prow.ap(), k.P[3].ap()[0:1, 0:E], -1.0, 1.0, ALU.mult,
             ALU.add, rd=(k.P[3].slot(0),), wr=(prow,), tag=tag + "_pr1")
        k.copy(prowb.ap(), prow.ap(), rd=(prow,), wr=(prowb,),
               tag=tag + "_prb")
        pbc = k.buf(tag + "_pbc", [8, E], F32)
        k.mm(k.P[3].ap()[0:8, 0:E], ones1.ap()[:, 0:8], prowb.ap(),
             start=True, stop=True, rd=(ones1, prowb),
             wr=(k.P[3].slot(0),), tag=tag + "_pbm")
        k.copy(pbc.ap(), k.P[3].ap()[0:8, 0:E], rd=(k.P[3].slot(0),),
               wr=(pbc,), tag=tag + "_pbe")
        smm = k.buf(tag + "_smm", [8, E], F32)
        k.ts(smm.ap(), pbc.ap(), 1.0, 1e30, ALU.subtract, ALU.mult,
             rd=(pbc,), wr=(smm,), tag=tag + "_msk")
        k.tt(smm.ap(), sim.ap(), smm.ap(), ALU.add, rd=(sim, smm),
             wr=(smm,), tag=tag + "_sm2")
        bsim = k.buf(tag + "_bsim", [8, 1], F32)
        k.reduce(bsim.ap(), smm.ap(), ALU.max, rd=(smm,), wr=(bsim,),
                 tag=tag + "_bs")
        eqbm = k.buf(tag + "_eqbm", [8, E], F32)
        k.ts(eqbm.ap(), smm.ap(), bsim.ap(), None, ALU.is_ge,
             rd=(smm, bsim), wr=(eqbm,), tag=tag + "_eqb2")
        cond = k.buf(tag + "_cond", [8, 1], F32)
        k.ts(cond.ap(), bsim.ap(), 0.7, None, ALU.is_ge, rd=(bsim,),
             wr=(cond,), tag=tag + "_c1")
        k.tt(cond.ap(), cond.ap(), notp.ap(), ALU.mult, rd=(cond, notp),
             wr=(cond,), tag=tag + "_c2")
        Rb = k.buf(tag + "_R", [8, E], BF16)
        r1 = k.buf(tag + "_r1", [8, E], F32)
        k.ts(r1.ap(), eqbm.ap(), cond.ap(), None, ALU.mult,
             rd=(eqbm, cond), wr=(r1,), tag=tag + "_r1")
        invc = k.buf(tag + "_ivc", [8, 1], F32)
        k.ts(invc.ap(), cond.ap(), -1.0, 1.0, ALU.mult, ALU.add,
             rd=(cond,), wr=(invc,), tag=tag + "_iv")
        r2 = k.buf(tag + "_r2", [8, E], F32)
        k.ts(r2.ap(), identf.ap()[0:8, 0:E], invc.ap(), None, ALU.mult,
             rd=(identf, invc), wr=(r2,), tag=tag + "_r2")
        k.tt(r1.ap(), r1.ap(), r2.ap(), ALU.add, rd=(r1, r2), wr=(r1,),
             tag=tag + "_rs")
        k.copy(Rb.ap(), r1.ap(), rd=(r1,), wr=(Rb,), tag=tag + "_rb")
        d0b = [k.buf(f"{tag}_d0b{i}", [128, E], BF16) for i in range(2)]
        for ti in range(NT):
            lg, mk, tp = lgt[ti % 2], msk[ti % 2], tmp[ti % 2]
            db_ = d0b[ti % 2]
            s0 = sc[:, 4 * ti + 0:4 * ti + 1]
            s2 = sc[:, 4 * ti + 2:4 * ti + 3]
            k.copy(db_.ap(), dense0.ap()[:, ti * E:(ti + 1) * E],
                   rd=(dense0.slot(ti),), wr=(db_,), tag=f"{tag}_d0c")
            pb = k.P[3]
            pv2 = pbf(pb)
            k.transpose(pv2[0:E, 0:128], db_.ap(), rd=(db_,),
                        wr=(pb.slot(0),), tag=f"{tag}_d0t")
            d0T = k.buf(f"{tag}_d0T{ti % 2}", [8, 128], BF16) \
                if ti < 2 else self._l3f_d0T[ti % 2]
            if ti < 2:
                if not hasattr(self, "_l3f_d0T"):
                    self._l3f_d0T = [None, None]
                self._l3f_d0T[ti % 2] = d0T
            k.copy(d0T.ap(), pv2[0:8, 0:128], rd=(pb.slot(0),),
                   wr=(d0T,), tag=f"{tag}_d0Te")
            k.mm(pb.ap()[:, 0:E], d0T.ap(), Rb.ap(), start=True, stop=True,
                 rd=(d0T, Rb), wr=(pb.slot(0),), tag=f"{tag}_dmm")
            k.copy(lg.ap(), pb.ap()[:, 0:E], rd=(pb.slot(0),), wr=(lg,),
                   tag=f"{tag}_dev")
            k.reduce(s0, lg.ap(), ALU.max, rd=(lg,),
                     wr=(sc.slot((ti, 0)),), tag=f"{tag}_fm1")
            k.ts(mk.ap(), lg.ap(), s0, None, ALU.is_ge,
                 rd=(lg, sc.slot((ti, 0))), wr=(mk,), tag=f"{tag}_feq")
            k.stt(tp.ap(), mk.ap(), -1e30, lg.ap(), ALU.mult, ALU.add,
                  rd=(mk, lg), wr=(tp,), tag=f"{tag}_fl2")
            k.reduce(s2, tp.ap(), ALU.max, rd=(tp,),
                     wr=(sc.slot((ti, 2)),), tag=f"{tag}_fm2")
            k.ts(mk.ap(), lg.ap(), s2, None, ALU.is_ge,
                 rd=(lg, sc.slot((ti, 2))), wr=(mk,), tag=f"{tag}_fkp")
            k.tt(mk.ap(), mk.ap(), lg.ap(), ALU.mult, rd=(mk, lg),
                 wr=(mk,), tag=f"{tag}_fw")
            k.ts(tp.ap(), lg.ap(), 0.0, None, ALU.is_gt, rd=(lg,),
                 wr=(tp,), tag=f"{tag}_fpos")
            k.tt(mk.ap(), mk.ap(), tp.ap(), ALU.mult, rd=(mk, tp),
                 wr=(mk,), tag=f"{tag}_fw2")
            k.ts(combine.ap()[:, ti * E:(ti + 1) * E], mk.ap(), 0.5, None,
                 ALU.mult, rd=(mk,), wr=(combine.slot(ti),),
                 tag=f"{tag}_cw")
        GU = self.input("L3_ffn_gate_up_proj")
        DN = self.input("L3_ffn_down_proj")
        gupT = k.buf(tag + "_gupT", [128, ND * 2048], BF16)
        dnT = k.buf(tag + "_dnT", [128, ND * 1024], BF16)
        hsb = k.buf(tag + "_h", [128, 8 * 512], BF16)
        t1b = k.buf(tag + "_t1", [128, 512], BF16)
        t11b = Buf(self.ln_sq[0].name, self.ln_sq[0].ap()[:, 0:512], k)
        for e in range(E):
            self.load_weight_T(f"{tag}_gu_{e}", GU[e], gupT, 0, 2048, 1024,
                               queue="sync")
            self.load_weight_T(f"{tag}_dn_{e}", DN[e], dnT, 0, 1024, 1024,
                               queue="sync")
            for ts4 in range(4):
                for fc in range(8):
                    p1, p11 = k.P[0], k.P[1]
                    sl = fc % 2
                    pv1 = p1.ap()[:, sl * 512:(sl + 1) * 512]
                    pv11 = p11.ap()[:, sl * 512:(sl + 1) * 512]
                    for dc in range(ND):
                        k.mm(pv1, gupT.ap()[:, dc * 2048 + fc * 128:
                                            dc * 2048 + (fc + 1) * 128],
                             self.xnf(dc)[:, ts4 * 512:(ts4 + 1) * 512],
                             start=(dc == 0), stop=(dc == ND - 1),
                             rd=(gupT, self.xn_fm), wr=(p1.slot(sl),),
                             tag=f"{tag}_gm")
                    for dc in range(ND):
                        k.mm(pv11, gupT.ap()[:, dc * 2048 + 1024 + fc * 128:
                                             dc * 2048 + 1024 +
                                             (fc + 1) * 128],
                             self.xnf(dc)[:, ts4 * 512:(ts4 + 1) * 512],
                             start=(dc == 0), stop=(dc == ND - 1),
                             rd=(gupT, self.xn_fm), wr=(p11.slot(sl),),
                             tag=f"{tag}_um")
                    k.act(t1b.ap(), pv1, AF.Silu,
                          rd=(p1.slot(sl),), wr=(t1b,), tag=f"{tag}_ge")
                    k.copy(t11b.ap(), pv11, rd=(p11.slot(sl),),
                           wr=(t11b,), tag=f"{tag}_ue")
                    k.tt(hsb.ap()[:, fc * 512:(fc + 1) * 512], t1b.ap(),
                         t11b.ap(), ALU.mult, rd=(t1b, t11b),
                         wr=(hsb.slot(fc),), tag=f"{tag}_hx")
                for fc in range(8):
                    for ti in range(4):
                        pb = k.P[ti]
                        for jh in range(2):
                            k.mm(pb.ap()[:, jh * 512:(jh + 1) * 512],
                                 hsb.ap()[:, fc * 512 + ti * 128:
                                          fc * 512 + (ti + 1) * 128],
                                 dnT.ap()[:, fc * 1024 + jh * 512:
                                          fc * 1024 + (jh + 1) * 512],
                                 start=(fc == 0), stop=(fc == 7),
                                 rd=(hsb.slot(fc), dnT), wr=(pb,),
                                 tag=f"{tag}_ym")
                for ti in range(4):
                    pb = k.P[ti]
                    tig = ts4 * 4 + ti
                    k.stt(self.xt(tig), pb.ap(),
                          combine[:, tig * E + e:tig * E + e + 1],
                          self.xt(tig), ALU.mult, ALU.add,
                          rd=(pb, combine.slot(tig), self.x.slot(tig)),
                          wr=(self.x.slot(tig),), tag=f"{tag}_acc")
        k.release(m)

    def emit_l0_attn(self, tag):
        k = self.k
        m = k.mark()
        H, DH, DF = 8, 16, 273
        ones1 = self.ones1()
        ones128 = self.k._consts["ones128_bf16"]
        s4 = float(np.sqrt(np.sqrt(0.5)))
        sq = [k.buf(f"{tag}_sq{i}", [128, 512], BF16) for i in range(2)]
        srow = k.buf(tag + "_srow", [1, 2048], F32)
        srowb = k.buf(tag + "_srowb", [1, 2048], BF16)
        scol = k.buf(tag + "_scol", [128, NT], F32)
        for sl in range(4):
            pb = k.P[3]
            for dc in range(ND):
                k.act(sq[dc % 2].ap(),
                      self.xnf(dc)[:, sl * 512:(sl + 1) * 512], AF.Square,
                      rd=(self.xn_fm,), wr=(sq[dc % 2],),
                      tag=f"{tag}_sq", engine="scalar")
                k.mm(pb.ap()[0:1, sl % 2 * 512:sl % 2 * 512 + 512],
                     ones128.ap(), sq[dc % 2].ap(),
                     start=(dc == 0), stop=(dc == ND - 1),
                     rd=(sq[dc % 2], ones128), wr=(pb.slot(sl % 2),),
                     tag=f"{tag}_ssm")
            k.copy(srow.ap()[:, sl * 512:(sl + 1) * 512],
                   pb.ap()[0:1, sl % 2 * 512:sl % 2 * 512 + 512],
                   rd=(pb.slot(sl % 2),), wr=(srow.slot(sl),),
                   tag=f"{tag}_sev")
        k.act(srow.ap(), srow.ap(), AF.Sqrt,
              rd=tuple(srow.slot(i) for i in range(4)), wr=(srow,),
              tag=f"{tag}_ssq")
        k.ts(srow.ap(), srow.ap(), 1e-12, None, ALU.max, rd=(srow,),
             wr=(srow,), tag=f"{tag}_smx")
        k.recip(srow.ap(), srow.ap(), rd=(srow,), wr=(srow,),
                tag=f"{tag}_src")
        k.ts(srow.ap(), srow.ap(), 32.0, None, ALU.mult, rd=(srow,),
             wr=(srow,), tag=f"{tag}_s32")
        k.copy(srowb.ap(), srow.ap(), rd=(srow,), wr=(srowb,),
               tag=f"{tag}_sbf")
        for ti in range(NT):
            pb = k.P[3]
            k.mm(pb.ap()[:, 0:1], srowb.ap()[:, ti * 128:(ti + 1) * 128],
                 ones1.ap()[:, 0:1], start=True, stop=True,
                 rd=(srowb, ones1), wr=(pb.slot(0),), tag=f"{tag}_scm")
            k.copy(scol.ap()[:, ti:ti + 1], pb.ap()[:, 0:1],
                   rd=(pb.slot(0),), wr=(scol.slot(ti),),
                   tag=f"{tag}_sce")
        gmc, _ = self.load_bias(tag + "_gm", self.input("L0_attn_gamma"),
                                1024, row=False, queue="scalar")
        gqc = k.buf(tag + "_gq", [128, 8], F32)
        k.ts(gqc.ap(), gmc.ap(), 0.25, None, ALU.mult, rd=(gmc,),
             wr=(gqc,), tag=f"{tag}_gq")
        wq = k.buf(tag + "_wq", [128, ND * 128], BF16)
        wkv = k.buf(tag + "_wkv", [128, ND * 256], BF16)
        self.load_weight_cols(tag + "_wql", self.input("L0_attn_w_q"), wq,
                              1024, 0, 128, queue="sync", scale_col=gqc)
        self.load_weight_cols(tag + "_wkvl", self.input("L0_attn_w_kv"),
                              wkv, 1024, 0, 256, queue="sync",
                              scale_col=gmc)
        qf = k.buf(tag + "_qf", [128, 2048], BF16)
        kf = k.buf(tag + "_kf", [128, 2048], BF16)
        vf = k.buf(tag + "_vf", [128, 2048], BF16)
        for (dst, wsrc, co) in ((qf, wq, 0), (kf, wkv, 0), (vf, wkv, 128)):
            ncol = 128 if wsrc is wq else 256
            for sl in range(4):
                pb = k.P[2]
                pv = pb.ap()[:, (sl % 2) * 512:(sl % 2) * 512 + 512]
                for dc in range(ND):
                    k.mm(pv, wsrc.ap()[:, dc * ncol + co:dc * ncol + co + 128],
                         self.xnf(dc)[:, sl * 512:(sl + 1) * 512],
                         start=(dc == 0), stop=(dc == ND - 1),
                         rd=(wsrc, self.xn_fm), wr=(pb.slot(sl % 2),),
                         tag=f"{tag}_pj")
                k.copy(dst.ap()[:, sl * 512:(sl + 1) * 512], pv,
                       rd=(pb.slot(sl % 2),), wr=(dst.slot(sl),),
                       tag=f"{tag}_pje", engine="scalar")
        qtm = k.buf(tag + "_qtm", [128, NT * 128], BF16)
        ktm = k.buf(tag + "_ktm", [128, NT * 128], BF16)
        vtm = k.buf(tag + "_vtm", [128, NT * 128], BF16)
        for (src_, dst) in ((qf, qtm), (kf, ktm), (vf, vtm)):
            for ti in range(NT):
                pb = k.P[3]
                pv = pbf(pb)
                k.transpose(pv[:, 1024 + (ti % 4) * 128:
                               1024 + (ti % 4 + 1) * 128],
                            src_.ap()[:, ti * 128:(ti + 1) * 128],
                            rd=(src_,), wr=(pb.slot(1),), tag=f"{tag}_tr")
                k.ts(dst.ap()[:, ti * 128:(ti + 1) * 128],
                     pv[:, 1024 + (ti % 4) * 128:1024 + (ti % 4 + 1) * 128],
                     scol[:, ti:ti + 1], None, ALU.mult,
                     rd=(pb.slot(1), scol.slot(ti)), wr=(dst.slot(ti),),
                     tag=f"{tag}_trs")
        kp = k.buf(tag + "_kp", [128, NT * DF], BF16)
        qp = k.buf(tag + "_qp", [128, NT * DF], BF16)
        qpf = k.buf(tag + "_qpf", [128, 3 * 2048], BF16)
        kvs = k.buf(tag + "_kvs", [128, 3 * 32], BF16)
        ksb = k.buf(tag + "_ksb", [128, DF], BF16)
        ksr = k.buf(tag + "_ksr", [1, DF], BF16)
        qsb = k.buf(tag + "_qsb", [128, 16], F32)
        dummy = Buf(self.ln_sq[0].name, self.ln_sq[0].ap()[:, 0:DF], k)
        dcol = k.buf(tag + "_dcol", [128, NT], F32)
        numsb = k.buf(tag + "_num", [16, 512], BF16)
        selH = self.k._consts["selH16"]
        kvsl = [(k.P[2], 0, 0), (k.P[2], 1, 512), (k.P[3], 0, 0)]
        csz = [128, 128, 17]
        cof = [0, 128, 256]
        for h in range(H):
            hb = h * 16
            for (tm, pp, scale_q) in ((ktm, kp, False), (qtm, qp, True)):
                for ti in range(NT):
                    base = ti * DF
                    blk = tm.ap()[:, ti * 128 + hb:ti * 128 + hb + 16]
                    k.memset(pp.ap()[:, base:base + 1], 1.0, rd=(),
                             wr=(pp.slot((ti, "o")),), tag=f"{tag}_po",
                             engine="gpsimd")
                    k.copy(pp.ap()[:, base + 1:base + 17], blk,
                           rd=(tm.slot(ti),), wr=(pp.slot((ti, "l")),),
                           tag=f"{tag}_pl", engine="gpsimd")
                    k.ts(qsb.ap(), blk, s4, None, ALU.mult,
                         rd=(tm.slot(ti),), wr=(qsb,), tag=f"{tag}_ps")
                    for j in range(16):
                        k.ts(pp.ap()[:, base + 17 + j * 16:
                                     base + 17 + (j + 1) * 16],
                             qsb.ap(), qsb[:, j:j + 1], None, ALU.mult,
                             rd=(qsb,), wr=(pp.slot((ti, j)),),
                             tag=f"{tag}_p2",
                             engine="vector" if j % 2 else "gpsimd")
            for c in range(3):
                pbuf, slot, off = kvsl[c]
                for ti in range(NT):
                    lhs = kp.ap()[0:128, ti * DF + cof[c]:
                                  ti * DF + cof[c] + csz[c]]
                    k.mm(pbuf.ap()[0:csz[c], off:off + 16], lhs,
                         vtm.ap()[:, ti * 128 + hb:ti * 128 + hb + 16],
                         start=(ti == 0), stop=False,
                         rd=(kp.slot((ti, "o")), kp.slot((ti, "l")),
                             vtm.slot(ti)) +
                            tuple(kp.slot((ti, j)) for j in range(16)),
                         wr=(pbuf.slot(slot),), tag=f"{tag}_kvm")
                    k.mm(pbuf.ap()[0:csz[c], off + 16:off + 17], lhs,
                         ones128.ap(),
                         start=False, stop=(ti == NT - 1),
                         rd=(ones128,),
                         wr=(pbuf.slot(slot),), tag=f"{tag}_ksm")
                k.copy(kvs.ap()[0:csz[c], c * 32:c * 32 + 17],
                       pbuf.ap()[0:csz[c], off:off + 17],
                       rd=(pbuf.slot(slot),), wr=(kvs.slot(c),),
                       tag=f"{tag}_kve")
            for c in range(3):
                pb = k.P[3]
                pv = pbf(pb)
                k.transpose(pv[0:1, 1024 + cof[c]:1024 + cof[c] + csz[c]],
                            kvs.ap()[0:csz[c], c * 32 + 16:c * 32 + 17],
                            rd=(kvs.slot(c),), wr=(pb.slot(1),),
                            tag=f"{tag}_kst")
            k.copy(ksr.ap(), pbf(k.P[3])[0:1, 1024:1024 + DF],
                   rd=(k.P[3].slot(1),), wr=(ksr,), tag=f"{tag}_kse")
            pb = k.P[3]
            k.mm(pb.ap()[:, 512:512 + DF * 0 + 273] if False else
                 pb.ap()[0:128, 512:785] if False else
                 pb.ap()[:, 512:512 + 256], ones1.ap(),
                 ksr.ap()[:, 0:256], start=True, stop=True,
                 rd=(ones1, ksr), wr=(pb.slot(1),), tag=f"{tag}_ksb1")
            k.mm(pb.ap()[:, 768:768 + 17], ones1.ap(),
                 ksr.ap()[:, 256:273], start=True, stop=True,
                 rd=(ones1, ksr), wr=(pb.slot(1),), tag=f"{tag}_ksb2")
            k.copy(ksb.ap()[:, 0:256], pb.ap()[:, 512:768],
                   rd=(pb.slot(1),), wr=(ksb,), tag=f"{tag}_ksbe")
            k.copy(ksb.ap()[:, 256:273], pb.ap()[:, 768:785],
                   rd=(pb.slot(1),), wr=(ksb,), tag=f"{tag}_ksbe2")
            for ti in range(NT):
                base = ti * DF
                rdq = (qp.slot((ti, "o")), qp.slot((ti, "l"))) + \
                    tuple(qp.slot((ti, j)) for j in range(16))
                k.stt(dummy.ap(), qp.ap()[:, base:base + DF], 1.0,
                      ksb.ap(), ALU.mult, ALU.mult,
                      rd=rdq + (ksb,), wr=(dummy,),
                      tag=f"{tag}_den", engine="vector")
                k.reduce(dcol[:, ti:ti + 1], dummy.ap(), ALU.add,
                         rd=(dummy,), wr=(dcol.slot(ti),),
                         tag=f"{tag}_dred")
                k.ts(dcol[:, ti:ti + 1], dcol[:, ti:ti + 1], 1e-5, None,
                     ALU.max, rd=(dcol.slot(ti),), wr=(dcol.slot(ti),),
                     tag=f"{tag}_dmx")
                k.recip(dcol[:, ti:ti + 1], dcol[:, ti:ti + 1],
                        rd=(dcol.slot(ti),), wr=(dcol.slot(ti),),
                        tag=f"{tag}_drc")
                k.ts(qp.ap()[:, base:base + DF], qp.ap()[:, base:base + DF],
                     dcol[:, ti:ti + 1], None, ALU.mult,
                     rd=rdq + (dcol.slot(ti),),
                     wr=(qp.slot((ti, "all")),), tag=f"{tag}_qsc")
            for c in range(3):
                for ti in range(NT):
                    pb = k.P[3]
                    pv = pbf(pb)
                    k.transpose(pv[0:csz[c],
                                   1024 + (ti % 4) * 128:
                                   1024 + (ti % 4) * 128 + 128],
                                qp.ap()[:, ti * DF + cof[c]:
                                        ti * DF + cof[c] + csz[c]],
                                rd=(qp.slot((ti, "all")),),
                                wr=(pb.slot(1),), tag=f"{tag}_qtr")
                    k.copy(qpf.ap()[0:csz[c],
                                    c * 2048 + ti * 128:
                                    c * 2048 + (ti + 1) * 128],
                           pv[0:csz[c], 1024 + (ti % 4) * 128:
                              1024 + (ti % 4) * 128 + 128],
                           rd=(pb.slot(1),), wr=(qpf.slot((c, ti)),),
                           tag=f"{tag}_qte")
            for sl in range(4):
                pb = k.P[3]
                for c in range(3):
                    k.mm(pb.ap()[0:16, 512:512 + 512] if False else
                         pb.ap()[0:16, 512:1024],
                         kvs.ap()[0:csz[c], c * 32:c * 32 + 16],
                         qpf.ap()[0:csz[c],
                                  c * 2048 + sl * 512:c * 2048 +
                                  (sl + 1) * 512],
                         start=(c == 0), stop=(c == 2),
                         rd=(kvs.slot(c),) +
                            tuple(qpf.slot((c, sl * 4 + q4))
                                  for q4 in range(4)),
                         wr=(pb.slot(1),), tag=f"{tag}_num")
                k.copy(numsb.ap(), pb.ap()[0:16, 512:1024],
                       rd=(pb.slot(1),), wr=(numsb,), tag=f"{tag}_nev")
                att_pb = k.P[sl // 2]
                k.mm(att_pb.ap()[:, (sl % 2) * 512:(sl % 2) * 512 + 512],
                     selH.ap()[0:16, h * 128:(h + 1) * 128], numsb.ap(),
                     start=(h == 0), stop=(h == H - 1),
                     rd=(selH, numsb), wr=(att_pb.slot(sl % 2),),
                     tag=f"{tag}_pack")
        att = self.xn_fm
        first = True
        for sl in range(4):
            att_pb = k.P[sl // 2]
            k.copy(att.ap()[:, sl * 512:(sl + 1) * 512],
                   att_pb.ap()[:, (sl % 2) * 512:(sl % 2) * 512 + 512],
                   rd=(att_pb.slot(sl % 2),),
                   wr=(att,) if first else (att.slot(("a", sl)),),
                   tag=f"{tag}_atte", engine="scalar")
            first = False
        wo = k.buf(tag + "_wo", [128, 1024], BF16)
        self.load_weight_cols(tag + "_wol", self.input("L0_attn_w_out"),
                              wo, 128, 0, 1024, queue="sync")
        for ti in range(NT):
            pb = k.P[2 + ti % 2]
            for jh in range(2):
                k.mm(pb.ap()[:, jh * 512:(jh + 1) * 512],
                     att.ap()[:, ti * 128:(ti + 1) * 128],
                     wo.ap()[:, jh * 512:(jh + 1) * 512],
                     start=True, stop=True,
                     rd=(att, wo), wr=(pb,), tag=f"{tag}_om")
            k.tt(self.xt(ti), pb.ap(), self.xt(ti), ALU.add,
                 rd=(pb, self.x.slot(ti)), wr=(self.x.slot(ti),),
                 tag=f"{tag}_acc")
        k.release(m)

    def emit_l1_attn(self, tag):
        k = self.k
        m = k.mark()
        DI, DS = 2048, 16
        NTC, TC = 8, 256
        ones1 = self.ones1()
        ident = self.k._consts["ident_bf16"]
        one1c = self.ccol(1.0)
        gmc, _ = self.load_bias(tag + "_gm", self.input("L1_attn_ln_gamma"),
                                1024, row=False, queue="scalar")
        dtl = k.buf(tag + "_dtl", [128, 2048], BF16)
        bcb = k.buf(tag + "_bcb", [32, 2048], BF16)
        bbeta = k.buf(tag + "_bbeta", [128, 32], F32)
        zdram = self.dram_scratch("zscr", [16, 128, 2048], BF16)
        udram = self.dram_scratch("uscr", [16, 128, 2048], BF16)
        W_in = self.input("L1_attn_W_in")
        dummy = Buf(self.ln_sq[0].name, self.ln_sq[0].ap(), k)
        mp0 = k.mark()
        _, btr = self.load_bias(tag + "_bt", self.input("L1_attn_ln_beta"),
                                1024, col=False, queue="scalar")
        btb = self.bf_row(tag + "_bt", btr, 1024)
        betab = k.buf(tag + "_betab", [128, 1024], BF16)
        pb = k.P[3]
        for jh in range(2):
            k.mm(pb.ap()[:, jh * 512:(jh + 1) * 512], ones1.ap(),
                 btb.ap()[:, jh * 512:(jh + 1) * 512], start=True,
                 stop=True, rd=(ones1, btb), wr=(pb.slot(jh),),
                 tag=f"{tag}_btm")
        k.copy(betab.ap(), pb.ap(), rd=(pb.slot(0), pb.slot(1)),
               wr=(betab,), tag=f"{tag}_bte")
        for oc in range(32):
            sg = self.wstage[self._stage_rr % 2]
            self._stage_rr += 1
            k.dma("sync", sg.ap(), W_in[oc * 128:(oc + 1) * 128, :],
                  rd=(), wr=(sg,), tag=f"{tag}_wrd{oc}")
            k.stt(dummy.ap(), sg.ap(), 1.0, betab.ap(), ALU.mult, ALU.mult,
                  rd=(sg, betab), wr=(dummy, bbeta.slot(oc)),
                  accum_out=bbeta.ap()[:, oc:oc + 1], tag=f"{tag}_bb")
        k.release(mp0)
        if getattr(self, "l1_stop", None) == 0:
            k.release(m)
            return
        mp = k.mark()
        u = k.buf(tag + "_u", [128, 16 * 2048], BF16)
        mp1 = k.mark()
        wT = [k.buf(f"{tag}_wT{i}", [128, 1024], BF16) for i in range(2)]
        _xs = k.buf(tag + "_xs", [128, 2051], BF16)
        xsst = [_xs, _xs]
        zst = [k.buf(f"{tag}_zs{i}", [128, 512], BF16) for i in range(2)]
        cwf = k.buf(tag + "_cwf", [128, 64], F32)
        k.dma("scalar", cwf.ap().rearrange("p (c j) -> p c j", j=4),
              self.input("L1_attn_conv_w")
              .rearrange("(c p) b j -> p c (b j)", p=128),
              rd=(), wr=(cwf,), tag=tag + "_cwd")
        cbc, _ = self.load_bias(tag + "_cb", self.input("L1_attn_conv_b"),
                                2048, row=False, queue="scalar")
        for oc in range(32):
            wt = wT[oc % 2]
            self.load_weight_T(f"{tag}_wi{oc}", W_in, wt, oc * 128, 128,
                               1024, queue="sync", scale_col=gmc)
            fc = oc % 16
            if oc < 16:
                xst = xsst[oc % 2]
                k.memset(xst.ap()[:, 0:3], 0.0, rd=(), wr=(xst.slot("z"),),
                         tag=f"{tag}_xz0", engine="gpsimd")
            for ts4 in range(4):
                pb2 = k.P[ts4 % 2]
                pv = pb2.ap()[:, 0:512]
                for dc in range(ND):
                    k.mm(pv, wt.ap()[:, dc * 128:(dc + 1) * 128],
                         self.xnf(dc)[:, ts4 * 512:(ts4 + 1) * 512],
                         start=(dc == 0), stop=(dc == ND - 1),
                         rd=(wt, self.xn_fm), wr=(pb2.slot(0),),
                         tag=f"{tag}_xm")
                if oc < 16:
                    k.ts(xst.ap()[:, 3 + ts4 * 512:3 + (ts4 + 1) * 512],
                         pv, bbeta[:, oc:oc + 1], None, ALU.add,
                         rd=(pb2.slot(0), bbeta.slot(oc)),
                         wr=(xst.slot(ts4),), tag=f"{tag}_xe")
                else:
                    zb = zst[ts4 % 2]
                    k.ts(dummy.ap()[:, 0:512], pv, bbeta[:, oc:oc + 1],
                         None, ALU.add,
                         rd=(pb2.slot(0), bbeta.slot(oc)),
                         wr=(dummy,), tag=f"{tag}_zb")
                    k.act(zb.ap(), dummy.ap()[:, 0:512], AF.Silu,
                          rd=(dummy,), wr=(zb,), tag=f"{tag}_zsl")
                    k.dma("gpsimd",
                          zdram.ap()[fc, :, ts4 * 512:(ts4 + 1) * 512],
                          zb.ap(), rd=(zb,),
                          wr=(zdram.slot((fc, ts4)),), tag=f"{tag}_zd")
            if oc < 16:
                rdx = (xst.slot("z"), xst.slot(0), xst.slot(1),
                       xst.slot(2), xst.slot(3), cwf)
                for hh in range(2):
                    o0 = hh * 1024
                    k.ts(dummy.ap()[:, 0:1024], xst.ap()[:, o0:o0 + 1024],
                         cwf[:, fc * 4:fc * 4 + 1], None, ALU.mult,
                         rd=rdx, wr=(dummy,), tag=f"{tag}_c0")
                    for j in range(1, 4):
                        k.stt(dummy.ap()[:, 0:1024],
                              xst.ap()[:, o0 + j:o0 + j + 1024],
                              cwf[:, fc * 4 + j:fc * 4 + j + 1],
                              dummy.ap()[:, 0:1024], ALU.mult, ALU.add,
                              rd=rdx + (dummy,), wr=(dummy,),
                              tag=f"{tag}_c{j}")
                    k.act(u.ap()[:, fc * 2048 + o0:fc * 2048 + o0 + 1024],
                          dummy.ap()[:, 0:1024], AF.Silu,
                          bias=cbc[:, fc:fc + 1],
                          rd=(dummy, cbc), wr=(u.slot(fc),),
                          tag=f"{tag}_us")
        for fc in range(16):
            k.dma("gpsimd", udram.ap()[fc], u.ap()[:, fc * 2048:
                                                   (fc + 1) * 2048],
                  rd=(u.slot(fc),), wr=(udram.slot(fc),),
                  tag=f"{tag}_ud{fc}")
        k.release(mp1)
        xpT = k.buf(tag + "_xpT", [128, 16 * 160], BF16)
        self.load_weight_T(tag + "_xp", self.input("L1_attn_W_xproj"), xpT,
                           0, 160, 2048, queue="sync")
        for sl in range(4):
            pbA = k.P[0]
            pbB = k.P[1]
            for dic in range(16):
                k.mm(pbA.ap()[:, 0:512],
                     xpT.ap()[:, dic * 160:dic * 160 + 128],
                     u.ap()[:, dic * 2048 + sl * 512:
                            dic * 2048 + (sl + 1) * 512],
                     start=(dic == 0), stop=(dic == 15),
                     rd=(xpT, u.slot(dic)), wr=(pbA.slot(0),),
                     tag=f"{tag}_pm")
                k.mm(pbB.ap()[0:32, 0:512],
                     xpT.ap()[:, dic * 160 + 128:dic * 160 + 160],
                     u.ap()[:, dic * 2048 + sl * 512:
                            dic * 2048 + (sl + 1) * 512],
                     start=(dic == 0), stop=(dic == 15),
                     rd=(xpT, u.slot(dic)), wr=(pbB.slot(0),),
                     tag=f"{tag}_pbm")
            k.copy(dtl.ap()[:, sl * 512:(sl + 1) * 512], pbA.ap()[:, 0:512],
                   rd=(pbA.slot(0),), wr=(dtl.slot(sl),),
                   tag=f"{tag}_pme", engine="scalar")
            k.copy(bcb.ap()[:, sl * 512:(sl + 1) * 512],
                   pbB.ap()[0:32, 0:512],
                   rd=(pbB.slot(0),), wr=(bcb.slot(sl),),
                   tag=f"{tag}_pbe", engine="vector")
        k.release(mp)
        if getattr(self, "l1_stop", None) == 1:
            k.release(m)
            return
        mp2 = k.mark()
        wdtT = k.buf(tag + "_wdt", [128, 2048], BF16)
        self.load_weight_T(tag + "_wdtl", self.input("L1_attn_W_dt"), wdtT,
                           0, 2048, 128, queue="sync")
        bdtc, _ = self.load_bias(tag + "_bdt", self.input("L1_attn_b_dt"),
                                 2048, row=False, queue="scalar")
        dpc, _ = self.load_bias(tag + "_dp", self.input("L1_attn_D_param"),
                                2048, row=False, queue="scalar")
        acol = k.buf(tag + "_acol", [128, 256], F32)
        k.dma("scalar", acol.ap().rearrange("p (c n) -> p c n", n=16),
              self.input("L1_attn_A_log").rearrange("(c p) n -> p c n",
                                                    p=128),
              rd=(), wr=(acol,), tag=tag + "_ald")
        k.act(acol.ap(), acol.ap(), AF.Exp, rd=(acol,), wr=(acol,),
              tag=tag + "_ale")
        k.ts(acol.ap(), acol.ap(), -1.0, None, ALU.mult, rd=(acol,),
             wr=(acol,), tag=tag + "_aln")
        carry = k.buf(tag + "_carry", [128, 256], F32)
        k.memset(carry.ap(), 0.0, rd=(), wr=(carry,), tag=tag + "_cz")
        delta = [k.buf(f"{tag}_dl{i}", [128, TC], F32) for i in range(2)]
        wti = k.buf(tag + "_wti", [128, TC], F32)
        bb_all = k.buf(tag + "_bba", [128, DS * TC], BF16)
        cb_all = k.buf(tag + "_cba", [128, DS * TC], BF16)
        rowst = k.buf(tag + "_row", [1, TC], BF16)
        datile = k.buf(tag + "_da", [128, TC], F32)
        xtile = k.buf(tag + "_xt", [128, TC], F32)
        htile = k.buf(tag + "_ht", [128, TC], F32)
        hctile = [k.buf(f"{tag}_hc{i}", [128, TC], BF16) for i in range(2)]
        y2t = k.buf(tag + "_y2", [128, 16 * TC], BF16)
        sfix = k.buf(tag + "_sfx", [128, TC], F32)
        ztc = k.buf(tag + "_ztc", [128, 16 * TC], BF16)
        utc = k.buf(tag + "_utc", [128, 16 * TC], BF16)
        woh = k.buf(tag + "_woh", [128, 16 * 512], BF16)
        W_out = self.input("L1_attn_W_out")
        for tc in range(NTC):
            t0 = tc * TC
            slq = t0 // 512
            k.dma("gpsimd", ztc.ap().rearrange("p (c t) -> p c t", t=TC),
                  zdram.ap()[:, :, t0:t0 + TC]
                  .rearrange("c p t -> p c t"),
                  rd=tuple(zdram.slot((c_, s_)) for c_ in range(16)
                           for s_ in range(4)),
                  wr=(ztc,), tag=f"{tag}_zl")
            k.dma("sync", utc.ap().rearrange("p (c t) -> p c t", t=TC),
                  udram.ap()[:, :, t0:t0 + TC]
                  .rearrange("c p t -> p c t"),
                  rd=tuple(udram.slot(c_) for c_ in range(16)),
                  wr=(utc,), tag=f"{tag}_ul")
            for n in range(DS):
                for (roff, dst) in ((n, bb_all), (DS + n, cb_all)):
                    pb = k.P[3]
                    k.mm(pb.ap()[0:1, 0:TC],
                         ident.ap()[0:32, roff:roff + 1],
                         bcb.ap()[0:32, t0:t0 + TC], start=True, stop=True,
                         rd=(bcb.slot(slq), ident), wr=(pb.slot(0),),
                         tag=f"{tag}_rx")
                    k.copy(rowst.ap(), pb.ap()[0:1, 0:TC],
                           rd=(pb.slot(0),), wr=(rowst,), tag=f"{tag}_re")
                    k.mm(pb.ap()[:, 512:512 + TC], ones1.ap(), rowst.ap(),
                         start=True, stop=True, rd=(ones1, rowst),
                         wr=(pb.slot(1),), tag=f"{tag}_rb")
                    k.copy(dst.ap()[:, n * TC:(n + 1) * TC],
                           pb.ap()[:, 512:512 + TC], rd=(pb.slot(1),),
                           wr=(dst.slot(n),), tag=f"{tag}_rbe")
            if getattr(self, "l1_stop", None) == 2:
                continue
            for dic in range(16):
                if getattr(self, "l1_stop", None) == 3 and dic > 0:
                    continue
                dl = delta[dic % 2]
                pb = k.P[2]
                sl2 = dic % 2
                pv = pb.ap()[:, sl2 * 512:sl2 * 512 + TC]
                k.mm(pv, wdtT.ap()[:, dic * 128:(dic + 1) * 128],
                     dtl.ap()[:, t0:t0 + TC], start=True, stop=True,
                     rd=(wdtT, dtl.slot(slq)), wr=(pb.slot(sl2),),
                     tag=f"{tag}_dm")
                if getattr(self, "l1_nosp", False):
                    k.copy(datile.ap(), pv, rd=(pb.slot(sl2),),
                           wr=(datile,), tag=f"{tag}_sa")
                else:
                    k.act(datile.ap(), pv, AF.Abs, bias=bdtc[:, dic:dic + 1],
                          rd=(pb.slot(sl2), bdtc), wr=(datile,),
                          tag=f"{tag}_sa")
                    k.act(datile.ap(), datile.ap(), AF.Exp, scale=-1.0,
                          rd=(datile,), wr=(datile,), tag=f"{tag}_se")
                    k.act(datile.ap(), datile.ap(), AF.Ln, bias=one1c.ap(),
                          rd=(datile, one1c), wr=(datile,), tag=f"{tag}_sl")
                if getattr(self, "l1_nosp", False):
                    k.copy(dl.ap(), datile.ap(), rd=(datile,), wr=(dl,),
                           tag=f"{tag}_sp")
                else:
                    k.ts(xtile.ap(), pv, bdtc[:, dic:dic + 1], 0.0, ALU.add,
                         ALU.max, rd=(pb.slot(sl2), bdtc), wr=(xtile,),
                         tag=f"{tag}_sr")
                    k.tt(dl.ap(), datile.ap(), xtile.ap(), ALU.add,
                         rd=(datile, xtile), wr=(dl,), tag=f"{tag}_sp")
                if getattr(self, "l1_stop", None) == 5:
                    continue
                k.tt(wti.ap(), dl.ap(),
                     utc.ap()[:, dic * TC:(dic + 1) * TC],
                     ALU.mult, rd=(dl, utc), wr=(wti,),
                     tag=f"{tag}_w")
                ypb = k.P[0 if dic % 2 == 0 else 1]
                ysl = (dic // 2) % 2
                ypv = ypb.ap()[:, ysl * 512:ysl * 512 + TC]
                for n in range(DS):
                    if getattr(self, "l1_stop", None) == 6 and n > 0:
                        continue
                    cc = dic * 16 + n
                    k.act(datile.ap(), dl.ap(), AF.Exp,
                          scale=acol[:, cc:cc + 1],
                          rd=(dl, acol), wr=(datile,), tag=f"{tag}_da")
                    k.tt(xtile.ap(), wti.ap(),
                         bb_all.ap()[:, n * TC:(n + 1) * TC], ALU.mult,
                         rd=(wti, bb_all.slot(n)), wr=(xtile,),
                         tag=f"{tag}_x", engine="gpsimd")
                    k.scan(htile.ap(), datile.ap(), xtile.ap(),
                           carry[:, cc:cc + 1], ALU.mult, ALU.add,
                           rd=(datile, xtile, carry.slot(cc)),
                           wr=(htile,), tag=f"{tag}_scan")
                    k.copy(carry.ap()[:, cc:cc + 1],
                           htile.ap()[:, TC - 1:TC],
                           rd=(htile,), wr=(carry.slot(cc),),
                           tag=f"{tag}_cu", engine="scalar")
                    hc_ = hctile[n % 2]
                    k.tt(hc_.ap(), htile.ap(),
                         cb_all.ap()[:, n * TC:(n + 1) * TC], ALU.mult,
                         rd=(htile, cb_all.slot(n)), wr=(hc_,),
                         tag=f"{tag}_hcm", engine="gpsimd")
                    k.mm(ypv, ident.ap(), hc_.ap(),
                         start=(n == 0), stop=(n == DS - 1),
                         rd=(ident, hc_), wr=(ypb.slot(ysl),),
                         tag=f"{tag}_yac")
                if getattr(self, "l1_stop", None) in (6, 7):
                    continue
                k.stt(sfix.ap(),
                      utc.ap()[:, dic * TC:(dic + 1) * TC],
                      dpc[:, dic:dic + 1], ypv, ALU.mult, ALU.add,
                      rd=(utc, dpc, ypb.slot(ysl)),
                      wr=(sfix,), tag=f"{tag}_ud")
                k.tt(y2t.ap()[:, dic * TC:(dic + 1) * TC], sfix.ap(),
                     ztc.ap()[:, dic * TC:(dic + 1) * TC], ALU.mult,
                     rd=(sfix, ztc), wr=(y2t.slot(dic),),
                     tag=f"{tag}_y2")
            if getattr(self, "l1_stop", None) in (2, 3, 4):
                continue
            for half in range(2):
                self.load_weight_T(f"{tag}_wo{tc}_{half}", W_out, woh,
                                   half * 512, 512, 2048, queue="sync")
                for tt2 in range(2):
                    ti = tc * 2 + tt2
                    pb3 = k.P[2 + tt2]
                    pvo = pb3.ap()[:, half * 512:(half + 1) * 512]
                    for dic in range(16):
                        k.mm(pvo,
                             y2t.ap()[:, dic * TC + tt2 * 128:
                                      dic * TC + (tt2 + 1) * 128],
                             woh.ap()[:, dic * 512:(dic + 1) * 512],
                             start=(dic == 0), stop=(dic == 15),
                             rd=(y2t.slot(dic), woh),
                             wr=(pb3.slot(half),), tag=f"{tag}_om")
                    k.tt(self.x[:, ti * 1024 + half * 512:
                                ti * 1024 + (half + 1) * 512],
                         pvo,
                         self.x[:, ti * 1024 + half * 512:
                                ti * 1024 + (half + 1) * 512],
                         ALU.add, rd=(pb3.slot(half), self.x.slot(ti)),
                         wr=(self.x.slot(ti),), tag=f"{tag}_acc")
        k.release(mp2)
        k.release(m)

    def emit_l2_ffn(self, tag, stages='all'):
        k = self.k
        W1 = self.input("L2_ffn_W1")
        W2 = self.input("L2_ffn_W2")
        m = k.mark()
        b1c, _ = self.load_bias(tag + "_b1", self.input("L2_ffn_b1"), 4096, row=False)
        _, b2r = self.load_bias(tag + "_b2", self.input("L2_ffn_b2"), 1024,
                                scale=0.5, col=False)
        b2rb = self.bf_row(tag + "_b2", b2r, 1024)
        ones1 = self.ones1()
        SH = 512
        HS = 4096 // SH
        NHC = SH // 128
        w1s = [k.buf(f"{tag}_w1s{i}", [128, ND * SH], BF16) for i in range(2)]
        w2s = [k.buf(f"{tag}_w2s{i}", [128, NHC * 1024], BF16) for i in range(2)]
        hsb = k.buf(f"{tag}_hsb", [128, NHC * 2048], BF16)
        for hs in range(HS):
            w1b, w2b = w1s[hs % 2], w2s[hs % 2]
            self.load_weight_cols(f"{tag}_w1_{hs}", W1, w1b, 1024,
                                  hs * SH, SH, queue="sync")
            self.load_weight_cols(f"{tag}_w2_{hs}",
                                  W2[hs * SH:(hs + 1) * SH, :],
                                  w2b, SH, 0, 1024, scale=0.5,
                                  queue="gpsimd")
            if stages == 'loads':
                continue
            for hc in range(NHC):
                for tsb in range(4):
                    g = hc * 4 + tsb
                    pb = k.P[g % 2]
                    sl = (g // 2) % 2
                    pv = pb.ap()[:, sl * 512:(sl + 1) * 512]
                    for dc in range(ND):
                        k.mm(pv,
                             w1b.ap()[:, dc * SH + hc * 128:
                                      dc * SH + (hc + 1) * 128],
                             self.xnf(dc)[:, tsb * 512:(tsb + 1) * 512],
                             start=(dc == 0), stop=(dc == ND - 1),
                             rd=(w1b, self.xn_fm),
                             wr=(pb.slot(sl),), tag=f"{tag}_h")
                    hcol = b1c[:, hs * NHC + hc:hs * NHC + hc + 1]
                    k.act(hsb.ap()[:, hc * 2048 + tsb * 512:
                                   hc * 2048 + (tsb + 1) * 512],
                          pv, AF.Relu, bias=hcol,
                          rd=(pb.slot(sl), b1c),
                          wr=(hsb.slot((hc, tsb)),), tag=f"{tag}_relu")
            if stages == 'h':
                continue
            for ti in range(NT):
                pb = k.P[2 + ti % 2]
                for hc in range(NHC):
                    last = (hc == NHC - 1) and hs > 0
                    for jh in range(2):
                        k.mm(pb.ap()[:, jh * 512:(jh + 1) * 512],
                             hsb.ap()[:, hc * 2048 + ti * 128:
                                      hc * 2048 + (ti + 1) * 128],
                             w2b.ap()[:, hc * 1024 + jh * 512:
                                      hc * 1024 + (jh + 1) * 512],
                             start=(hc == 0), stop=last,
                             rd=(hsb.slot((hc, ti // 4)), w2b), wr=(pb,),
                             tag=f"{tag}_y")
                if hs == 0:
                    for jh in range(2):
                        k.mm(pb.ap()[:, jh * 512:(jh + 1) * 512],
                             ones1.ap(),
                             b2rb.ap()[:, jh * 512:(jh + 1) * 512],
                             start=False, stop=True,
                             rd=(ones1, b2rb), wr=(pb,), tag=f"{tag}_yb")
                k.tt(self.xt(ti), pb.ap(), self.xt(ti), ALU.add,
                     rd=(pb, self.x.slot(ti)), wr=(self.x.slot(ti),),
                     tag=f"{tag}_acc")
        k.release(m)



SUBLAYERS_ALL = [
    "L0_ffn_a", "L0_attn", "L0_ffn_b",
    "L1_ffn_a", "L1_attn", "L1_ffn_b",
    "L2_ffn_a", "L2_attn", "L2_ffn_b",
    "L3_ffn_a", "L3_attn", "L3_ffn_b",
]


def build_nc(sublayers=None):
    sublayers = SUBLAYERS_ALL if sublayers is None else sublayers
    nc = bass.Bass("TRN2", target_bir_lowering=False)
    kb = Kb(nc)
    kb.prewarm_consts()
    kb.emit_load_x()
    for i, sl in enumerate(sublayers):
        if sl == "copyonly":
            continue
        add_back = sl in ("L1_attn", "L3_attn")
        kb.emit_ln(f"ln{i}", add_back=add_back)
        if sl == "lnonly":
            continue
        if sl.startswith("L2_ffn"):
            st_ = "all"
            if sl.endswith("loads"): st_ = "loads"
            if sl.endswith("hstage"): st_ = "h"
            kb.emit_l2_ffn(f"s{i}", stages=st_)
        elif sl.startswith("L0_ffn"):
            kb.emit_l0_ffn(f"s{i}")
        elif sl.startswith("L1_ffn"):
            kb.emit_l1_ffn(f"s{i}")
        elif sl.startswith("L3_ffn"):
            kb.emit_l3_ffn(f"s{i}")
        elif sl == "L0_attn":
            kb.emit_l0_attn(f"s{i}")
        elif sl == "L1_attn":
            kb.emit_l1_attn(f"s{i}")
        elif sl == "L2_attn":
            kb.emit_l2_attn(f"s{i}")
        elif sl == "L3_attn":
            kb.emit_l3_attn(f"s{i}")
        else:
            raise ValueError(sl)
    kb.emit_store_out()
    stack = kb.k.finalize_and_emit()
    return nc, kb, stack


def kernel(**inputs):
    nc, kb, stack = build_nc()
    used = set(kb.inputs.keys())
    x_full = np.ascontiguousarray(inputs["L0_attn_x"], dtype=np.float32)
    in_maps = []
    for c in range(8):
        d = {}
        for name in used:
            if name == "L0_attn_x":
                d[name] = np.ascontiguousarray(x_full[c])
            else:
                d[name] = np.ascontiguousarray(inputs[name], dtype=np.float32)
        in_maps.append(d)
    res = run_bass_kernel_spmd(nc, in_maps, core_ids=list(range(8)))
    stack.close()
    out = np.stack([res.results[c]["out"] for c in range(8)], axis=0)
    return out.astype(np.float32)
